# version 03
# baseline (speedup 1.0000x reference)
import numpy as np
import ml_dtypes
import contextlib

"""Recording/replay emit framework with automatic semaphore dependency edges.

Buffers tracked by tensor name (or explicit string key). Every RAW/WAR/WAW
gets a semaphore edge (race-detector-sound, incl. same-engine). DMA ops use
per-queue round-robin semaphore lanes with at most one outstanding DMA per
lane (issuer pre-waits its lane), so lane_sem >= 16*k soundly means the k-th
use of that lane completed.
"""
import concourse.bass as bass
import concourse.mybir as mybir

ENGINES = ("sync", "scalar", "vector", "tensor", "gpsimd")
NLANES = 8


def bufkey(x):
    if isinstance(x, str):
        return x
    t = getattr(x, "tensor", x)
    return t.name.split("__s")[0]


class Op:
    __slots__ = ("i", "eng", "fn", "reads", "writes", "dma", "group",
                 "deps", "seq", "need_inc", "sem_val_after", "lane", "tag", "bar",
                 "noinstr", "local")

    def __init__(self, i, eng, fn, reads, writes, dma, group, tag):
        self.i = i
        self.eng = eng
        self.fn = fn
        self.reads = reads
        self.writes = writes
        self.dma = dma
        self.group = group
        self.deps = set()
        self.seq = None
        self.need_inc = False
        self.sem_val_after = None
        self.lane = None
        self.tag = tag
        self.bar = False
        self.noinstr = False
        self.local = False


class B:
    def __init__(self, nc):
        self.nc = nc
        self.ops = []
        self.last_write = {}
        self.readers = {}
        self.open_groups = {}
        self.last_bar = None

    def emit(self, eng, fn=None, reads=(), writes=(), dma=False, group=None,
             group_end=False, tag=None, noinstr=False, local=False):
        assert eng in ENGINES
        op = Op(len(self.ops), eng, fn, [bufkey(r) for r in reads],
                [bufkey(w) for w in writes], dma, group, tag)
        op.noinstr = noinstr
        op.local = local
        deps = set()
        for k in op.reads:
            lw = self.last_write.get(k)
            if lw is not None:
                deps.add(lw)
        for k in op.writes:
            lw = self.last_write.get(k)
            if lw is not None:
                deps.add(lw)
            for r in self.readers.get(k, ()):
                deps.add(r)
        if self.last_bar is not None:
            for k in op.reads + op.writes:
                if k not in self.last_write and k not in self.readers:
                    deps.add(self.last_bar)
                    break
        if group is not None:
            grp = self.open_groups.setdefault(group, [])
            deps -= set(grp)
            grp.append(op.i)
        op.deps = deps
        self.ops.append(op)
        for k in op.reads:
            self.readers.setdefault(k, []).append(op.i)
        if group is None or group_end:
            if group is not None:
                self.open_groups.pop(group)
            for k in op.writes:
                self.last_write[k] = op.i
                self.readers[k] = []
        return op

    def inherit(self, new_key, old_keys):
        lw = sorted({self.last_write[k] for k in old_keys if k in self.last_write})
        rd = [i for k in old_keys for i in self.readers.get(k, ())]
        if lw:
            self.last_write[new_key] = lw[-1]
        extra = sorted(set(rd) | set(lw[:-1]))
        if extra:
            self.readers[new_key] = extra
        return new_key

    def barrier(self, eng="vector", tag="bar"):
        op = self.emit(eng, None, tag=tag)
        deps = set(self.last_write.values())
        for rl in self.readers.values():
            deps.update(rl)
        deps.discard(op.i)
        deps = {d for d in deps if not self.ops[d].local}
        op.deps = deps
        op.bar = True
        for k in list(self.last_write):
            self.last_write[k] = op.i
            self.readers[k] = []
        self.last_bar = op.i
        return op

    def finalize(self):
        nc = self.nc
        ops = self.ops
        seqs = {e: [] for e in ENGINES}
        for op in ops:
            op.seq = len(seqs[op.eng])
            seqs[op.eng].append(op)
        for op in ops:
            for d in op.deps:
                dop = ops[d]
                if dop.local and dop.eng == op.eng:
                    continue
                dop.need_inc = True
        comp_count = {e: 0 for e in ENGINES}
        lane_count = {}
        lane_rr = {e: 0 for e in ENGINES}
        pre_waits = {}
        for op in ops:
            if op.dma:
                lane = lane_rr[op.eng]
                lane_rr[op.eng] = (lane + 1) % NLANES
                op.lane = lane
                key = (op.eng, lane)
                prior = lane_count.get(key, 0)
                if prior > 0:
                    pre_waits[op.i] = (("l", op.eng, lane), 16 * prior)
                lane_count[key] = prior + 1
                op.sem_val_after = 16 * (prior + 1)
                op.need_inc = True
            elif op.need_inc:
                comp_count[op.eng] += 1
                op.sem_val_after = comp_count[op.eng]
        import contextlib
        self._semstack = contextlib.ExitStack()
        csem = {e: self._semstack.enter_context(nc.semaphore(f"c_{e}"))
                for e in ENGINES if comp_count[e] > 0}
        lsem = {k: self._semstack.enter_context(nc.semaphore(f"l_{k[0]}_{k[1]}"))
                for k in lane_count}
        known = {e: {} for e in ENGINES}
        waits = []
        for op in ops:
            need = {}
            for d in sorted(op.deps):
                dop = ops[d]
                if dop.sem_val_after is None:
                    continue
                key = (("l", dop.eng, dop.lane) if dop.dma else ("c", dop.eng))
                v = dop.sem_val_after
                if need.get(key, 0) < v:
                    need[key] = v
            if op.i in pre_waits:
                key, v = pre_waits[op.i]
                if need.get(key, 0) < v:
                    need[key] = v
            k = known[op.eng]
            todo = []
            for key, v in sorted(need.items(), key=lambda kv: str(kv[0])):
                if k.get(key, 0) < v:
                    todo.append((key, v))
                    k[key] = v
            waits.append(todo)
        self.waits = waits
        self._csem, self._lsem = csem, lsem

        def sem_of(key):
            if key[0] == "c":
                return csem[key[1]]
            return lsem[(key[1], key[2])]

        block_cm = nc.Block()
        block = block_cm.__enter__()

        def run_engine(engine_name):
            def runner(eobj):
                try:
                    eobj.get_value_cache().nocache = True
                except Exception:
                    pass
                for op in seqs[engine_name]:
                    todo = waits[op.i]
                    for key, v in todo:
                        eobj.wait_ge(sem_of(key), v)
                    try:
                        instr = op.fn(eobj) if op.fn is not None else eobj.nop()
                    except Exception:
                        import traceback as _tb
                        _tb.print_exc()
                        raise
                    if op.need_inc and op.sem_val_after is not None:
                        sem = (lsem[(op.eng, op.lane)] if op.dma else csem[op.eng])
                        amt = 16 if op.dma else 1
                        if instr is not None:
                            instr.then_inc(sem, amt)
                        else:
                            eobj.sem_inc(sem, amt)
                    if op.dma:
                        eobj.get_value_cache().barrier()
            return runner

        block.sync(run_engine("sync"))
        block.scalar(run_engine("scalar"))
        block.vector(run_engine("vector"))
        block.tensor(run_engine("tensor"))
        block.gpsimd(run_engine("gpsimd"))
        block_cm.__exit__(None, None, None)
        self._semstack.close()


"""Megakernel builder for mk_hybrid_d1 (see DESIGN.md).

Per-core program; 8 cores SPMD. Core c owns batch rows core_rows(c)
(stride-4 pair layout). All emitters share a Ctx.
"""
import numpy as np
import contextlib
import concourse.bass as bass
import concourse.mybir as mybir

F32 = mybir.dt.float32
BF16 = mybir.dt.bfloat16
I32 = mybir.dt.int32
AF = mybir.ActivationFunctionType
ALU = mybir.AluOpType
AX = mybir.AxisListType

D = 1024
NCHUNK = 8
TCORE = 4096
NT = 32
TROW = 2048
NTR = 16
DILS = [1, 2, 4, 8]
NORM = 1.0 / np.sqrt(128.0)
HOFF = [0, 2048, 3072, 3584]




class Ctx:
    def __init__(self, nc, b):
        self.nc = nc
        self.b = b
        self.stack = contextlib.ExitStack()
        self.t = {}
        self.dram = {}
        self.stage_stack = None
        self.stage_id = 0

    def _name(self, name):
        return name if self.stage_stack is None else f"{name}__s{self.stage_id}"

    def _stk(self):
        return self.stack if self.stage_stack is None else self.stage_stack

    def sbuf(self, name, shape, dtype):
        h = self._stk().enter_context(self.nc.sbuf_tensor(self._name(name), shape, dtype))
        self.t[name] = h
        return h

    def psum(self, name, shape, dtype=F32):
        h = self._stk().enter_context(self.nc.psum_tensor(self._name(name), shape, dtype))
        self.t[name] = h
        return h

    def stage_begin(self):
        assert self.stage_stack is None
        self.stage_id += 1
        self.stage_stack = contextlib.ExitStack()

    def stage_end(self):
        self.stage_stack.close()
        self.stage_stack = None
        self.b.barrier()


def emit_ln_tile(ctx, resid_tile, resid_key, dst_tm, dst_key, mean_ap, rstd_ap,
                 mk="lmean", rk="lrstd", bneg_ap=None, bneg_key=None):
    b = ctx.b
    t = ctx.t
    sq, ssq, s, m2, bb = t["lsq"], t["lssq"], t["ls"], t["lm2"], t["lbb"]

    b.emit("scalar", lambda e: e.activation(out=sq[:], in_=resid_tile, func=AF.Square, accum_out=ssq[:]),
           reads=[resid_key], writes=[sq, ssq])
    b.emit("vector", lambda e: e.reduce_sum(s[:], resid_tile, axis=AX.X),
           reads=[resid_key], writes=[s])
    b.emit("vector", lambda e: e.tensor_scalar_mul(mean_ap, s[:], 1.0 / D),
           reads=[s], writes=[mk])
    b.emit("scalar", lambda e: e.activation(out=m2[:], in_=mean_ap, func=AF.Square),
           reads=[mk], writes=[m2])
    b.emit("scalar", lambda e: e.activation(out=bb[:], in_=m2[:], func=AF.Copy, scale=-1.0, bias=1e-5),
           reads=[m2], writes=[bb])
    b.emit("scalar", lambda e: e.activation(out=rstd_ap, in_=ssq[:], func=AF.Sqrt, scale=1.0 / D, bias=bb[:]),
           reads=[ssq, bb], writes=[rk])
    b.emit("vector", lambda e: e.reciprocal(rstd_ap, rstd_ap), reads=[rk], writes=[rk])
    bbk = bneg_key if bneg_key is not None else "lbb"
    bba = bb[:, :] if bneg_ap is None else bneg_ap
    b.emit("vector", lambda e: e.tensor_mul(bba, mean_ap, rstd_ap), reads=[mk, rk], writes=[bbk])
    b.emit("vector", lambda e: e.tensor_scalar_mul(bba, bba, -1.0), reads=[bbk], writes=[bbk])
    b.emit("scalar", lambda e: e.activation(out=dst_tm, in_=resid_tile, func=AF.Identity, scale=rstd_ap, bias=bba),
           reads=[resid_key, rk, bbk], writes=[dst_key])


def emit_tm2fm(ctx, src_tm, src_key, dst_fm_fn, dst_key, eng="sync"):
    b = ctx.b
    for c in range(NCHUNK):
        b.emit(eng, lambda e, c=c: e.dma_start(
            out=dst_fm_fn(c), in_=src_tm[:, c * 128:(c + 1) * 128], transpose=True),
            reads=[src_key], writes=[dst_key], dma=True)


def dil_ap(xfm_chunk, dil, s0, cnt_s):
    base = xfm_chunk.rearrange("p (seg s) -> p seg s", s=128)
    return base[:, ::dil, s0:s0 + cnt_s].rearrange("p n s -> p s n")


def emit_attn_a(ctx):
    nc, b, t = ctx.nc, ctx.b, ctx.t
    resid = t["resid"]
    xfm = t["xfm"]
    xlt = t["xlt"]
    q, k, v = t["aQ"], t["aK"], t["aV"]
    vst = t["aVs"]
    att = t["aAtt"]
    sraw = t["aSraw"]
    sm = t["aSm"]
    rowmax, rowsum = t["aRmax"], t["aRsum"]
    ysb = t["aY"]
    dfm = t["aDfm"]
    dtm = t["aDtm"]
    masks = t["amasks"]
    bout = t["bout"]
    woutT = t["woutT"]
    wq = t["awq"]; wk = t["awk"]; wv = t["awv"]
    wqkvT = ctx.dram["wqkvT"]
    ps_qkv = t["ps_qkv"]; ps_s = t["ps_s"]; ps_av = t["ps_av"]; ps_y = t["ps_y"]

    def load_head_w(h, buf):
        for wi, wsb in enumerate((wq, wk, wv)):
            src = wqkvT[wi, h].rearrange("(c p) k -> p c k", p=128)
            b.emit("gpsimd", lambda e, wsb=wsb, src=src, buf=buf: e.dma_start(
                out=wsb[:, buf], in_=src), reads=["wqkvT"], writes=[wsb], dma=True)

    load_head_w(0, 0)
    for r in range(2):
        for i in range(NTR):
            ti = 16 * r + i
            emit_ln_tile(ctx, resid[:, ti, :], "resid", xlt[:, :], "xlt",
                         t["lmean"][:, 0:1], t["lrstd"][:, 0:1])
            emit_tm2fm(ctx, xlt, "xlt", lambda c, i=i: xfm[:, c, i * 128:(i + 1) * 128], "xfm",
                       eng="sync")
        for h, dil in enumerate(DILS):
            L = 16 // dil
            ncols = 128 * L
            buf = 0
            if r == 0 and h == 0:
                pass
            if h > 0 or r > 0:
                load_head_w(h, 0)
            for j0 in range(0, ncols, 512):
              w = min(512, ncols - j0)
              for (wsb, dst, tag) in ((wq, q, "q"), (wk, k, "k")):
                    cs = w // L
                    s0 = j0 // L
                    for cc in range(NCHUNK):
                        ap = dil_ap(xfm[:, cc, :], dil, s0, cs)
                        b.emit("tensor", lambda e, wsb=wsb, buf=buf, cc=cc, ap=ap, w=w: e.matmul(
                            ps_qkv[:, 0:w], wsb[:, buf, cc, :], ap,
                            start=(cc == 0), stop=(cc == NCHUNK - 1)),
                            reads=[wsb, "xfm"], writes=[ps_qkv], group=f"qk{r}{h}{tag}{j0}",
                            group_end=(cc == NCHUNK - 1))
                    b.emit("vector", lambda e, dst=dst, w=w: e.tensor_copy(
                        dst[:, 0:w], ps_qkv[:, 0:w]),
                        reads=[ps_qkv], writes=[dst])
              for gl in range(w // 128):
                g = j0 // 128 + gl
                s0 = (g * 128) // L
                cs = 128 // L
                for cc in range(NCHUNK):
                    ap = dil_ap(xfm[:, cc, :], dil, s0, cs)
                    b.emit("scalar", lambda e, ap=ap: e.activation(
                        out=vst[:, :], in_=ap, func=AF.Copy),
                        reads=["xfm"], writes=["aVs"])
                    b.emit("tensor", lambda e, cc=cc, buf=buf: e.matmul(
                        ps_qkv[:, 0:128], vst[:, :], wv[:, buf, cc, :],
                        start=(cc == 0), stop=(cc == NCHUNK - 1)),
                        reads=[wv, "aVs"], writes=[ps_qkv], group=f"v{r}{h}{g}",
                        group_end=(cc == NCHUNK - 1))
                b.emit("vector", lambda e, gl=gl: e.tensor_copy(
                    v[:, gl * 128:(gl + 1) * 128], ps_qkv[:, 0:128]),
                    reads=[ps_qkv], writes=[v])
              for gl in range(w // 128):
                g = j0 // 128 + gl
                o = gl * 128
                b.emit("tensor", lambda e, o=o: e.matmul(
                    ps_s[:, :], k[:, o:o + 128], q[:, o:o + 128], start=True, stop=True),
                    reads=[k, q], writes=[ps_s], group=f"s{r}{h}{g}", group_end=True)
                b.emit("vector", lambda e, h=h: e.scalar_tensor_tensor(
                    sraw[:], ps_s[:, :], NORM, masks[:, h, :], ALU.mult, ALU.add),
                    reads=[ps_s, masks], writes=[sraw])
                b.emit("vector", lambda e: e.reduce_max(rowmax[:], sraw[:], axis=AX.X, negate=True),
                       reads=[sraw], writes=[rowmax])
                b.emit("scalar", lambda e: e.activation(out=sraw[:], in_=sraw[:], func=AF.Exp, bias=rowmax[:]),
                       reads=[sraw, rowmax], writes=[sraw])
                b.emit("vector", lambda e: e.reduce_sum(rowsum[:], sraw[:], axis=AX.X),
                       reads=[sraw], writes=[rowsum])
                b.emit("vector", lambda e: e.reciprocal(rowsum[:], rowsum[:]),
                       reads=[rowsum], writes=[rowsum])
                b.emit("vector", lambda e: e.tensor_scalar(sm[:], sraw[:], rowsum[:], None, ALU.mult),
                       reads=[sraw, rowsum], writes=[sm])
                b.emit("tensor", lambda e, o=o: e.matmul(
                    ps_av[:, :], v[:, o:o + 128], sm[:, :], start=True, stop=True),
                    reads=[v, sm], writes=[ps_av], group=f"av{r}{h}{g}", group_end=True)
                b.emit("vector", lambda e, h=h, g=g: e.tensor_copy(
                    att[:, HOFF[h] + g * 128:HOFF[h] + (g + 1) * 128], ps_av[:, :]),
                    reads=[ps_av], writes=[att])
        PS_OFF = [0, 512, 768, 896]
        SS = [(True, True), (True, False), (False, False), (False, True)]
        for tau in range(4):
          for half in range(2):
            for oc4 in range(4):
                oc = 4 * half + oc4
                for h, dil in enumerate(DILS):
                    wsl = 512 // dil
                    a0 = HOFF[h] + tau * wsl
                    p0 = PS_OFF[h]
                    st, sp = SS[h]
                    b.emit("tensor", lambda e, h=h, oc=oc, a0=a0, p0=p0, w=wsl, st=st, sp=sp: e.matmul(
                        ps_y[:, p0:p0 + w], woutT[:, h, oc * 128:(oc + 1) * 128], att[:, a0:a0 + w],
                        start=st, stop=sp),
                        reads=[woutT, att], writes=[ps_y], group=f"y{r}{tau}{oc}",
                        group_end=(h == 3))
                b.emit("vector", lambda e: e.tensor_copy(ysb[:], ps_y[:, 0:512]),
                       reads=[ps_y], writes=[ysb])
                for h, dil in enumerate(DILS[1:], start=1):
                    p0 = PS_OFF[h]
                    w = 512 // dil
                    b.emit("vector", lambda e, p0=p0, w=w, dil=dil: e.tensor_add(
                        ysb[:, ::dil], ysb[:, ::dil], ps_y[:, p0:p0 + w]),
                        reads=[ps_y, ysb], writes=[ysb])
                b.emit("scalar", lambda e, oc=oc, oc4=oc4: e.activation(
                    out=dfm[:, oc4, :], in_=ysb[:], func=AF.Identity, scale=1.0,
                    bias=bout[:, oc:oc + 1]),
                    reads=[ysb, bout], writes=[dfm])
            for tt in range(4):
                ti = 16 * r + tau * 4 + tt
                for cc4 in range(4):
                    b.emit("sync", lambda e, cc4=cc4, tt=tt: e.dma_start(
                        out=dtm[:, cc4 * 128:(cc4 + 1) * 128],
                        in_=dfm[:, cc4, tt * 128:(tt + 1) * 128], transpose=True),
                        reads=[dfm], writes=[dtm], dma=True)
                b.emit("vector", lambda e, ti=ti, half=half: e.tensor_add(
                    resid[:, ti, 512 * half:512 * (half + 1)],
                    resid[:, ti, 512 * half:512 * (half + 1)], dtm[:, :]),
                    reads=[dtm, "resid"], writes=["resid"])


def alloc_attn_a(ctx):
    ctx.sbuf("woutT", [128, 4, D], BF16)
    ctx.sbuf("bout", [128, NCHUNK], F32)
    ctx.sbuf("amasks", [128, 4, 128], BF16)
    ctx.sbuf("aQ", [128, 512], BF16)
    ctx.sbuf("aVs", [128, 128], BF16)
    ctx.sbuf("aK", [128, 512], BF16)
    ctx.sbuf("aV", [128, 512], BF16)
    ctx.sbuf("aAtt", [128, 3840], BF16)
    ctx.sbuf("aSraw", [128, 128], F32)
    ctx.sbuf("aSm", [128, 128], BF16)
    ctx.sbuf("aRmax", [128, 1], F32)
    ctx.sbuf("aRsum", [128, 1], F32)
    ctx.sbuf("aY", [128, 512], F32)
    ctx.sbuf("aDfm", [128, 4, 512], BF16)
    ctx.sbuf("aDtm", [128, 512], BF16)
    ctx.sbuf("awq", [128, 1, NCHUNK, 128], BF16)
    ctx.sbuf("awk", [128, 1, NCHUNK, 128], BF16)
    ctx.sbuf("awv", [128, 1, NCHUNK, 128], BF16)
    ctx.psum("ps_qkv", [128, 512])
    ctx.psum("ps_s", [128, 128])
    ctx.psum("ps_av", [128, 128])
    ctx.psum("ps_y", [128, 1024])


def alloc_common(ctx):
    ctx.sbuf("resid", [128, NT, D], F32)
    ctx.sbuf("xfm", [128, NCHUNK, TROW], BF16)
    ctx.sbuf("xlt", [128, D], BF16)
    ctx.sbuf("lsq", [128, D], BF16)
    ctx.sbuf("lssq", [128, 1], F32)
    ctx.sbuf("ls", [128, 1], F32)
    ctx.sbuf("lm2", [128, 1], F32)
    ctx.sbuf("lbb", [128, 1], F32)
    ctx.sbuf("lmean", [128, NT], F32)
    ctx.sbuf("lrstd", [128, NT], F32)


def make_attn_a_masks():
    m = np.full((128, 4, 128), -1e30, np.float32)
    for h, dil in enumerate(DILS):
        L = 16 // dil
        for p in range(128):
            s1, n = divmod(p, L)
            for f in range(128):
                s2, mm = divmod(f, L)
                if s1 == s2 and n >= mm:
                    m[p, h, f] = 0.0
    return m


def alloc_attn_b(ctx):
    ctx.sbuf("bqf", [128, 8], F32)
    ctx.sbuf("bkf", [128, 8], F32)
    ctx.sbuf("bqk", [128, 8], F32)
    ctx.sbuf("bld", [16, 1], F32)
    ctx.sbuf("blam", [16, 1], F32)
    ctx.sbuf("bpm", [16, 8], F32)
    ctx.sbuf("blamp", [16, 8], F32)
    ctx.sbuf("bhsel", [128, 2], F32)
    ctx.sbuf("bE1", [16, 128], F32)
    ctx.sbuf("bE2", [2, 128], F32)
    ctx.sbuf("bnw", [128, 8], F32)
    ctx.sbuf("blamd", [128, 8], F32)
    ctx.sbuf("bcd", [128, 8], F32)
    ctx.sbuf("bscale", [128, 8], F32)
    ctx.sbuf("bones", [128, 1], BF16)
    ctx.sbuf("bonesf", [1, 128], F32)
    ctx.sbuf("byscan", [128, 1024], F32)
    ctx.sbuf("bsq", [128, 1024], BF16)
    ctx.sbuf("browrms", [1, 512], F32)
    ctx.sbuf("beps", [1, 1], F32)
    ctx.sbuf("browssb", [128, 512], F32)
    ctx.sbuf("bowT", [128, NCHUNK, D], BF16)
    ctx.sbuf("bY", [128, 512], F32)
    ctx.sbuf("bDfm", [128, 4, 512], BF16)
    ctx.sbuf("bDtm", [128, 512], BF16)
    ctx.psum("ps_b1", [128, 512])
    ctx.psum("ps_b2", [128, 512])
    ctx.psum("ps_rms", [1, TROW])
    ctx.psum("ps_tab", [16, 8])


def emit_attn_b_tables(ctx):
    b, t = ctx.b, ctx.t
    dr = ctx.dram
    for name, dst in (("bq_fm", "bqf"), ("bk_fm", "bkf"), ("bnw_fm", "bnw")):
        b.emit("gpsimd", lambda e, name=name, dst=dst: e.dma_start(
            out=t[dst][:, :], in_=dr[name][:, :]), reads=[name], writes=[dst], dma=True)
    for name, dst in (("bld_d", "bld"), ("bpm_d", "bpm"), ("bhsel_d", "bhsel"),
                      ("bE1_d", "bE1"), ("bE2_d", "bE2")):
        b.emit("gpsimd", lambda e, name=name, dst=dst: e.dma_start(
            out=t[dst][:, :], in_=dr[name][:, :]), reads=[name], writes=[dst], dma=True)
    b.emit("vector", lambda e: e.tensor_mul(t["bqk"][:], t["bqf"][:], t["bkf"][:]),
           reads=["bqf", "bkf"], writes=["bqk"])
    b.emit("tensor", lambda e: e.matmul(t["ps_tab"][0:2, :], t["bhsel"][:, :], t["bqk"][:, :],
                                        start=True, stop=True),
           reads=["bhsel", "bqk"], writes=["ps_tab"], group="btab1", group_end=True)
    b.emit("vector", lambda e: e.tensor_copy(t["bqk"][0:2, :], t["ps_tab"][0:2, :]),
           reads=["ps_tab"], writes=["bqk"])
    b.emit("tensor", lambda e: e.matmul(t["ps_b1"][:, 0:8], t["bE2"][0:2, :], t["bqk"][0:2, :], start=True, stop=True),
           reads=["bE2", "bqk"], writes=["ps_b1"], group="btab2", group_end=True)
    b.emit("vector", lambda e: e.tensor_copy(t["bcd"][:, :], t["ps_b1"][:, 0:8]),
           reads=["ps_b1"], writes=["bcd"])
    b.emit("scalar", lambda e: e.activation(out=t["blam"][:, :], in_=t["bld"][:, :], func=AF.Sigmoid),
           reads=["bld"], writes=["blam"])
    b.emit("vector", lambda e: e.tensor_scalar(t["blamp"][:, :], t["bpm"][:, :], t["blam"][:, 0:1], None, ALU.mult),
           reads=["bpm", "blam"], writes=["blamp"])
    b.emit("tensor", lambda e: e.matmul(t["ps_b1"][:, 0:8], t["bE1"][:, :], t["blamp"][:, :], start=True, stop=True),
           reads=["bE1", "blamp"], writes=["ps_b1"], group="btab3", group_end=True)
    b.emit("vector", lambda e: e.tensor_copy(t["blamd"][:, :], t["ps_b1"][:, 0:8]),
           reads=["ps_b1"], writes=["blamd"])
    b.emit("vector", lambda e: e.tensor_mul(t["bscale"][:, :], t["bcd"][:, :], t["bnw"][:, :]),
           reads=["bcd", "bnw"], writes=["bscale"])


def emit_attn_b(ctx):
    nc, b, t = ctx.nc, ctx.b, ctx.t
    resid = t["resid"]
    xfm = t["xfm"]; xlt = t["xlt"]
    yscan = t["byscan"]; sq = t["bsq"]
    rowrms = t["browrms"]; rmsb = t["browssb"]
    owT = t["bowT"]; ysb = t["bY"]; dfm = t["bDfm"]; dtm = t["bDtm"]
    ps1 = t["ps_b1"]; ps2 = t["ps_b2"]; psr = t["ps_rms"]
    b.emit("gpsimd", lambda e: e.dma_start(
        out=owT[:, :, :], in_=ctx.dram["bowT_d"].rearrange("(c p) od -> p c od", p=128)),
        reads=["bowT_d"], writes=["bowT"], dma=True)
    b.emit("gpsimd", lambda e: e.dma_start(
        out=t["bones"][:, :], in_=ctx.dram["ones_d"][:, 0:1]), reads=["ones_d"], writes=["bones"], dma=True)
    b.emit("gpsimd", lambda e: e.dma_start(
        out=t["bonesf"][:, :], in_=ctx.dram["onesf_d"][0:1, :]), reads=["onesf_d"], writes=["bonesf"], dma=True)
    b.emit("vector", lambda e: e.memset(t["beps"][:, :], 1e-6), writes=["beps"])
    for r in range(2):
        for i in range(NTR):
            ti = 16 * r + i
            emit_ln_tile(ctx, resid[:, ti, :], "resid", xlt[:, :], "xlt",
                         t["lmean"][:, 0:1], t["lrstd"][:, 0:1])
            emit_tm2fm(ctx, xlt, "xlt", lambda c, i=i: xfm[:, c, i * 128:(i + 1) * 128], "xfm",
                       eng="sync")
        for cc in range(NCHUNK):
            lam_b = t["blamd"][:, cc:cc + 1].broadcast_to((128, 1024))
            for hh in range(2):
                init = 0.0 if hh == 0 else yscan[:, 1023:1024]
                b.emit("vector", lambda e, cc=cc, hh=hh, init=init, lam_b=lam_b: e.tensor_tensor_scan(
                    yscan[:, :], lam_b, xfm[:, cc, hh * 1024:(hh + 1) * 1024], init, ALU.mult, ALU.add),
                    reads=["blamd", "xfm", "byscan"], writes=["byscan"])
                b.emit("scalar", lambda e, cc=cc: e.activation(
                    out=sq[:, :], in_=yscan[:, :], func=AF.Square, scale=t["bcd"][:, cc:cc + 1]),
                    reads=["byscan", "bcd"], writes=["bsq"])
                b.emit("scalar", lambda e, cc=cc, hh=hh: e.activation(
                    out=xfm[:, cc, hh * 1024:(hh + 1) * 1024], in_=yscan[:, :], func=AF.Identity, scale=t["bscale"][:, cc:cc + 1]),
                    reads=["byscan", "bscale"], writes=["xfm"])
                for tl in range(2):
                    tt = hh * 2 + tl
                    b.emit("tensor", lambda e, tt=tt, tl=tl, cc=cc: e.matmul(
                        psr[0:1, tt * 512:(tt + 1) * 512], t["bones"][:, :], sq[:, tl * 512:(tl + 1) * 512],
                        start=(cc == 0), stop=(cc == NCHUNK - 1)),
                        reads=["bones", "bsq"], writes=["ps_rms"], group=f"brms{r}{tt}",
                        group_end=(cc == NCHUNK - 1))
        for tt in range(4):
            b.emit("vector", lambda e, tt=tt: e.tensor_copy(rowrms[:, :], psr[0:1, tt * 512:(tt + 1) * 512]),
                   reads=["ps_rms"], writes=["browrms"])
            b.emit("scalar", lambda e: e.activation(out=rowrms[:, :], in_=rowrms[:, :],
                                                    func=AF.Sqrt, scale=1.0 / D, bias=t["beps"][:, :]),
                   reads=["browrms", "beps"], writes=["browrms"])
            b.emit("vector", lambda e: e.reciprocal(rowrms[:, :], rowrms[:, :]),
                   reads=["browrms"], writes=["browrms"])
            b.emit("tensor", lambda e: e.matmul(
                ps2[:, :], t["bonesf"][0:1, :], rowrms[0:1, :],
                start=True, stop=True),
                reads=["bonesf", "browrms"], writes=["ps_b2"], group=f"brb{r}{tt}", group_end=True)
            b.emit("vector", lambda e: e.tensor_copy(rmsb[:, :], ps2[:, :]),
                   reads=["ps_b2"], writes=["browssb"])
            for half in range(2):
                for oc4 in range(4):
                    oc = 4 * half + oc4
                    for cc in range(NCHUNK):
                        b.emit("tensor", lambda e, oc=oc, cc=cc, tt=tt: e.matmul(
                            ps1[:, :], owT[:, cc, oc * 128:(oc + 1) * 128],
                            xfm[:, cc, tt * 512:(tt + 1) * 512],
                            start=(cc == 0), stop=(cc == NCHUNK - 1)),
                            reads=["bowT", "xfm"], writes=["ps_b1"], group=f"bow{r}{tt}{oc}",
                            group_end=(cc == NCHUNK - 1))
                    b.emit("vector", lambda e: e.tensor_mul(ysb[:, :], ps1[:, :], rmsb[:, :]),
                           reads=["ps_b1", "browssb"], writes=["bY"])
                    b.emit("scalar", lambda e, oc4=oc4: e.activation(
                        out=dfm[:, oc4, :], in_=ysb[:], func=AF.Copy, scale=1.0, bias=0.0),
                        reads=["bY"], writes=["bDfm"])
                for tloc in range(4):
                    ti = 16 * r + tt * 4 + tloc
                    for cc4 in range(4):
                        b.emit("sync", lambda e, cc4=cc4, tloc=tloc: e.dma_start(
                            out=dtm[:, cc4 * 128:(cc4 + 1) * 128],
                            in_=dfm[:, cc4, tloc * 128:(tloc + 1) * 128], transpose=True),
                            reads=["bDfm"], writes=["bDtm"], dma=True)
                    b.emit("vector", lambda e, ti=ti, half=half: e.tensor_add(
                        resid[:, ti, 512 * half:512 * (half + 1)],
                        resid[:, ti, 512 * half:512 * (half + 1)], dtm[:, :]),
                        reads=["bDtm", "resid"], writes=["resid"])





def emit_tr_32x128(ctx, dst, dst_key, src, src_key):
    b = ctx.b
    for i in range(4):
        b.emit("vector", lambda e, i=i: e.transpose(
            dst[32 * i:32 * (i + 1), 0:32], src[0:32, 32 * i:32 * (i + 1)]),
            reads=[src_key], writes=[dst_key])


def emit_tr_128x32(ctx, dst, dst_key, src, src_key):
    b = ctx.b
    for i in range(4):
        b.emit("vector", lambda e, i=i: e.transpose(
            dst[0:32, 32 * i:32 * (i + 1)], src[32 * i:32 * (i + 1), 0:32]),
            reads=[src_key], writes=[dst_key])



def emit_select(ctx, out_ap, out_key, mask_ap, mask_keys, true_ap, true_keys, false_ap, false_keys, n=8):
    b, t = ctx.b, ctx.t
    scr = t["fsel"]
    sa = scr[:, 0:n]
    b.emit("vector", lambda e: e.tensor_sub(sa, false_ap, true_ap),
           reads=list(false_keys) + list(true_keys), writes=["fsel"])
    b.emit("vector", lambda e: e.tensor_mul(sa, sa, mask_ap),
           reads=list(mask_keys) + ["fsel"], writes=["fsel"])
    b.emit("vector", lambda e: e.tensor_sub(out_ap, false_ap, sa),
           reads=list(false_keys) + ["fsel"], writes=[out_key])

from concourse.bass_core_rust import IndirectOffsetOnAxis

NEXP = 7
HALF_PAD = 1280
EXEC_ROWS = 768
MROWS = NEXP * 2 * HALF_PAD + 4


def alloc_ffn(ctx, ncores):
    ctx.sbuf("lbb2", [128, NT], F32)
    ctx.sbuf("frwT", [128, NCHUNK, 8], BF16)
    ctx.sbuf("frb", [128, 8], F32)
    ctx.sbuf("fiota", [128, 8], F32)
    ctx.sbuf("f99", [128, 8], F32)
    ctx.sbuf("fp8", [128, 8], F32)
    ctx.sbuf("fsel", [128, 8], F32)
    ctx.sbuf("fm1", [128, 1], F32)
    ctx.sbuf("fm2", [128, 1], F32)
    ctx.sbuf("foh", [128, 8], F32)
    ctx.sbuf("find", [128, 8], F32)
    ctx.sbuf("findb", [128, 8], BF16)
    ctx.sbuf("fe1", [128, NT], F32)
    ctx.sbuf("fe2", [128, NT], F32)
    ctx.sbuf("fw1", [128, NT], F32)
    ctx.sbuf("fw2", [128, NT], F32)
    ctx.sbuf("fwn", [128, NT], F32)
    ctx.sbuf("fjr1", [128, NT], F32)
    ctx.sbuf("fjr2", [128, NT], F32)
    ctx.sbuf("fgx1", [128, NT], F32)
    ctx.sbuf("fgx2", [128, NT], F32)
    ctx.sbuf("fjrt", [8, 128], F32)
    ctx.sbuf("frun", [8, 1], F32)
    ctx.sbuf("fscr8w", [8, max(16, 2 * ncores)], F32)
    ctx.sbuf("fones8", [8, 1], F32)
    ctx.sbuf("fonesf", [1, 128], F32)
    ctx.sbuf("fcnt", [8, 2], F32)
    ctx.sbuf("fagt", [8, ncores * 2], F32)
    ctx.sbuf("fmsk", [8, 2 * ncores * 2], F32)
    ctx.sbuf("foff", [8, 2], F32)
    ctx.sbuf("foffb", [128, 2, 8], F32)
    ctx.sbuf("fhoff", [128, 1], F32)
    ctx.sbuf("ftrash", [128, 1], F32)
    ctx.sbuf("fzrowc", [128, 1], F32)
    ctx.sbuf("fgidx", [128, 1], I32)
    ctx.sbuf("fgidx2", [128, 1], I32)
    ctx.sbuf("fscr", [128, 8], F32)
    ctx.sbuf("fscr1", [128, 1], F32)
    ctx.sbuf("fscr2", [128, 1], F32)
    ctx.sbuf("fkept", [128, 1], F32)
    ctx.sbuf("stab", [1, 256], I32)


def alloc_ffn_stage(ctx):
    ctx.sbuf("fxf", [128, NCHUNK, 128], BF16)
    ctx.sbuf("flgT", [32, 128], F32)
    ctx.sbuf("ftp", [128, 32], F32)
    ctx.sbuf("fz0", [128, D], BF16)
    ctx.psum("fps1", [128, 512])
    ctx.psum("fps2", [128, 512])
    ctx.psum("fps3", [128, 512])
    ctx.psum("fps4", [128, 512])


def alloc_ffn_expert(ctx):
    nc = ctx.nc
    resid_off = nc.lookup_mloc(ctx.t["resid"]).addr
    xfm_off = nc.lookup_mloc(ctx.t["xfm"]).addr
    offs = [0]
    def at(name, shape, dtype, base=None):
        nbytes = int(np.prod(shape[1:])) * (2 if dtype == BF16 else 4)
        if base is None:
            h = nc.alloc_sbuf_tensor_at(name, shape, dtype, offset=resid_off + offs[0])
            offs[0] += nbytes
        else:
            h = nc.alloc_sbuf_tensor_at(name, shape, dtype, offset=base)
        ctx.t[name] = h
        return h
    at("fhfm", [128, 32, EXEC_ROWS], BF16)
    at("fxg", [128, NCHUNK, EXEC_ROWS], BF16)
    at("fyfm", [128, NCHUNK, EXEC_ROWS], BF16)
    at("fxgt6", [128, 6, D], BF16)
    at("fytm6", [128, 6, D], BF16)
    at("fzero", [128, D], BF16)
    assert offs[0] <= NT * D * 4, f"expert alias overflow {offs[0]}"
    at("fwi", [128, NCHUNK, 1024], BF16, base=xfm_off)
    at("fwo", [128, 32, 256], BF16, base=xfm_off + NCHUNK * 1024 * 2)
    at("frt", [128, D], F32, base=xfm_off)
    at("fgath1", [128, D], BF16, base=xfm_off + 4096)
    at("fgath2", [128, D], BF16, base=xfm_off + 6144)
    at("fdelta", [128, D], F32, base=xfm_off + 8192)


EXPERT_KEYS = ["fhfm", "fxg", "fyfm", "fwi", "fwo", "fxgt6", "fytm6",
               "fzero", "frt", "fgath1", "fgath2", "fdelta"]

ST_CI = 0
ST_CO = 64
ST_SUB = 192
ST_ZROW = 199
ST_ZF = 200
ST_YTRASH = 207
QSEQ = [0, 1, 2, 0, 1, 2, 3]




def _dyn_reg(ctx, e, eng):
    import bass_rust
    regs = ctx.t.setdefault("_regs", {})
    key = "dyn_" + eng
    if key not in regs:
        h = e.register("dynreg").__enter__()
        regs[key] = (h, bass_rust.RuntimeValue(h, min_val=0, max_val=1 << 27))
    return regs[key]


def emit_dyn_dma(ctx, stab_idx, dram_base, dims, sbuf_ap, write_to_dram,
                 reads, writes, tag=None, eng="gpsimd"):
    b, t = ctx.b, ctx.t
    from concourse.ap import AP
    rk = "dynreg_" + eng
    chk = "dynch_" + eng
    def ld(e, stab_idx=stab_idx, eng=eng):
        h, rv = _dyn_reg(ctx, e, eng)
        e.reg_load(h, t["stab"][0:1, stab_idx:stab_idx + 1])
    b.emit(eng, ld, reads=["stab"], writes=[rk], noinstr=True, local=True)
    def go(e, eng=eng):
        h, rv = ctx.t["_regs"]["dyn_" + eng]
        dap = AP(dram_base.tensor, rv, [list(d) for d in dims])
        if write_to_dram:
            return e.dma_start(out=dap, in_=sbuf_ap)
        return e.dma_start(out=sbuf_ap, in_=dap)
    b.emit(eng, go, reads=list(reads) + [rk, chk],
           writes=list(writes) + [chk], dma=True, tag=tag)

def alloc_convert(ctx):
    nc = ctx.nc
    xfm_off = nc.lookup_mloc(ctx.t["xfm"]).addr
    ctx.t["cvt32"] = nc.alloc_sbuf_tensor_at("cvt32", [128, 4096], F32, offset=xfm_off)
    ctx.t["cvt16"] = nc.alloc_sbuf_tensor_at("cvt16", [128, 4096], BF16, offset=xfm_off + 4096 * 4)


def emit_ffn_convert_q(ctx, q):
    nc, b, t = ctx.nc, ctx.b, ctx.t
    dr = ctx.dram
    c32, c16 = t["cvt32"], t["cvt16"]
    for dc in range(NCHUNK):
        emit_dyn_dma(ctx, ST_CI + q * 8 + dc, dr["expw_i"],
                     [[4096, 128], [1, 4096]], c32[:, :], False,
                     reads=["expw_i"], writes=["cvt32"], eng="sync")
        b.emit("vector", lambda e: e.tensor_copy(c16[:, :], c32[:, :]),
               reads=["cvt32"], writes=["cvt16"])
        b.emit("gpsimd", lambda e, q=q, dc=dc: e.dma_start(
            out=dr["wscr_i"][q, dc, :, :], in_=c16[:, :]),
            reads=["cvt16"], writes=["wscr_i"], dma=True)
    for icg in range(8):
        emit_dyn_dma(ctx, ST_CO + q * 8 + icg, dr["expw_o"],
                     [[1024, 128], [131072, 4], [1, 1024]],
                     c32[:, :].rearrange("p (g f) -> p g f", g=4), False,
                     reads=["expw_o"], writes=["cvt32"], eng="scalar")
        b.emit("scalar", lambda e: e.activation(out=c16[:, :], in_=c32[:, :], func=AF.Copy),
               reads=["cvt32"], writes=["cvt16"])
        b.emit("gpsimd", lambda e, q=q, icg=icg: e.dma_start(
            out=dr["wscr_o"][q, icg * 4:(icg + 1) * 4].rearrange("g p f -> p g f"),
            in_=c16[:, :].rearrange("p (g f) -> p g f", g=4)),
            reads=["cvt16"], writes=["wscr_o"], dma=True)


def emit_ffn(ctx, call, ncores, cap, out_dram=None):
    nc, b, t = ctx.nc, ctx.b, ctx.t
    import bass_rust
    S = f"F{call}_"
    resid = t["resid"]
    dr = ctx.dram
    mslab, yslab = dr["mslab"], dr["yslab"]
    mkey, ykey = "mslab" + S, "yslab" + S
    b.inherit(mkey, ["mslab", "mslabF0_", "yslabF0_"])
    b.inherit(ykey, ["yslab", "mslabF0_", "yslabF0_"])
    xlt = t["xlt"]; fxf = t["fxf"]
    flgT, ftp, fz0 = t["flgT"], t["ftp"], t["fz0"]
    fps1, fps2, fps3, fps4 = t["fps1"], t["fps2"], t["fps3"], t["fps4"]
    b.emit("vector", lambda e: e.memset(flgT[:, :], 0.0), writes=["flgT"])
    b.emit("vector", lambda e: e.memset(ftp[:, :], 0.0), writes=["ftp"])
    b.emit("vector", lambda e: e.memset(t["fonesf"][:, :], 1.0), writes=["fonesf"])
    b.emit("vector", lambda e: e.memset(t["frun"][:, :], 0.0), writes=["frun"])
    b.emit("vector", lambda e: e.memset(t["fones8"][:, :], 1.0), writes=["fones8"])
    if call == 0:
        b.emit("vector", lambda e: e.memset(fz0[:, :], 0.0), writes=["fz0"])
        for ee in range(NEXP):
            emit_dyn_dma(ctx, ST_ZF + ee, mslab,
                         [[1024, 128], [131072, 6], [1, 1024]],
                         fz0[:, None, :].broadcast_to((128, 6, D)), True,
                         reads=["fz0"], writes=[mkey])
            emit_dyn_dma(ctx, ST_ZF + ee, yslab,
                         [[1024, 128], [131072, 10], [1, 1024]],
                         fz0[:, None, :].broadcast_to((128, 10, D)), True,
                         reads=["fz0"], writes=[ykey])
        emit_dyn_dma(ctx, ST_YTRASH, yslab, [[1024, 1], [1, 1024]],
                     fz0[0:1, :], True, reads=["fz0"], writes=[ykey])
    for ti in range(NT):
        emit_ln_tile(ctx, resid[:, ti, :], "resid", xlt[:, :], "xlt",
                     t["lmean"][:, ti:ti + 1], t["lrstd"][:, ti:ti + 1],
                     bneg_ap=t["lbb2"][:, ti:ti + 1], bneg_key="lbb2")
        emit_tm2fm(ctx, xlt, "xlt", lambda c: fxf[:, c, :], "fxf", eng="sync")
        for cc in range(NCHUNK):
            b.emit("tensor", lambda e, cc=cc: e.matmul(
                fps1[0:8, 0:128], t["frwT"][:, cc, :], fxf[:, cc, :],
                start=(cc == 0), stop=(cc == NCHUNK - 1)),
                reads=["frwT", "fxf"], writes=["fps1"], group=S + f"lg{ti}",
                group_end=(cc == NCHUNK - 1))
        b.emit("vector", lambda e: e.tensor_copy(flgT[0:8, :], fps1[0:8, 0:128]),
               reads=["fps1"], writes=["flgT"])
        emit_tr_32x128(ctx, ftp, "ftp", flgT, "flgT")
        b.emit("vector", lambda e: e.tensor_add(t["fp8"][:, :], ftp[:, 0:8], t["frb"][:, :]),
               reads=["ftp", "frb"], writes=["fp8"])
        b.emit("vector", lambda e: e.reduce_max(t["fm1"][:, :], t["fp8"][:, :], axis=AX.X, negate=True),
               reads=["fp8"], writes=["fm1"])
        b.emit("scalar", lambda e: e.activation(out=t["fp8"][:, :], in_=t["fp8"][:, :], func=AF.Exp, bias=t["fm1"][:, :]),
               reads=["fp8", "fm1"], writes=["fp8"])
        b.emit("vector", lambda e: e.reduce_sum(t["fm1"][:, :], t["fp8"][:, :], axis=AX.X),
               reads=["fp8"], writes=["fm1"])
        b.emit("vector", lambda e: e.reciprocal(t["fm1"][:, :], t["fm1"][:, :]),
               reads=["fm1"], writes=["fm1"])
        b.emit("vector", lambda e: e.tensor_scalar(t["fp8"][:, :], t["fp8"][:, :], t["fm1"][:, :], None, ALU.mult),
               reads=["fp8", "fm1"], writes=["fp8"])
        b.emit("vector", lambda e: e.reduce_max(t["fm1"][:, :], t["fp8"][:, :], axis=AX.X),
               reads=["fp8"], writes=["fm1"])
        b.emit("vector", lambda e: e.tensor_scalar(t["foh"][:, :], t["fp8"][:, :], t["fm1"][:, :], None, ALU.is_ge),
               reads=["fp8", "fm1"], writes=["foh"])
        b.emit("vector", lambda e: e.scalar_tensor_tensor(t["fscr"][:, :], t["foh"][:, :], -2.0, t["fp8"][:, :], ALU.mult, ALU.add),
               reads=["foh", "fp8"], writes=["fscr"])
        b.emit("vector", lambda e: e.reduce_max(t["fm2"][:, :], t["fscr"][:, :], axis=AX.X),
               reads=["fscr"], writes=["fm2"])
        b.emit("vector", lambda e: e.tensor_scalar(t["find"][:, :], t["fp8"][:, :], t["fm2"][:, :], None, ALU.is_ge),
               reads=["fp8", "fm2"], writes=["find"])
        b.emit("scalar", lambda e: e.activation(out=t["findb"][:, :], in_=t["find"][:, :], func=AF.Copy),
               reads=["find"], writes=["findb"])
        b.emit("vector", lambda e: e.tensor_mul(t["fscr"][:, :], t["fp8"][:, :], t["find"][:, :]),
               reads=["fp8", "find"], writes=["fscr"])
        b.emit("vector", lambda e, ti=ti: e.tensor_copy(t["fwn"][:, ti:ti + 1], t["fscr"][:, 7:8]),
               reads=["fscr"], writes=["fwn"])
        emit_select(ctx, t["fscr"][:, 0:7], "fscr", t["find"][:, 0:7], ["find"],
                    t["fiota"][:, 0:7], ["fiota"], t["f99"][:, 0:7], ["f99"], n=7)
        b.emit("vector", lambda e, ti=ti: e.tensor_reduce(t["fe1"][:, ti:ti + 1], t["fscr"][:, 0:7], AX.X, ALU.min),
               reads=["fscr"], writes=["fe1"])
        b.emit("vector", lambda e, ti=ti: e.tensor_scalar(t["foh"][:, 0:7], t["fiota"][:, 0:7], t["fe1"][:, ti:ti + 1], None, ALU.is_equal),
               reads=["fiota", "fe1"], writes=["foh"])
        b.emit("vector", lambda e: e.tensor_scalar(t["fsel"][:, 0:7], t["fscr"][:, 0:7], -99.0, None, ALU.add),
               reads=["fscr"], writes=["fsel"])
        b.emit("vector", lambda e: e.tensor_mul(t["fsel"][:, 0:7], t["fsel"][:, 0:7], t["foh"][:, 0:7]),
               reads=["fsel", "foh"], writes=["fsel"])
        b.emit("vector", lambda e: e.tensor_sub(t["fscr"][:, 0:7], t["fscr"][:, 0:7], t["fsel"][:, 0:7]),
               reads=["fscr", "fsel"], writes=["fscr"])
        b.emit("vector", lambda e, ti=ti: e.tensor_reduce(t["fe2"][:, ti:ti + 1], t["fscr"][:, 0:7], AX.X, ALU.min),
               reads=["fscr"], writes=["fe2"])
        b.emit("vector", lambda e: e.tensor_mul(t["fscr"][:, 0:7], t["foh"][:, 0:7], t["fp8"][:, 0:7]),
               reads=["foh", "fp8"], writes=["fscr"])
        b.emit("vector", lambda e, ti=ti: e.tensor_reduce(t["fw1"][:, ti:ti + 1], t["fscr"][:, 0:7], AX.X, ALU.add),
               reads=["fscr"], writes=["fw1"])
        b.emit("vector", lambda e, ti=ti: e.tensor_scalar(t["fscr"][:, 0:7], t["fiota"][:, 0:7], t["fe2"][:, ti:ti + 1], None, ALU.is_equal),
               reads=["fiota", "fe2"], writes=["fscr"])
        b.emit("vector", lambda e: e.tensor_mul(t["fscr"][:, 0:7], t["fscr"][:, 0:7], t["fp8"][:, 0:7]),
               reads=["fscr", "fp8"], writes=["fscr"])
        b.emit("vector", lambda e, ti=ti: e.tensor_reduce(t["fw2"][:, ti:ti + 1], t["fscr"][:, 0:7], AX.X, ALU.add),
               reads=["fscr"], writes=["fw2"])
        b.emit("vector", lambda e: e.tensor_copy(ftp[:, 0:8], t["findb"][:, :]),
               reads=["findb"], writes=["ftp"])
        emit_tr_128x32(ctx, flgT, "flgT", ftp, "ftp")
        b.emit("vector", lambda e: e.tensor_tensor_scan(
            t["fjrt"][:, :], t["fones8"][:, 0:1].broadcast_to((8, 128)),
            flgT[0:8, :], t["frun"][:, 0:1], ALU.mult, ALU.add),
            reads=["fones8", "flgT", "frun"], writes=["fjrt"])
        b.emit("vector", lambda e: e.tensor_copy(t["frun"][:, :], t["fjrt"][:, 127:128]),
               reads=["fjrt"], writes=["frun"])
        if ti == 15:
            b.emit("vector", lambda e: e.tensor_copy(t["fcnt"][:, 0:1], t["frun"][:, :]),
                   reads=["frun"], writes=["fcnt"])
        if ti == 31:
            b.emit("vector", lambda e: e.tensor_copy(t["fcnt"][:, 1:2], t["frun"][:, :]),
                   reads=["frun"], writes=["fcnt"])
            b.emit("vector", lambda e: e.tensor_sub(t["fcnt"][:, 1:2], t["fcnt"][:, 1:2], t["fcnt"][:, 0:1]),
                   reads=["fcnt"], writes=["fcnt"])
        b.emit("vector", lambda e: e.tensor_sub(t["fjrt"][:, :], t["fjrt"][:, :], flgT[0:8, :]),
               reads=["fjrt", "flgT"], writes=["fjrt"])
        b.emit("vector", lambda e: e.tensor_copy(flgT[0:8, :], t["fjrt"][:, :]),
               reads=["fjrt"], writes=["flgT"])
        emit_tr_32x128(ctx, ftp, "ftp", flgT, "flgT")
        for (esel, jrn) in (("fe1", "fjr1"), ("fe2", "fjr2")):
            b.emit("vector", lambda e, ti=ti, esel=esel: e.tensor_scalar(
                t["foh"][:, :], t["fiota"][:, :], t[esel][:, ti:ti + 1], None, ALU.is_equal),
                reads=["fiota", esel], writes=["foh"])
            b.emit("vector", lambda e: e.tensor_mul(t["fscr"][:, :], t["foh"][:, :], ftp[:, 0:8]),
                   reads=["foh", "ftp"], writes=["fscr"])
            b.emit("vector", lambda e, ti=ti, jrn=jrn: e.tensor_reduce(
                t[jrn][:, ti:ti + 1], t["fscr"][:, :], AX.X, ALU.add),
                reads=["fscr"], writes=[jrn])
        for (esel, jrn, gxn, gidxn) in (("fe1", "fjr1", "fgx1", "fgidx"),
                                        ("fe2", "fjr2", "fgx2", "fgidx2")):
            b.emit("vector", lambda e, ti=ti, esel=esel, jrn=jrn: e.scalar_tensor_tensor(
                t["fscr1"][:, :], t[esel][:, ti:ti + 1], 2.0 * HALF_PAD, t[jrn][:, ti:ti + 1],
                ALU.mult, ALU.add), reads=[esel, jrn], writes=["fscr1"])
            b.emit("vector", lambda e: e.tensor_add(t["fscr1"][:, :], t["fscr1"][:, :], t["fhoff"][:, :]),
                   reads=["fscr1", "fhoff"], writes=["fscr1"])
            b.emit("vector", lambda e: e.tensor_tensor(t["fscr1"][:, :], t["fscr1"][:, :], t["ftrash"][:, :], ALU.min),
                   reads=["fscr1", "ftrash"], writes=["fscr1"])
            b.emit("vector", lambda e, ti=ti, gxn=gxn: e.tensor_copy(t[gxn][:, ti:ti + 1], t["fscr1"][:, :]),
                   reads=["fscr1"], writes=[gxn])
            b.emit("vector", lambda e, gidxn=gidxn: e.tensor_copy(t[gidxn][:, :], t["fscr1"][:, :]),
                   reads=["fscr1"], writes=[gidxn])
            b.emit("gpsimd", lambda e, gidxn=gidxn: e.indirect_dma_start(
                mslab[:, :], IndirectOffsetOnAxis(t[gidxn][:, 0:1], 0), xlt[:, :], None),
                reads=[gidxn, "xlt"], writes=[mkey], dma=True)
    for qq in range(4):
        b.emit("gpsimd", lambda e, qq=qq: e.dma_start(
            out=dr["respill"][qq * 8:(qq + 1) * 8].rearrange("T p f -> p T f"),
            in_=resid[:, qq * 8:(qq + 1) * 8, :]),
            reads=["resid"], writes=["respill"], dma=True)
    b.emit("gpsimd", lambda e: e.dma_start(out=dr["agin"][:, :], in_=t["fcnt"][:, :]),
           reads=["fcnt"], writes=["agin"], dma=True)
    b.emit("gpsimd", lambda e, ncores=ncores: e.collective_compute(
        "AllGather", ALU.bypass, replica_groups=[list(range(ncores))],
        ins=[dr["agin"][:, :]], outs=[dr["agout"][:, :]]),
        reads=["agin", mkey], writes=["agout"], tag="AG")
    b.emit("gpsimd", lambda e: e.dma_start(
        out=t["fagt"][:, :].rearrange("e (c r) -> e c r", r=2),
        in_=dr["agout"].rearrange("(c e) r -> e c r", e=8)),
        reads=["agout"], writes=["fagt"], dma=True)
    for row in range(2):
        b.emit("vector", lambda e, row=row, ncores=ncores: e.tensor_mul(
            t["fscr8w"][:, 0:2 * ncores], t["fagt"][:, :],
            t["fmsk"][:, row * 2 * ncores:(row + 1) * 2 * ncores]),
            reads=["fagt", "fmsk"], writes=["fscr8w"])
        b.emit("vector", lambda e, row=row, ncores=ncores: e.tensor_reduce(
            t["foff"][:, row:row + 1], t["fscr8w"][:, 0:2 * ncores], AX.X, ALU.add),
            reads=["fscr8w"], writes=["foff"])
    for row in range(2):
        b.emit("vector", lambda e: e.memset(flgT[0:32, 0:32], 0.0), writes=["flgT"])
        b.emit("vector", lambda e, row=row: e.tensor_copy(flgT[0:8, 0:1], t["foff"][:, row:row + 1]),
               reads=["foff"], writes=["flgT"])
        b.emit("vector", lambda e: e.transpose(ftp[0:32, 0:32], flgT[0:32, 0:32]),
               reads=["flgT"], writes=["ftp"])
        b.emit("tensor", lambda e: e.matmul(
            fps1[0:128, 0:8], t["fonesf"][0:1, :], ftp[0:1, 0:8],
            start=True, stop=True),
            reads=["fonesf", "ftp"], writes=["fps1"], group=S + f"offb{row}", group_end=True)
        b.emit("vector", lambda e, row=row: e.tensor_copy(t["foffb"][:, row, :], fps1[0:128, 0:8]),
               reads=["fps1"], writes=["foffb"])
    ekeys = EXPERT_KEYS
    for k in ekeys:
        base = ["resid", "xfm"] if k in ("fwi", "fwo") else ["resid"]
        b.inherit(k + S, base + ([k + "F0_"] if call == 1 else []))
    hk, xgk, yk, wik, wok, xgt6k, ytm6k, zk, rtk, g1k, g2k, dk = [k + S for k in ekeys]
    fhfm, fxg, fyfm, fwi, fwo, fxgt6, fytm6, fzero, frt, fg1b, fg2b, fdelta = [t[k] for k in ekeys]
    b.emit("vector", lambda e: e.memset(fzero[:, :], 0.0), writes=[zk])
    b.emit("vector", lambda e: e.memset(fzero[:, :], 0.0), writes=[zk])
    emit_dyn_dma(ctx, ST_ZROW, yslab, [[1024, 1], [1, 1024]],
                 fzero[0:1, :], True, reads=[zk], writes=[ykey], eng="scalar")
    done_q = ctx.t.setdefault("_doneq", set())
    for j in range(NEXP):
        q = QSEQ[j]
        if call == 0 and q not in done_q:
            emit_ffn_convert_q(ctx, q)
            done_q.add(q)
        emit_dyn_dma(ctx, ST_SUB + j, mslab,
                     [[1024, 128], [131072, 6], [1, 1024]],
                     fxgt6[:, :, :], False, reads=[mkey], writes=[xgt6k])
        for kk in range(6):
            for cc in range(NCHUNK):
                b.emit("sync", lambda e, kk=kk, cc=cc: e.dma_start(
                    out=fxg[:, cc, kk * 128:(kk + 1) * 128],
                    in_=fxgt6[:, kk, cc * 128:(cc + 1) * 128], transpose=True),
                    reads=[xgt6k], writes=[xgk], dma=True)
        for g in range(4):
            b.emit("gpsimd", lambda e, g=g, q=q: e.dma_start(
                out=fwi[:, :, :],
                in_=dr["wscr_i"][q, :, :, g * 1024:(g + 1) * 1024].rearrange("dc p i -> p dc i")),
                reads=["wscr_i"], writes=[wik], dma=True)
            for icl in range(8):
                ic = g * 8 + icl
                for (pk, lo, w) in (("fps1", 0, 512), ("fps2", 512, 256)):
                    ps = t[pk]
                    for dc in range(NCHUNK):
                        b.emit("tensor", lambda e, ps=ps, icl=icl, dc=dc, lo=lo, w=w: e.matmul(
                            ps[:, 0:w], fwi[:, dc, icl * 128:(icl + 1) * 128],
                            fxg[:, dc, lo:lo + w],
                            start=(dc == 0), stop=(dc == NCHUNK - 1)),
                            reads=[wik, xgk], writes=[pk], group=S + f"h{j}_{ic}_{lo}",
                            group_end=(dc == NCHUNK - 1))
                b.emit("scalar", lambda e, ic=ic: e.activation(
                    out=fhfm[:, ic, 0:512], in_=fps1[:, 0:512], func=AF.Relu),
                    reads=["fps1"], writes=[hk])
                b.emit("scalar", lambda e, ic=ic: e.activation(
                    out=fhfm[:, ic, 512:768], in_=fps2[:, 0:256], func=AF.Relu),
                    reads=["fps2"], writes=[hk])
        for og in range(4):
            b.emit("gpsimd", lambda e, og=og, q=q: e.dma_start(
                out=fwo[:, :, :],
                in_=dr["wscr_o"][q, :, :, og * 256:(og + 1) * 256].rearrange("ic p o -> p ic o")),
                reads=["wscr_o"], writes=[wok], dma=True)
            for ocl in range(2):
                oc = og * 2 + ocl
                for (pk, lo, w) in (("fps3", 0, 512), ("fps4", 512, 256)):
                    ps = t[pk]
                    for ic in range(32):
                        b.emit("tensor", lambda e, ps=ps, ocl=ocl, ic=ic, lo=lo, w=w: e.matmul(
                            ps[:, 0:w], fwo[:, ic, ocl * 128:(ocl + 1) * 128],
                            fhfm[:, ic, lo:lo + w],
                            start=(ic == 0), stop=(ic == 31)),
                            reads=[wok, hk], writes=[pk], group=S + f"y{j}_{oc}_{lo}",
                            group_end=(ic == 31))
                b.emit("scalar", lambda e, oc=oc: e.activation(
                    out=fyfm[:, oc, 0:512], in_=fps3[:, 0:512], func=AF.Copy),
                    reads=["fps3"], writes=[yk])
                b.emit("scalar", lambda e, oc=oc: e.activation(
                    out=fyfm[:, oc, 512:768], in_=fps4[:, 0:256], func=AF.Copy),
                    reads=["fps4"], writes=[yk])
        for kk in range(6):
            for cc in range(NCHUNK):
                b.emit("sync", lambda e, kk=kk, cc=cc: e.dma_start(
                    out=fytm6[:, kk, cc * 128:(cc + 1) * 128],
                    in_=fyfm[:, cc, kk * 128:(kk + 1) * 128], transpose=True),
                    reads=[yk], writes=[ytm6k], dma=True)
        emit_dyn_dma(ctx, ST_SUB + j, yslab,
                     [[1024, 128], [131072, 6], [1, 1024]],
                     fytm6[:, :, :], True, reads=[ytm6k], writes=[ykey])
    groups = [[2 * p, 2 * p + 1] for p in range(ncores // 2)]
    b.emit("gpsimd", lambda e, groups=groups: e.collective_compute(
        "AllReduce", ALU.add, replica_groups=groups,
        ins=[dr["barin"][:, :]], outs=[dr["barout"][:, :]]),
        reads=[ykey, "barin"], writes=["ybar" + S], tag="BAR")
    b.inherit("resid", [mkey] + [k + S for k in ekeys])
    for k in (rtk, g1k, g2k, dk):
        b.inherit(k, [wik])
    for ti in range(NT):
        row = ti // 16
        b.emit("gpsimd", lambda e, ti=ti: e.dma_start(
            out=frt[:, :], in_=dr["respill"][ti]),
            reads=["respill"], writes=[rtk], dma=True)
        b.emit("scalar", lambda e, ti=ti: e.activation(
            out=xlt[:, :], in_=frt[:, :], func=AF.Identity,
            scale=t["lrstd"][:, ti:ti + 1], bias=t["lbb2"][:, ti:ti + 1]),
            reads=[rtk, "lrstd", "lbb2"], writes=["xlt"])
        for (esel, jrn, gxn, gidxn, gbuf, gbk) in (
                ("fe1", "fjr1", "fgx1", "fgidx", fg1b, g1k),
                ("fe2", "fjr2", "fgx2", "fgidx2", fg2b, g2k)):
            b.emit("vector", lambda e, ti=ti, esel=esel: e.tensor_scalar(
                t["foh"][:, :], t["fiota"][:, :], t[esel][:, ti:ti + 1], None, ALU.is_equal),
                reads=["fiota", esel], writes=["foh"])
            b.emit("vector", lambda e, row=row: e.tensor_mul(t["fscr"][:, :], t["foh"][:, :], t["foffb"][:, row, :]),
                   reads=["foh", "foffb"], writes=["fscr"])
            b.emit("vector", lambda e: e.tensor_reduce(t["fscr1"][:, :], t["fscr"][:, :], AX.X, ALU.add),
                   reads=["fscr"], writes=["fscr1"])
            b.emit("vector", lambda e, ti=ti, jrn=jrn: e.tensor_add(t["fscr1"][:, :], t["fscr1"][:, :], t[jrn][:, ti:ti + 1]),
                   reads=["fscr1", jrn], writes=["fscr1"])
            b.emit("vector", lambda e, cap=cap: e.tensor_single_scalar(t["fkept"][:, :], t["fscr1"][:, :], float(cap), ALU.is_lt),
                   reads=["fscr1"], writes=["fkept"])
            b.emit("vector", lambda e, ti=ti, esel=esel: e.tensor_single_scalar(t["fscr2"][:, :], t[esel][:, ti:ti + 1], 50.0, ALU.is_lt),
                   reads=[esel], writes=["fscr2"])
            b.emit("vector", lambda e: e.tensor_mul(t["fkept"][:, :], t["fkept"][:, :], t["fscr2"][:, :]),
                   reads=["fkept", "fscr2"], writes=["fkept"])
            emit_select(ctx, t["fscr1"][:, :], "fscr1", t["fkept"][:, :], ["fkept"],
                        t[gxn][:, ti:ti + 1], [gxn], t["fzrowc"][:, :], ["fzrowc"], n=1)
            b.emit("vector", lambda e, gidxn=gidxn: e.tensor_copy(t[gidxn][:, :], t["fscr1"][:, :]),
                   reads=["fscr1"], writes=[gidxn])
            b.emit("gpsimd", lambda e, gidxn=gidxn, gbuf=gbuf: e.indirect_dma_start(
                gbuf[:, :], None, yslab[:, :], IndirectOffsetOnAxis(t[gidxn][:, 0:1], 0)),
                reads=[gidxn, ykey, "ybar" + S], writes=[gbk], dma=True)
        b.emit("vector", lambda e, ti=ti: e.tensor_scalar(
            fdelta[:, :], fg1b[:, :], t["fw1"][:, ti:ti + 1], None, ALU.mult),
            reads=[g1k, "fw1"], writes=[dk])
        b.emit("vector", lambda e, ti=ti: e.scalar_tensor_tensor(
            fdelta[:, :], fg2b[:, :], t["fw2"][:, ti:ti + 1], fdelta[:, :], ALU.mult, ALU.add),
            reads=[g2k, "fw2", dk], writes=[dk])
        b.emit("vector", lambda e, ti=ti: e.scalar_tensor_tensor(
            fdelta[:, :], xlt[:, :], t["fwn"][:, ti:ti + 1], fdelta[:, :], ALU.mult, ALU.add),
            reads=["xlt", "fwn", dk], writes=[dk])
        b.emit("vector", lambda e, ti=ti: e.tensor_add(resid[:, ti, :], frt[:, :], fdelta[:, :]),
               reads=[rtk, dk], writes=["resid"])
        if out_dram is not None:
            b.emit("gpsimd", lambda e, ti=ti: e.dma_start(
                out=out_dram[ti], in_=resid[:, ti, :]),
                reads=["resid"], writes=["outd"], dma=True)


def emit_ffn_consts(ctx):
    b, t, dr = ctx.b, ctx.t, ctx.dram
    b.emit("gpsimd", lambda e: e.dma_start(
        out=t["frwT"][:, :, :], in_=dr["rwT_d"].rearrange("(c p) e -> p c e", p=128)),
        reads=["rwT_d"], writes=["frwT"], dma=True)
    for name, dst in (("rb_bc", "frb"), ("iota_d", "fiota"), ("c99_d", "f99"),
                      ("hoff_d", "fhoff"), ("trash_d", "ftrash"), ("zrowc_d", "fzrowc"),
                      ("fmsk_d", "fmsk")):
        b.emit("gpsimd", lambda e, name=name, dst=dst: e.dma_start(
            out=t[dst][:], in_=dr[name][:]), reads=[name], writes=[dst], dma=True)
    b.emit("gpsimd", lambda e: e.dma_start(out=t["stab"][:, :], in_=dr["stab_d"][:, :]),
           reads=["stab_d"], writes=["stab"], dma=True)
    b.emit("gpsimd", lambda e: e.dma_start(out=dr["barin"][0:1, 0:8], in_=t["frb"][0:1, 0:8]),
           reads=["frb"], writes=["barin"], dma=True)


TRASH0 = NEXP * 2 * HALF_PAD
"""Host-side input prep for the FFN (and full kernel): per-core tables/consts.
Layout-only numpy (transpose/reshape/broadcast/constants) -- no arithmetic on
input VALUES."""

TRASH0 = NEXP * 2 * HALF_PAD


def core_rows(c, ncores):
    if ncores == 8:
        P, h = divmod(c, 2)
        return [P + 4 * h, P + 4 * h + 8]
    elif ncores == 2:
        return [c, c + 2]
    raise ValueError(ncores)


def ffn_core_inputs(core, ncores, router_w, router_b):
    half = core % 2
    nb = 2 * ncores
    rows = core_rows(core, ncores)
    d = {}
    d["rwT_d"] = np.ascontiguousarray(router_w.T)
    d["rb_bc"] = np.ascontiguousarray(np.broadcast_to(router_b[None, :], (128, 8)))
    d["iota_d"] = np.ascontiguousarray(np.broadcast_to(np.arange(8, dtype=np.float32)[None, :], (128, 8)))
    d["c99_d"] = np.full((128, 8), 99.0, np.float32)
    d["hoff_d"] = np.full((128, 1), float(half * HALF_PAD), np.float32)
    d["trash_d"] = np.full((128, 1), float(TRASH0 + half), np.float32)
    d["zrowc_d"] = np.full((128, 1), float(TRASH0 + 2 + half), np.float32)
    msk = np.zeros((2, ncores, 2), np.float32)
    for row in range(2):
        myb = rows[row]
        for c2 in range(ncores):
            r2rows = core_rows(c2, ncores)
            for r2 in range(2):
                if c2 != core and r2rows[r2] < myb:
                    msk[row, c2, r2] = 1.0
    d["fmsk_d"] = np.ascontiguousarray(
        np.broadcast_to(msk.reshape(1, -1), (8, 2 * ncores * 2)))
    stab = np.zeros(256, np.int32)
    conv = [0, 1, 2, 3] if half == 0 else [4, 5, 6, 3]
    for q, e in enumerate(conv):
        for dc in range(8):
            stab[q * 8 + dc] = e * 4194304 + dc * 524288
        for icg in range(8):
            stab[64 + q * 8 + icg] = e * 4194304 + icg * 524288
    SUB_E0 = [0, 1, 2, 0, 1, 2, 3]
    SUB_HP = [0, 0, 0, 1, 1, 1, half]
    for j in range(7):
        e = SUB_E0[j] + (4 * half if j < 6 else 0)
        hp = SUB_HP[j]
        stab[192 + j] = (e * 2 * HALF_PAD + hp * HALF_PAD) * 1024
    stab[199] = (TRASH0 + 2 + half) * 1024
    for ee in range(7):
        stab[200 + ee] = (ee * 2 * HALF_PAD + half * HALF_PAD) * 1024
    stab[207] = (TRASH0 + half) * 1024
    d["stab_d"] = stab.reshape(1, 256)
    return d




from concourse.bass_utils import run_bass_kernel_spmd

NCORES = 8
CAP = 4096


def _build():
    nc = bass.Bass("TRN2", target_bir_lowering=False)
    b = B(nc)
    ctx = Ctx(nc, b)
    dr = ctx.dram
    x = nc.dram_tensor("x", [2, 2048, 1024], F32, kind="ExternalInput").ap()
    y = nc.dram_tensor("y", [2, 2048, 1024], F32, kind="ExternalOutput").ap()
    dr["wqkvT"] = nc.dram_tensor("wqkvT", [3, 4, 1024, 128], F32, kind="ExternalInput").ap()
    for name, shape in [("woutTd", [512, 1024]), ("boutd", [128, 8]), ("masksd", [128, 4, 128])]:
        dr[name] = nc.dram_tensor(name, shape, F32, kind="ExternalInput").ap()
    for name, shape in [("bq_fm", [128, 8]), ("bk_fm", [128, 8]), ("bnw_fm", [128, 8]),
                        ("bld_d", [16, 1]), ("bpm_d", [16, 8]), ("bhsel_d", [128, 2]),
                        ("bE1_d", [16, 128]), ("bE2_d", [2, 128]),
                        ("bowT_d", [1024, 1024]), ("onesf_d", [1, 128])]:
        dr[name] = nc.dram_tensor(name, shape, F32, kind="ExternalInput").ap()
    dr["ones_d"] = nc.dram_tensor("ones_d", [128, 1], BF16, kind="ExternalInput").ap()
    dr["expw_i"] = nc.dram_tensor("expw_i", [7, 1024, 4096], F32, kind="ExternalInput").ap()
    dr["expw_o"] = nc.dram_tensor("expw_o", [7, 4096, 1024], F32, kind="ExternalInput").ap()
    dr["rwT_d"] = nc.dram_tensor("rwT_d", [1024, 8], F32, kind="ExternalInput").ap()
    for name, shape, dt in [("rb_bc", [128, 8], F32), ("iota_d", [128, 8], F32),
                            ("c99_d", [128, 8], F32), ("hoff_d", [128, 1], F32),
                            ("trash_d", [128, 1], F32), ("zrowc_d", [128, 1], F32),
                            ("fmsk_d", [8, 2 * NCORES * 2], F32), ("stab_d", [1, 256], I32)]:
        dr[name] = nc.dram_tensor(name, shape, dt, kind="ExternalInput").ap()
    dr["mslab"] = nc.dram_tensor("mslab", [MROWS, D], BF16, addr_space="Shared").ap()
    dr["yslab"] = nc.dram_tensor("yslab", [MROWS, D], BF16, addr_space="Shared").ap()
    dr["wscr_i"] = nc.dram_tensor("wscr_i", [4, 8, 128, 4096], BF16).ap()
    dr["wscr_o"] = nc.dram_tensor("wscr_o", [4, 32, 128, 1024], BF16).ap()
    dr["respill"] = nc.dram_tensor("respill", [NT, 128, D], F32).ap()
    dr["agin"] = nc.dram_tensor("agin", [8, 2], F32).ap()
    dr["agout"] = nc.dram_tensor("agout", [NCORES * 8, 2], F32).ap()
    dr["barin"] = nc.dram_tensor("barin", [1, 8], F32).ap()
    dr["barout"] = nc.dram_tensor("barout", [1, 8], F32).ap()

    alloc_common(ctx)
    alloc_ffn(ctx, NCORES)
    alloc_ffn_expert(ctx)
    alloc_convert(ctx)
    t = ctx.t
    with ctx.stack:
        for r in range(2):
            b.emit("sync", lambda e, r=r: e.dma_start(
                out=t["resid"][:, 16 * r:16 * (r + 1), :],
                in_=x[r].rearrange("(T p) f -> p T f", p=128)),
                reads=["x"], writes=["resid"], dma=True)
        emit_ffn_consts(ctx)
        ctx.stage_begin()
        alloc_attn_a(ctx)
        b.emit("gpsimd", lambda e: e.dma_start(
            out=t["woutT"][:, :, :], in_=dr["woutTd"].rearrange("(h p) o -> p h o", p=128)),
            writes=["woutT"], dma=True)
        b.emit("gpsimd", lambda e: e.dma_start(out=t["bout"][:, :], in_=dr["boutd"][:, :]),
               writes=["bout"], dma=True)
        b.emit("gpsimd", lambda e: e.dma_start(out=t["amasks"][:, :, :], in_=dr["masksd"][:, :, :]),
               writes=["amasks"], dma=True)
        emit_attn_a(ctx)
        ctx.stage_end()
        ctx.stage_begin()
        alloc_ffn_stage(ctx)
        emit_ffn(ctx, 0, NCORES, CAP)
        ctx.stage_end()
        ctx.stage_begin()
        alloc_attn_b(ctx)
        emit_attn_b_tables(ctx)
        emit_attn_b(ctx)
        ctx.stage_end()
        ctx.stage_begin()
        alloc_ffn_stage(ctx)
        ytiles = [y[ti // 16].rearrange("(T p) f -> p T f", p=128)[:, ti % 16, :]
                  for ti in range(NT)]
        emit_ffn(ctx, 1, NCORES, CAP, out_dram=ytiles)
        ctx.stage_end()
        b.emit("gpsimd", None, reads=["outd"])
        b.finalize()
    return nc


def _host_tables():
    pm = np.zeros((16, 8), np.float32)
    for h in range(16):
        pm[h, h // 2] = 1
    hsel = np.zeros((128, 2), np.float32)
    for p in range(128):
        hsel[p, p // 64] = 1
    E1 = np.zeros((16, 128), np.float32)
    for h in range(16):
        for p in range(128):
            if h % 2 == p // 64:
                E1[h, p] = 1
    return pm, hsel, E1


def kernel(**inputs):
    f32 = lambda a: np.ascontiguousarray(np.asarray(a, dtype=np.float32))
    x = f32(inputs["L0_attn_a_x"])
    Wk = f32(inputs["L0_attn_a_Wk"])
    Wq = f32(inputs["L0_attn_a_Wq"])
    Wv = f32(inputs["L0_attn_a_Wv"])
    W_out = f32(inputs["L0_attn_a_W_out"])
    b_out = f32(inputs["L0_attn_a_b_out"])
    bq = f32(inputs["L0_attn_b_q"])
    bk = f32(inputs["L0_attn_b_k"])
    log_decay = f32(inputs["L0_attn_b_log_decay"])
    bnw = f32(inputs["L0_attn_b_norm_weight"])
    bow = f32(inputs["L0_attn_b_o_w"])
    router_w = f32(inputs["L0_ffn_router_w"])
    router_b = f32(inputs["L0_ffn_router_b"])
    wi = f32(inputs["L0_ffn_experts_inter"])
    wo = f32(inputs["L0_ffn_experts_out"])

    wqkvT = np.ascontiguousarray(np.stack([
        Wq.transpose(0, 2, 1), Wk.transpose(0, 2, 1), Wv.transpose(0, 2, 1)]))
    pm, hsel, E1 = _host_tables()
    common = {
        "wqkvT": wqkvT,
        "woutTd": np.ascontiguousarray(W_out.T),
        "boutd": np.ascontiguousarray(b_out.reshape(8, 128).T),
        "masksd": make_attn_a_masks(),
        "bq_fm": np.ascontiguousarray(bq.reshape(8, 128).T),
        "bk_fm": np.ascontiguousarray(bk.reshape(8, 128).T),
        "bnw_fm": np.ascontiguousarray(bnw.reshape(8, 128).T),
        "bld_d": np.ascontiguousarray(log_decay.reshape(16, 1)),
        "bpm_d": pm, "bhsel_d": hsel, "bE1_d": E1,
        "bE2_d": np.ascontiguousarray(hsel.T),
        "bowT_d": np.ascontiguousarray(bow.T),
        "onesf_d": np.ones((1, 128), np.float32),
        "ones_d": np.ones((128, 1), ml_dtypes.bfloat16),
        "expw_i": wi, "expw_o": wo,
    }
    in_maps = []
    for c in range(NCORES):
        rows = core_rows(c, NCORES)
        d = {"x": np.ascontiguousarray(x[rows])}
        d.update(common)
        d.update(ffn_core_inputs(c, NCORES, router_w, router_b))
        in_maps.append(d)

    nc = _build()
    res = run_bass_kernel_spmd(nc, in_maps, core_ids=list(range(NCORES)))
    out = np.empty((16, 2048, 1024), np.float32)
    for c in range(NCORES):
        out[core_rows(c, NCORES)] = np.asarray(res.results[c]["y"], dtype=np.float32)
    return out


# version 05
# speedup vs baseline: 1.0245x; 1.0245x over previous
import numpy as np
import ml_dtypes
import contextlib

"""Recording/replay emit framework with automatic semaphore dependency edges.

Buffers tracked by tensor name (or explicit string key). Every RAW/WAR/WAW
gets a semaphore edge (race-detector-sound, incl. same-engine). DMA ops use
per-queue round-robin semaphore lanes with at most one outstanding DMA per
lane (issuer pre-waits its lane), so lane_sem >= 16*k soundly means the k-th
use of that lane completed.
"""
import concourse.bass as bass
import concourse.mybir as mybir

ENGINES = ("sync", "scalar", "vector", "tensor", "gpsimd")
NLANES = 8


def bufkey(x):
    if isinstance(x, str):
        return x
    t = getattr(x, "tensor", x)
    return t.name.split("__s")[0]


class Op:
    __slots__ = ("i", "eng", "fn", "reads", "writes", "dma", "group",
                 "deps", "seq", "need_inc", "sem_val_after", "lane", "tag", "bar",
                 "noinstr", "local")

    def __init__(self, i, eng, fn, reads, writes, dma, group, tag):
        self.i = i
        self.eng = eng
        self.fn = fn
        self.reads = reads
        self.writes = writes
        self.dma = dma
        self.group = group
        self.deps = set()
        self.seq = None
        self.need_inc = False
        self.sem_val_after = None
        self.lane = None
        self.tag = tag
        self.bar = False
        self.noinstr = False
        self.local = False


class B:
    def __init__(self, nc):
        self.nc = nc
        self.ops = []
        self.last_write = {}
        self.readers = {}
        self.open_groups = {}
        self.last_bar = None

    def emit(self, eng, fn=None, reads=(), writes=(), dma=False, group=None,
             group_end=False, tag=None, noinstr=False, local=False):
        assert eng in ENGINES
        op = Op(len(self.ops), eng, fn, [bufkey(r) for r in reads],
                [bufkey(w) for w in writes], dma, group, tag)
        op.noinstr = noinstr
        op.local = local
        deps = set()
        for k in op.reads:
            lw = self.last_write.get(k)
            if lw is not None:
                deps.add(lw)
        for k in op.writes:
            lw = self.last_write.get(k)
            if lw is not None:
                deps.add(lw)
            for r in self.readers.get(k, ()):
                deps.add(r)
        if self.last_bar is not None:
            for k in op.reads + op.writes:
                if k not in self.last_write and k not in self.readers:
                    deps.add(self.last_bar)
                    break
        if group is not None:
            grp = self.open_groups.setdefault(group, [])
            deps -= set(grp)
            grp.append(op.i)
        op.deps = deps
        self.ops.append(op)
        for k in op.reads:
            self.readers.setdefault(k, []).append(op.i)
        if group is None or group_end:
            if group is not None:
                self.open_groups.pop(group)
            for k in op.writes:
                self.last_write[k] = op.i
                self.readers[k] = []
        return op

    def inherit(self, new_key, old_keys):
        lw = sorted({self.last_write[k] for k in old_keys if k in self.last_write})
        rd = [i for k in old_keys for i in self.readers.get(k, ())]
        if lw:
            self.last_write[new_key] = lw[-1]
        extra = sorted(set(rd) | set(lw[:-1]))
        if extra:
            self.readers[new_key] = extra
        return new_key

    def barrier(self, eng="vector", tag="bar"):
        op = self.emit(eng, None, tag=tag)
        deps = set(self.last_write.values())
        for rl in self.readers.values():
            deps.update(rl)
        deps.discard(op.i)
        deps = {d for d in deps if not self.ops[d].local}
        op.deps = deps
        op.bar = True
        for k in list(self.last_write):
            self.last_write[k] = op.i
            self.readers[k] = []
        self.last_bar = op.i
        return op

    def finalize(self):
        nc = self.nc
        ops = self.ops
        seqs = {e: [] for e in ENGINES}
        for op in ops:
            op.seq = len(seqs[op.eng])
            seqs[op.eng].append(op)
        for op in ops:
            for d in op.deps:
                dop = ops[d]
                if dop.local and dop.eng == op.eng:
                    continue
                dop.need_inc = True
        comp_count = {e: 0 for e in ENGINES}
        lane_count = {}
        lane_rr = {e: 0 for e in ENGINES}
        pre_waits = {}
        for op in ops:
            if op.dma:
                lane = lane_rr[op.eng]
                lane_rr[op.eng] = (lane + 1) % NLANES
                op.lane = lane
                key = (op.eng, lane)
                prior = lane_count.get(key, 0)
                if prior > 0:
                    pre_waits[op.i] = (("l", op.eng, lane), 16 * prior)
                lane_count[key] = prior + 1
                op.sem_val_after = 16 * (prior + 1)
                op.need_inc = True
            elif op.need_inc:
                comp_count[op.eng] += 1
                op.sem_val_after = comp_count[op.eng]
        import contextlib
        self._semstack = contextlib.ExitStack()
        csem = {e: self._semstack.enter_context(nc.semaphore(f"c_{e}"))
                for e in ENGINES if comp_count[e] > 0}
        lsem = {k: self._semstack.enter_context(nc.semaphore(f"l_{k[0]}_{k[1]}"))
                for k in lane_count}
        known = {e: {} for e in ENGINES}
        waits = []
        for op in ops:
            need = {}
            for d in sorted(op.deps):
                dop = ops[d]
                if dop.sem_val_after is None:
                    continue
                key = (("l", dop.eng, dop.lane) if dop.dma else ("c", dop.eng))
                v = dop.sem_val_after
                if need.get(key, 0) < v:
                    need[key] = v
            if op.i in pre_waits:
                key, v = pre_waits[op.i]
                if need.get(key, 0) < v:
                    need[key] = v
            k = known[op.eng]
            todo = []
            for key, v in sorted(need.items(), key=lambda kv: str(kv[0])):
                if k.get(key, 0) < v:
                    todo.append((key, v))
                    k[key] = v
            waits.append(todo)
        self.waits = waits
        self._csem, self._lsem = csem, lsem

        def sem_of(key):
            if key[0] == "c":
                return csem[key[1]]
            return lsem[(key[1], key[2])]

        block_cm = nc.Block()
        block = block_cm.__enter__()

        def run_engine(engine_name):
            def runner(eobj):
                try:
                    eobj.get_value_cache().nocache = True
                except Exception:
                    pass
                for op in seqs[engine_name]:
                    todo = waits[op.i]
                    for key, v in todo:
                        eobj.wait_ge(sem_of(key), v)
                    try:
                        instr = op.fn(eobj) if op.fn is not None else eobj.nop()
                    except Exception:
                        import traceback as _tb
                        _tb.print_exc()
                        raise
                    if op.need_inc and op.sem_val_after is not None:
                        sem = (lsem[(op.eng, op.lane)] if op.dma else csem[op.eng])
                        amt = 16 if op.dma else 1
                        if instr is not None:
                            instr.then_inc(sem, amt)
                        else:
                            eobj.sem_inc(sem, amt)
                    if op.dma:
                        eobj.get_value_cache().barrier()
            return runner

        block.sync(run_engine("sync"))
        block.scalar(run_engine("scalar"))
        block.vector(run_engine("vector"))
        block.tensor(run_engine("tensor"))
        block.gpsimd(run_engine("gpsimd"))
        block_cm.__exit__(None, None, None)
        self._semstack.close()


"""Megakernel builder for mk_hybrid_d1 (see DESIGN.md).

Per-core program; 8 cores SPMD. Core c owns batch rows core_rows(c)
(stride-4 pair layout). All emitters share a Ctx.
"""
import numpy as np
import contextlib
import concourse.bass as bass
import concourse.mybir as mybir

F32 = mybir.dt.float32
BF16 = mybir.dt.bfloat16
I32 = mybir.dt.int32
AF = mybir.ActivationFunctionType
ALU = mybir.AluOpType
AX = mybir.AxisListType

D = 1024
NCHUNK = 8
TCORE = 4096
NT = 32
TROW = 2048
NTR = 16
DILS = [1, 2, 4, 8]
NORM = 1.0 / np.sqrt(128.0)
HOFF = [0, 2048, 3072, 3584]




class Ctx:
    def __init__(self, nc, b):
        self.nc = nc
        self.b = b
        self.stack = contextlib.ExitStack()
        self.t = {}
        self.dram = {}
        self.stage_stack = None
        self.stage_id = 0

    def _name(self, name):
        return name if self.stage_stack is None else f"{name}__s{self.stage_id}"

    def _stk(self):
        return self.stack if self.stage_stack is None else self.stage_stack

    def sbuf(self, name, shape, dtype):
        h = self._stk().enter_context(self.nc.sbuf_tensor(self._name(name), shape, dtype))
        self.t[name] = h
        return h

    def psum(self, name, shape, dtype=F32):
        h = self._stk().enter_context(self.nc.psum_tensor(self._name(name), shape, dtype))
        self.t[name] = h
        return h

    def stage_begin(self):
        assert self.stage_stack is None
        self.stage_id += 1
        self.stage_stack = contextlib.ExitStack()

    def stage_end(self):
        self.stage_stack.close()
        self.stage_stack = None
        self.b.barrier()


def emit_ln_tile(ctx, resid_tile, resid_key, dst_tm, dst_key, mean_ap, rstd_ap,
                 mk="lmean", rk="lrstd", bneg_ap=None, bneg_key=None, sfx="",
                 sq_ap=None, sq_key=None):
    b = ctx.b
    t = ctx.t
    ssq, s, m2, bb = t["lssq" + sfx], t["ls" + sfx], t["lm2" + sfx], t["lbb" + sfx]
    if sq_ap is None:
        sq, sqk = t["lsq"][:], "lsq"
    else:
        sq, sqk = sq_ap, sq_key

    b.emit("scalar", lambda e: e.activation(out=sq, in_=resid_tile, func=AF.Square, accum_out=ssq[:]),
           reads=[resid_key], writes=[sqk, ssq])
    b.emit("vector", lambda e: e.reduce_sum(s[:], resid_tile, axis=AX.X),
           reads=[resid_key], writes=[s])
    b.emit("vector", lambda e: e.tensor_scalar_mul(mean_ap, s[:], 1.0 / D),
           reads=[s], writes=[mk])
    b.emit("scalar", lambda e: e.activation(out=m2[:], in_=mean_ap, func=AF.Square),
           reads=[mk], writes=[m2])
    b.emit("scalar", lambda e: e.activation(out=bb[:], in_=m2[:], func=AF.Copy, scale=-1.0, bias=1e-5),
           reads=[m2], writes=[bb])
    b.emit("scalar", lambda e: e.activation(out=rstd_ap, in_=ssq[:], func=AF.Sqrt, scale=1.0 / D, bias=bb[:]),
           reads=[ssq, bb], writes=[rk])
    b.emit("vector", lambda e: e.reciprocal(rstd_ap, rstd_ap), reads=[rk], writes=[rk])
    bbk = bneg_key if bneg_key is not None else "lbb"
    bba = bb[:, :] if bneg_ap is None else bneg_ap
    b.emit("vector", lambda e: e.tensor_mul(bba, mean_ap, rstd_ap), reads=[mk, rk], writes=[bbk])
    b.emit("vector", lambda e: e.tensor_scalar_mul(bba, bba, -1.0), reads=[bbk], writes=[bbk])
    b.emit("scalar", lambda e: e.activation(out=dst_tm, in_=resid_tile, func=AF.Identity, scale=rstd_ap, bias=bba),
           reads=[resid_key, rk, bbk], writes=[dst_key])


def emit_tm2fm(ctx, src_tm, src_key, dst_fm_fn, dst_key, eng="sync"):
    b = ctx.b
    for c in range(NCHUNK):
        b.emit(eng, lambda e, c=c: e.dma_start(
            out=dst_fm_fn(c), in_=src_tm[:, c * 128:(c + 1) * 128], transpose=True),
            reads=[src_key], writes=[dst_key], dma=True)


def dil_ap(xfm_chunk, dil, s0, cnt_s):
    base = xfm_chunk.rearrange("p (seg s) -> p seg s", s=128)
    return base[:, ::dil, s0:s0 + cnt_s].rearrange("p n s -> p s n")


def emit_attn_a(ctx):
    nc, b, t = ctx.nc, ctx.b, ctx.t
    resid = t["resid"]
    xfm = t["xfm"]
    xlt = t["xlt"]
    q, k, v = t["aQ"], t["aK"], t["aV"]
    vst = t["aVs"]
    att = t["aAtt"]
    sraw = t["aSraw"]
    sm = t["aSm"]
    rowmax, rowsum = t["aRmax"], t["aRsum"]
    ysb = t["aY"]
    dfm = t["aDfm"]
    dtm = t["aDtm"]
    masks = t["amasks"]
    bout = t["bout"]
    woutT = t["woutT"]
    wq = t["awq"]; wk = t["awk"]; wv = t["awv"]
    wqkvT = ctx.dram["wqkvT"]
    ps_qkv = t["ps_qkv"]; ps_s = t["ps_s"]; ps_av = t["ps_av"]; ps_y = t["ps_y"]

    def load_head_w(h, buf):
        for wi, wsb in enumerate((wq, wk, wv)):
            src = wqkvT[wi, h].rearrange("(c p) k -> p c k", p=128)
            b.emit("gpsimd", lambda e, wsb=wsb, src=src, buf=buf: e.dma_start(
                out=wsb[:, buf], in_=src), reads=["wqkvT"], writes=[wsb], dma=True)

    load_head_w(0, 0)
    for r in range(2):
        for i in range(NTR):
            ti = 16 * r + i
            emit_ln_tile(ctx, resid[:, ti, :], "resid", xlt[:, :], "xlt",
                         t["lmean"][:, 0:1], t["lrstd"][:, 0:1])
            emit_tm2fm(ctx, xlt, "xlt", lambda c, i=i: xfm[:, c, i * 128:(i + 1) * 128], "xfm",
                       eng="sync")
        for h, dil in enumerate(DILS):
            L = 16 // dil
            ncols = 128 * L
            buf = 0
            if r == 0 and h == 0:
                pass
            if h > 0 or r > 0:
                load_head_w(h, 0)
            for j0 in range(0, ncols, 512):
              w = min(512, ncols - j0)
              for (wsb, dst, tag) in ((wq, q, "q"), (wk, k, "k")):
                    cs = w // L
                    s0 = j0 // L
                    for cc in range(NCHUNK):
                        ap = dil_ap(xfm[:, cc, :], dil, s0, cs)
                        b.emit("tensor", lambda e, wsb=wsb, buf=buf, cc=cc, ap=ap, w=w: e.matmul(
                            ps_qkv[:, 0:w], wsb[:, buf, cc, :], ap,
                            start=(cc == 0), stop=(cc == NCHUNK - 1)),
                            reads=[wsb, "xfm"], writes=[ps_qkv], group=f"qk{r}{h}{tag}{j0}",
                            group_end=(cc == NCHUNK - 1))
                    b.emit("vector", lambda e, dst=dst, w=w: e.tensor_copy(
                        dst[:, 0:w], ps_qkv[:, 0:w]),
                        reads=[ps_qkv], writes=[dst])
              for gl in range(w // 128):
                g = j0 // 128 + gl
                s0 = (g * 128) // L
                cs = 128 // L
                for cc in range(NCHUNK):
                    ap = dil_ap(xfm[:, cc, :], dil, s0, cs)
                    b.emit("scalar", lambda e, ap=ap: e.activation(
                        out=vst[:, :], in_=ap, func=AF.Copy),
                        reads=["xfm"], writes=["aVs"])
                    b.emit("tensor", lambda e, cc=cc, buf=buf: e.matmul(
                        ps_qkv[:, 0:128], vst[:, :], wv[:, buf, cc, :],
                        start=(cc == 0), stop=(cc == NCHUNK - 1)),
                        reads=[wv, "aVs"], writes=[ps_qkv], group=f"v{r}{h}{g}",
                        group_end=(cc == NCHUNK - 1))
                b.emit("vector", lambda e, gl=gl: e.tensor_copy(
                    v[:, gl * 128:(gl + 1) * 128], ps_qkv[:, 0:128]),
                    reads=[ps_qkv], writes=[v])
              for gl in range(w // 128):
                g = j0 // 128 + gl
                o = gl * 128
                b.emit("tensor", lambda e, o=o: e.matmul(
                    ps_s[:, :], k[:, o:o + 128], q[:, o:o + 128], start=True, stop=True),
                    reads=[k, q], writes=[ps_s], group=f"s{r}{h}{g}", group_end=True)
                b.emit("vector", lambda e, h=h: e.scalar_tensor_tensor(
                    sraw[:], ps_s[:, :], NORM, masks[:, h, :], ALU.mult, ALU.add),
                    reads=[ps_s, masks], writes=[sraw])
                b.emit("vector", lambda e: e.reduce_max(rowmax[:], sraw[:], axis=AX.X, negate=True),
                       reads=[sraw], writes=[rowmax])
                b.emit("scalar", lambda e: e.activation(out=sraw[:], in_=sraw[:], func=AF.Exp, bias=rowmax[:]),
                       reads=[sraw, rowmax], writes=[sraw])
                b.emit("vector", lambda e: e.reduce_sum(rowsum[:], sraw[:], axis=AX.X),
                       reads=[sraw], writes=[rowsum])
                b.emit("vector", lambda e: e.reciprocal(rowsum[:], rowsum[:]),
                       reads=[rowsum], writes=[rowsum])
                b.emit("vector", lambda e: e.tensor_scalar(sm[:], sraw[:], rowsum[:], None, ALU.mult),
                       reads=[sraw, rowsum], writes=[sm])
                b.emit("tensor", lambda e, o=o: e.matmul(
                    ps_av[:, :], v[:, o:o + 128], sm[:, :], start=True, stop=True),
                    reads=[v, sm], writes=[ps_av], group=f"av{r}{h}{g}", group_end=True)
                b.emit("vector", lambda e, h=h, g=g: e.tensor_copy(
                    att[:, HOFF[h] + g * 128:HOFF[h] + (g + 1) * 128], ps_av[:, :]),
                    reads=[ps_av], writes=[att])
        PS_OFF = [0, 512, 768, 896]
        SS = [(True, True), (True, False), (False, False), (False, True)]
        for tau in range(4):
          for half in range(2):
            for oc4 in range(4):
                oc = 4 * half + oc4
                for h, dil in enumerate(DILS):
                    wsl = 512 // dil
                    a0 = HOFF[h] + tau * wsl
                    p0 = PS_OFF[h]
                    st, sp = SS[h]
                    b.emit("tensor", lambda e, h=h, oc=oc, a0=a0, p0=p0, w=wsl, st=st, sp=sp: e.matmul(
                        ps_y[:, p0:p0 + w], woutT[:, h, oc * 128:(oc + 1) * 128], att[:, a0:a0 + w],
                        start=st, stop=sp),
                        reads=[woutT, att], writes=[ps_y], group=f"y{r}{tau}{oc}",
                        group_end=(h == 3))
                b.emit("vector", lambda e: e.tensor_copy(ysb[:], ps_y[:, 0:512]),
                       reads=[ps_y], writes=[ysb])
                for h, dil in enumerate(DILS[1:], start=1):
                    p0 = PS_OFF[h]
                    w = 512 // dil
                    b.emit("vector", lambda e, p0=p0, w=w, dil=dil: e.tensor_add(
                        ysb[:, ::dil], ysb[:, ::dil], ps_y[:, p0:p0 + w]),
                        reads=[ps_y, ysb], writes=[ysb])
                b.emit("scalar", lambda e, oc=oc, oc4=oc4: e.activation(
                    out=dfm[:, oc4, :], in_=ysb[:], func=AF.Identity, scale=1.0,
                    bias=bout[:, oc:oc + 1]),
                    reads=[ysb, bout], writes=[dfm])
            for tt in range(4):
                ti = 16 * r + tau * 4 + tt
                for cc4 in range(4):
                    b.emit("sync", lambda e, cc4=cc4, tt=tt: e.dma_start(
                        out=dtm[:, cc4 * 128:(cc4 + 1) * 128],
                        in_=dfm[:, cc4, tt * 128:(tt + 1) * 128], transpose=True),
                        reads=[dfm], writes=[dtm], dma=True)
                b.emit("vector", lambda e, ti=ti, half=half: e.tensor_add(
                    resid[:, ti, 512 * half:512 * (half + 1)],
                    resid[:, ti, 512 * half:512 * (half + 1)], dtm[:, :]),
                    reads=[dtm, "resid"], writes=["resid"])


def alloc_attn_a(ctx):
    ctx.sbuf("woutT", [128, 4, D], BF16)
    ctx.sbuf("bout", [128, NCHUNK], F32)
    ctx.sbuf("amasks", [128, 4, 128], BF16)
    ctx.sbuf("aQ", [128, 512], BF16)
    ctx.sbuf("aVs", [128, 128], BF16)
    ctx.sbuf("aK", [128, 512], BF16)
    ctx.sbuf("aV", [128, 512], BF16)
    ctx.sbuf("aAtt", [128, 3840], BF16)
    ctx.sbuf("aSraw", [128, 128], F32)
    ctx.sbuf("aSm", [128, 128], BF16)
    ctx.sbuf("aRmax", [128, 1], F32)
    ctx.sbuf("aRsum", [128, 1], F32)
    ctx.sbuf("aY", [128, 512], F32)
    ctx.sbuf("aDfm", [128, 4, 512], BF16)
    ctx.sbuf("aDtm", [128, 512], BF16)
    ctx.sbuf("awq", [128, 1, NCHUNK, 128], BF16)
    ctx.sbuf("awk", [128, 1, NCHUNK, 128], BF16)
    ctx.sbuf("awv", [128, 1, NCHUNK, 128], BF16)
    ctx.psum("ps_qkv", [128, 512])
    ctx.psum("ps_s", [128, 128])
    ctx.psum("ps_av", [128, 128])
    ctx.psum("ps_y", [128, 1024])


def alloc_common(ctx):
    ctx.sbuf("resid", [128, NT, D], F32)
    ctx.sbuf("xfm", [128, NCHUNK, TROW], BF16)
    ctx.sbuf("xlt", [128, D], BF16)
    ctx.sbuf("lsq", [128, D], BF16)
    ctx.sbuf("lssq", [128, 1], F32)
    ctx.sbuf("ls", [128, 1], F32)
    ctx.sbuf("lm2", [128, 1], F32)
    ctx.sbuf("lbb", [128, 1], F32)
    ctx.sbuf("lmean", [128, NT], F32)
    ctx.sbuf("lrstd", [128, NT], F32)


def make_attn_a_masks():
    m = np.full((128, 4, 128), -1e30, np.float32)
    for h, dil in enumerate(DILS):
        L = 16 // dil
        for p in range(128):
            s1, n = divmod(p, L)
            for f in range(128):
                s2, mm = divmod(f, L)
                if s1 == s2 and n >= mm:
                    m[p, h, f] = 0.0
    return m


def alloc_attn_b(ctx):
    ctx.sbuf("bqf", [128, 8], F32)
    ctx.sbuf("bkf", [128, 8], F32)
    ctx.sbuf("bqk", [128, 8], F32)
    ctx.sbuf("bld", [16, 1], F32)
    ctx.sbuf("blam", [16, 1], F32)
    ctx.sbuf("bpm", [16, 8], F32)
    ctx.sbuf("blamp", [16, 8], F32)
    ctx.sbuf("bhsel", [128, 2], F32)
    ctx.sbuf("bE1", [16, 128], F32)
    ctx.sbuf("bE2", [2, 128], F32)
    ctx.sbuf("bnw", [128, 8], F32)
    ctx.sbuf("blamd", [128, 8], F32)
    ctx.sbuf("bcd", [128, 8], F32)
    ctx.sbuf("bscale", [128, 8], F32)
    ctx.sbuf("bones", [128, 1], BF16)
    ctx.sbuf("bonesf", [1, 128], F32)
    ctx.sbuf("byscan", [128, 1024], F32)
    ctx.sbuf("bsq", [128, 1024], BF16)
    ctx.sbuf("browrms", [1, 512], F32)
    ctx.sbuf("beps", [1, 1], F32)
    ctx.sbuf("browssb", [128, 512], F32)
    ctx.sbuf("bowT", [128, NCHUNK, D], BF16)
    ctx.sbuf("bY", [128, 512], F32)
    ctx.sbuf("bDfm", [128, 4, 512], BF16)
    ctx.sbuf("bDtm", [128, 512], BF16)
    ctx.psum("ps_b1", [128, 512])
    ctx.psum("ps_b2", [128, 512])
    ctx.psum("ps_rms", [1, TROW])
    ctx.psum("ps_tab", [16, 8])


def emit_attn_b_tables(ctx):
    b, t = ctx.b, ctx.t
    dr = ctx.dram
    for name, dst in (("bq_fm", "bqf"), ("bk_fm", "bkf"), ("bnw_fm", "bnw")):
        b.emit("gpsimd", lambda e, name=name, dst=dst: e.dma_start(
            out=t[dst][:, :], in_=dr[name][:, :]), reads=[name], writes=[dst], dma=True)
    for name, dst in (("bld_d", "bld"), ("bpm_d", "bpm"), ("bhsel_d", "bhsel"),
                      ("bE1_d", "bE1"), ("bE2_d", "bE2")):
        b.emit("gpsimd", lambda e, name=name, dst=dst: e.dma_start(
            out=t[dst][:, :], in_=dr[name][:, :]), reads=[name], writes=[dst], dma=True)
    b.emit("vector", lambda e: e.tensor_mul(t["bqk"][:], t["bqf"][:], t["bkf"][:]),
           reads=["bqf", "bkf"], writes=["bqk"])
    b.emit("tensor", lambda e: e.matmul(t["ps_tab"][0:2, :], t["bhsel"][:, :], t["bqk"][:, :],
                                        start=True, stop=True),
           reads=["bhsel", "bqk"], writes=["ps_tab"], group="btab1", group_end=True)
    b.emit("vector", lambda e: e.tensor_copy(t["bqk"][0:2, :], t["ps_tab"][0:2, :]),
           reads=["ps_tab"], writes=["bqk"])
    b.emit("tensor", lambda e: e.matmul(t["ps_b1"][:, 0:8], t["bE2"][0:2, :], t["bqk"][0:2, :], start=True, stop=True),
           reads=["bE2", "bqk"], writes=["ps_b1"], group="btab2", group_end=True)
    b.emit("vector", lambda e: e.tensor_copy(t["bcd"][:, :], t["ps_b1"][:, 0:8]),
           reads=["ps_b1"], writes=["bcd"])
    b.emit("scalar", lambda e: e.activation(out=t["blam"][:, :], in_=t["bld"][:, :], func=AF.Sigmoid),
           reads=["bld"], writes=["blam"])
    b.emit("vector", lambda e: e.tensor_scalar(t["blamp"][:, :], t["bpm"][:, :], t["blam"][:, 0:1], None, ALU.mult),
           reads=["bpm", "blam"], writes=["blamp"])
    b.emit("tensor", lambda e: e.matmul(t["ps_b1"][:, 0:8], t["bE1"][:, :], t["blamp"][:, :], start=True, stop=True),
           reads=["bE1", "blamp"], writes=["ps_b1"], group="btab3", group_end=True)
    b.emit("vector", lambda e: e.tensor_copy(t["blamd"][:, :], t["ps_b1"][:, 0:8]),
           reads=["ps_b1"], writes=["blamd"])
    b.emit("vector", lambda e: e.tensor_mul(t["bscale"][:, :], t["bcd"][:, :], t["bnw"][:, :]),
           reads=["bcd", "bnw"], writes=["bscale"])


def emit_attn_b(ctx):
    nc, b, t = ctx.nc, ctx.b, ctx.t
    resid = t["resid"]
    xfm = t["xfm"]; xlt = t["xlt"]
    yscan = t["byscan"]; sq = t["bsq"]
    rowrms = t["browrms"]; rmsb = t["browssb"]
    owT = t["bowT"]; ysb = t["bY"]; dfm = t["bDfm"]; dtm = t["bDtm"]
    ps1 = t["ps_b1"]; ps2 = t["ps_b2"]; psr = t["ps_rms"]
    b.emit("gpsimd", lambda e: e.dma_start(
        out=owT[:, :, :], in_=ctx.dram["bowT_d"].rearrange("(c p) od -> p c od", p=128)),
        reads=["bowT_d"], writes=["bowT"], dma=True)
    b.emit("gpsimd", lambda e: e.dma_start(
        out=t["bones"][:, :], in_=ctx.dram["ones_d"][:, 0:1]), reads=["ones_d"], writes=["bones"], dma=True)
    b.emit("gpsimd", lambda e: e.dma_start(
        out=t["bonesf"][:, :], in_=ctx.dram["onesf_d"][0:1, :]), reads=["onesf_d"], writes=["bonesf"], dma=True)
    b.emit("vector", lambda e: e.memset(t["beps"][:, :], 1e-6), writes=["beps"])
    for r in range(2):
        for i in range(NTR):
            ti = 16 * r + i
            emit_ln_tile(ctx, resid[:, ti, :], "resid", xlt[:, :], "xlt",
                         t["lmean"][:, 0:1], t["lrstd"][:, 0:1])
            emit_tm2fm(ctx, xlt, "xlt", lambda c, i=i: xfm[:, c, i * 128:(i + 1) * 128], "xfm",
                       eng="sync")
        for cc in range(NCHUNK):
            lam_b = t["blamd"][:, cc:cc + 1].broadcast_to((128, 1024))
            for hh in range(2):
                init = 0.0 if hh == 0 else yscan[:, 1023:1024]
                b.emit("vector", lambda e, cc=cc, hh=hh, init=init, lam_b=lam_b: e.tensor_tensor_scan(
                    yscan[:, :], lam_b, xfm[:, cc, hh * 1024:(hh + 1) * 1024], init, ALU.mult, ALU.add),
                    reads=["blamd", "xfm", "byscan"], writes=["byscan"])
                b.emit("scalar", lambda e, cc=cc: e.activation(
                    out=sq[:, :], in_=yscan[:, :], func=AF.Square, scale=t["bcd"][:, cc:cc + 1]),
                    reads=["byscan", "bcd"], writes=["bsq"])
                b.emit("scalar", lambda e, cc=cc, hh=hh: e.activation(
                    out=xfm[:, cc, hh * 1024:(hh + 1) * 1024], in_=yscan[:, :], func=AF.Identity, scale=t["bscale"][:, cc:cc + 1]),
                    reads=["byscan", "bscale"], writes=["xfm"])
                for tl in range(2):
                    tt = hh * 2 + tl
                    b.emit("tensor", lambda e, tt=tt, tl=tl, cc=cc: e.matmul(
                        psr[0:1, tt * 512:(tt + 1) * 512], t["bones"][:, :], sq[:, tl * 512:(tl + 1) * 512],
                        start=(cc == 0), stop=(cc == NCHUNK - 1)),
                        reads=["bones", "bsq"], writes=["ps_rms"], group=f"brms{r}{tt}",
                        group_end=(cc == NCHUNK - 1))
        for tt in range(4):
            b.emit("vector", lambda e, tt=tt: e.tensor_copy(rowrms[:, :], psr[0:1, tt * 512:(tt + 1) * 512]),
                   reads=["ps_rms"], writes=["browrms"])
            b.emit("scalar", lambda e: e.activation(out=rowrms[:, :], in_=rowrms[:, :],
                                                    func=AF.Sqrt, scale=1.0 / D, bias=t["beps"][:, :]),
                   reads=["browrms", "beps"], writes=["browrms"])
            b.emit("vector", lambda e: e.reciprocal(rowrms[:, :], rowrms[:, :]),
                   reads=["browrms"], writes=["browrms"])
            b.emit("tensor", lambda e: e.matmul(
                ps2[:, :], t["bonesf"][0:1, :], rowrms[0:1, :],
                start=True, stop=True),
                reads=["bonesf", "browrms"], writes=["ps_b2"], group=f"brb{r}{tt}", group_end=True)
            b.emit("vector", lambda e: e.tensor_copy(rmsb[:, :], ps2[:, :]),
                   reads=["ps_b2"], writes=["browssb"])
            for half in range(2):
                for oc4 in range(4):
                    oc = 4 * half + oc4
                    for cc in range(NCHUNK):
                        b.emit("tensor", lambda e, oc=oc, cc=cc, tt=tt: e.matmul(
                            ps1[:, :], owT[:, cc, oc * 128:(oc + 1) * 128],
                            xfm[:, cc, tt * 512:(tt + 1) * 512],
                            start=(cc == 0), stop=(cc == NCHUNK - 1)),
                            reads=["bowT", "xfm"], writes=["ps_b1"], group=f"bow{r}{tt}{oc}",
                            group_end=(cc == NCHUNK - 1))
                    b.emit("vector", lambda e: e.tensor_mul(ysb[:, :], ps1[:, :], rmsb[:, :]),
                           reads=["ps_b1", "browssb"], writes=["bY"])
                    b.emit("scalar", lambda e, oc4=oc4: e.activation(
                        out=dfm[:, oc4, :], in_=ysb[:], func=AF.Copy, scale=1.0, bias=0.0),
                        reads=["bY"], writes=["bDfm"])
                for tloc in range(4):
                    ti = 16 * r + tt * 4 + tloc
                    for cc4 in range(4):
                        b.emit("sync", lambda e, cc4=cc4, tloc=tloc: e.dma_start(
                            out=dtm[:, cc4 * 128:(cc4 + 1) * 128],
                            in_=dfm[:, cc4, tloc * 128:(tloc + 1) * 128], transpose=True),
                            reads=["bDfm"], writes=["bDtm"], dma=True)
                    b.emit("vector", lambda e, ti=ti, half=half: e.tensor_add(
                        resid[:, ti, 512 * half:512 * (half + 1)],
                        resid[:, ti, 512 * half:512 * (half + 1)], dtm[:, :]),
                        reads=["bDtm", "resid"], writes=["resid"])





def emit_tr_32x128(ctx, dst, dst_key, src, src_key):
    b = ctx.b
    for i in range(4):
        b.emit("vector", lambda e, i=i: e.transpose(
            dst[32 * i:32 * (i + 1), 0:32], src[0:32, 32 * i:32 * (i + 1)]),
            reads=[src_key], writes=[dst_key])


def emit_tr_128x32(ctx, dst, dst_key, src, src_key):
    b = ctx.b
    for i in range(4):
        b.emit("vector", lambda e, i=i: e.transpose(
            dst[0:32, 32 * i:32 * (i + 1)], src[32 * i:32 * (i + 1), 0:32]),
            reads=[src_key], writes=[dst_key])



def emit_select(ctx, out_ap, out_key, mask_ap, mask_keys, true_ap, true_keys, false_ap, false_keys, n=8,
                scr=None, scrk=None):
    b, t = ctx.b, ctx.t
    if scr is None:
        scr, scrk = t["fsel"], "fsel"
    sa = scr[:, 0:n]
    b.emit("vector", lambda e: e.tensor_sub(sa, false_ap, true_ap),
           reads=list(false_keys) + list(true_keys), writes=[scrk])
    b.emit("vector", lambda e: e.tensor_mul(sa, sa, mask_ap),
           reads=list(mask_keys) + [scrk], writes=[scrk])
    b.emit("vector", lambda e: e.tensor_sub(out_ap, false_ap, sa),
           reads=list(false_keys) + [scrk], writes=[out_key])

from concourse.bass_core_rust import IndirectOffsetOnAxis

NEXP = 7
HALF_PAD = 1280
EXEC_ROWS = 768
MROWS = NEXP * 2 * HALF_PAD + 4


def alloc_ffn(ctx, ncores):
    ctx.sbuf("lbb2", [128, NT], F32)
    ctx.sbuf("frwT", [128, NCHUNK, 8], BF16)
    ctx.sbuf("frb", [128, 8], F32)
    ctx.sbuf("fiota", [128, 8], F32)
    ctx.sbuf("f99", [128, 8], F32)
    ctx.sbuf("fp8", [128, 8], F32)
    ctx.sbuf("fsel", [128, 8], F32)
    ctx.sbuf("fm1", [128, 1], F32)
    ctx.sbuf("fm2", [128, 1], F32)
    ctx.sbuf("foh", [128, 8], F32)
    ctx.sbuf("find", [128, 8], F32)
    ctx.sbuf("findb", [128, 8], BF16)
    ctx.sbuf("fe1", [128, NT], F32)
    ctx.sbuf("fe2", [128, NT], F32)
    ctx.sbuf("fw1", [128, NT], F32)
    ctx.sbuf("fw2", [128, NT], F32)
    ctx.sbuf("fwn", [128, NT], F32)
    ctx.sbuf("fjr1", [128, NT], F32)
    ctx.sbuf("fjr2", [128, NT], F32)
    ctx.sbuf("fgx1", [128, NT], F32)
    ctx.sbuf("fgx2", [128, NT], F32)
    ctx.sbuf("fzero1", [128, 1], F32)
    ctx.sbuf("frun", [8, 1], F32)
    ctx.sbuf("fscr8w", [8, max(16, 2 * ncores)], F32)
    ctx.sbuf("fones8", [8, 1], F32)
    ctx.sbuf("fonesf", [1, 128], F32)
    ctx.sbuf("fcnt", [8, 2], F32)
    ctx.sbuf("fagt", [8, ncores * 2], F32)
    ctx.sbuf("fmsk", [8, 2 * ncores * 2], F32)
    ctx.sbuf("foff", [8, 2], F32)
    ctx.sbuf("foffb", [128, 2, 8], F32)
    ctx.sbuf("fhoff", [128, 1], F32)
    ctx.sbuf("ftrash", [128, 1], F32)
    ctx.sbuf("fzrowc", [128, 1], F32)
    ctx.sbuf("fgidx", [128, 1], I32)
    ctx.sbuf("fgidx2", [128, 1], I32)
    ctx.sbuf("fscr", [128, 8], F32)
    ctx.sbuf("fscr1", [128, 1], F32)
    ctx.sbuf("fscr2", [128, 1], F32)
    ctx.sbuf("fkept", [128, 1], F32)
    ctx.sbuf("stab", [1, 256], I32)


def alloc_ffn_stage(ctx):
    ctx.sbuf("flgT", [32, 128], F32)
    ctx.sbuf("ftp", [128, 32], F32)
    ctx.sbuf("fz0", [128, D], BF16)
    for s in range(2):
        ctx.sbuf(f"fxlt{s}", [128, D], BF16)
        ctx.sbuf(f"fxfs{s}", [128, NCHUNK, 128], BF16)
        ctx.sbuf(f"flgs{s}", [32, 128], F32)
        ctx.sbuf(f"ftpa{s}", [128, 32], F32)
        ctx.sbuf(f"ftpb{s}", [128, 32], BF16)
        ctx.sbuf(f"ftpc{s}", [128, 32], F32)
        for nm in ("fp8s", "fohs", "finds", "fscrs", "fsels"):
            ctx.sbuf(f"{nm}{s}", [128, 8], F32)
        ctx.sbuf(f"findbs{s}", [128, 8], BF16)
        for nm in ("fm1s", "fm2s", "fsc1c", "fkeptc", "fscr2c"):
            ctx.sbuf(f"{nm}{s}", [128, 1], F32)
        ctx.sbuf(f"fgidx1{s}", [128, 1], I32)
        ctx.sbuf(f"fgidx2{s}", [128, 1], I32)
        for nm in ("lssq", "ls", "lm2", "lbb"):
            ctx.sbuf(f"{nm}_{s}", [128, 1], F32)
    ctx.sbuf("indfm", [32, 4096], BF16)
    ctx.sbuf("jrfm", [32, 4096], F32)
    ctx.psum("fps1", [128, 512])
    ctx.psum("fps2", [128, 512])
    ctx.psum("fps3", [128, 512])
    ctx.psum("fps4", [128, 512])


def alloc_ffn_expert(ctx):
    nc = ctx.nc
    resid_off = nc.lookup_mloc(ctx.t["resid"]).addr
    xfm_off = nc.lookup_mloc(ctx.t["xfm"]).addr
    offs = [0]
    def at(name, shape, dtype, base=None):
        nbytes = int(np.prod(shape[1:])) * (2 if dtype == BF16 else 4)
        if base is None:
            h = nc.alloc_sbuf_tensor_at(name, shape, dtype, offset=resid_off + offs[0])
            offs[0] += nbytes
        else:
            h = nc.alloc_sbuf_tensor_at(name, shape, dtype, offset=base)
        ctx.t[name] = h
        return h
    at("fhfm", [128, 32, EXEC_ROWS], BF16)
    at("fxg", [128, NCHUNK, EXEC_ROWS], BF16)
    at("fyfm", [128, NCHUNK, EXEC_ROWS], BF16)
    at("fxgt6", [128, 6, D], BF16)
    at("fytm6", [128, 6, D], BF16)
    at("fzero", [128, D], BF16)
    assert offs[0] <= NT * D * 4, f"expert alias overflow {offs[0]}"
    at("fwi", [128, NCHUNK, 1024], BF16, base=xfm_off)
    at("fwo", [128, 32, 256], BF16, base=xfm_off + NCHUNK * 1024 * 2)
    at("frt0", [128, D], F32, base=xfm_off)
    at("frt1", [128, D], F32, base=xfm_off + 4096)
    at("fg1b0", [128, D], BF16, base=xfm_off + 8192)
    at("fg1b1", [128, D], BF16, base=xfm_off + 10240)
    at("fg2b0", [128, D], BF16, base=xfm_off + 12288)
    at("fg2b1", [128, D], BF16, base=xfm_off + 14336)
    at("fdelta0", [128, D], F32, base=xfm_off + 16384)
    at("fdelta1", [128, D], F32, base=xfm_off + 20480)
    at("fxltd0", [128, D], BF16, base=xfm_off + 24576)
    at("fxltd1", [128, D], BF16, base=xfm_off + 26624)


EXPERT_KEYS = ["fhfm", "fxg", "fyfm", "fwi", "fwo", "fxgt6", "fytm6", "fzero"]
COMBINE_KEYS = ["frt0", "frt1", "fg1b0", "fg1b1", "fg2b0", "fg2b1",
                "fdelta0", "fdelta1", "fxltd0", "fxltd1"]

ST_CI = 0
ST_CO = 64
ST_SUB = 192
ST_ZROW = 199
ST_ZF = 200
ST_YTRASH = 207
QSEQ = [0, 1, 2, 0, 1, 2, 3]




def _dyn_reg(ctx, e, eng):
    import bass_rust
    regs = ctx.t.setdefault("_regs", {})
    key = "dyn_" + eng
    if key not in regs:
        h = e.register("dynreg").__enter__()
        regs[key] = (h, bass_rust.RuntimeValue(h, min_val=0, max_val=1 << 27))
    return regs[key]


def emit_dyn_dma(ctx, stab_idx, dram_base, dims, sbuf_ap, write_to_dram,
                 reads, writes, tag=None, eng="gpsimd"):
    b, t = ctx.b, ctx.t
    from concourse.ap import AP
    rk = "dynreg_" + eng
    chk = "dynch_" + eng
    def ld(e, stab_idx=stab_idx, eng=eng):
        h, rv = _dyn_reg(ctx, e, eng)
        e.reg_load(h, t["stab"][0:1, stab_idx:stab_idx + 1])
    b.emit(eng, ld, reads=["stab"], writes=[rk], noinstr=True, local=True)
    def go(e, eng=eng):
        h, rv = ctx.t["_regs"]["dyn_" + eng]
        dap = AP(dram_base.tensor, rv, [list(d) for d in dims])
        if write_to_dram:
            return e.dma_start(out=dap, in_=sbuf_ap)
        return e.dma_start(out=sbuf_ap, in_=dap)
    b.emit(eng, go, reads=list(reads) + [rk, chk],
           writes=list(writes) + [chk], dma=True, tag=tag)

def alloc_convert(ctx):
    nc = ctx.nc
    xfm_off = nc.lookup_mloc(ctx.t["xfm"]).addr
    ctx.t["cvt32"] = nc.alloc_sbuf_tensor_at("cvt32", [128, 4096], F32, offset=xfm_off)
    ctx.t["cvt16"] = nc.alloc_sbuf_tensor_at("cvt16", [128, 4096], BF16, offset=xfm_off + 4096 * 4)


def emit_ffn_convert_q(ctx, q):
    nc, b, t = ctx.nc, ctx.b, ctx.t
    dr = ctx.dram
    c32, c16 = t["cvt32"], t["cvt16"]
    b.inherit("cvt32", ["fwiF0__0", "fwiF0__1", "fwoF0__0", "fwoF0__1"])
    b.inherit("cvt16", ["fwiF0__0", "fwiF0__1", "fwoF0__0", "fwoF0__1"])
    for dc in range(NCHUNK):
        emit_dyn_dma(ctx, ST_CI + q * 8 + dc, dr["expw_i"],
                     [[4096, 128], [1, 4096]], c32[:, :], False,
                     reads=["expw_i"], writes=["cvt32"], eng="sync")
        b.emit("vector", lambda e: e.tensor_copy(c16[:, :], c32[:, :]),
               reads=["cvt32"], writes=["cvt16"])
        b.emit("gpsimd", lambda e, q=q, dc=dc: e.dma_start(
            out=dr["wscr_i"][q, dc, :, :], in_=c16[:, :]),
            reads=["cvt16"], writes=["wscr_i"], dma=True)
    for icg in range(8):
        emit_dyn_dma(ctx, ST_CO + q * 8 + icg, dr["expw_o"],
                     [[1024, 128], [131072, 4], [1, 1024]],
                     c32[:, :].rearrange("p (g f) -> p g f", g=4), False,
                     reads=["expw_o"], writes=["cvt32"], eng="scalar")
        b.emit("scalar", lambda e: e.activation(out=c16[:, :], in_=c32[:, :], func=AF.Copy),
               reads=["cvt32"], writes=["cvt16"])
        b.emit("gpsimd", lambda e, q=q, icg=icg: e.dma_start(
            out=dr["wscr_o"][q, icg * 4:(icg + 1) * 4].rearrange("g p f -> p g f"),
            in_=c16[:, :].rearrange("p (g f) -> p g f", g=4)),
            reads=["cvt16"], writes=["wscr_o"], dma=True)


def emit_ffn(ctx, call, ncores, cap, out_dram=None):
    nc, b, t = ctx.nc, ctx.b, ctx.t
    import bass_rust
    S = f"F{call}_"
    resid = t["resid"]
    dr = ctx.dram
    mslab, yslab = dr["mslab"], dr["yslab"]
    mkey, ykey = "mslab" + S, "yslab" + S
    b.inherit(mkey, ["mslab", "mslabF0_", "yslabF0_"])
    b.inherit(ykey, ["yslab", "mslabF0_", "yslabF0_"])
    xlt = t["xlt"]
    flgT, ftp, fz0 = t["flgT"], t["ftp"], t["fz0"]
    fps1, fps2, fps3, fps4 = t["fps1"], t["fps2"], t["fps3"], t["fps4"]
    b.emit("vector", lambda e: e.memset(flgT[:, :], 0.0), writes=["flgT"])
    b.emit("vector", lambda e: e.memset(ftp[:, :], 0.0), writes=["ftp"])
    b.emit("vector", lambda e: e.memset(t["fonesf"][:, :], 1.0), writes=["fonesf"])
    b.emit("vector", lambda e: e.memset(t["frun"][:, :], 0.0), writes=["frun"])
    b.emit("vector", lambda e: e.memset(t["fones8"][:, :], 1.0), writes=["fones8"])
    b.emit("vector", lambda e: e.memset(t["fzero1"][:, :], 0.0), writes=["fzero1"])
    if call == 0:
        b.emit("vector", lambda e: e.memset(fz0[:, :], 0.0), writes=["fz0"])
        for ee in range(NEXP):
            emit_dyn_dma(ctx, ST_ZF + ee, mslab,
                         [[1024, 128], [131072, 6], [1, 1024]],
                         fz0[:, None, :].broadcast_to((128, 6, D)), True,
                         reads=["fz0"], writes=[mkey])
            emit_dyn_dma(ctx, ST_ZF + ee, yslab,
                         [[1024, 128], [131072, 10], [1, 1024]],
                         fz0[:, None, :].broadcast_to((128, 10, D)), True,
                         reads=["fz0"], writes=[ykey])
        emit_dyn_dma(ctx, ST_YTRASH, yslab, [[1024, 1], [1, 1024]],
                     fz0[0:1, :], True, reads=["fz0"], writes=[ykey])
    for s in range(2):
        b.emit("vector", lambda e, h=t[f"ftpb{s}"]: e.memset(h[:, :], 0.0), writes=[f"ftpb{s}"])
        b.emit("vector", lambda e, h=t[f"flgs{s}"]: e.memset(h[:, :], 0.0), writes=[f"flgs{s}"])
    b.emit("vector", lambda e, h=t["jrfm"]: e.memset(h[:, :], 0.0), writes=["jrfm"])
    indfm, jrfm = t["indfm"], t["jrfm"]
    for ti in range(NT):
        s = ti % 2
        def sx(name, s=s):
            return t[f"{name}{s}"], f"{name}{s}"
        xls, xlk = sx("fxlt")
        fxs, fxk = sx("fxfs")
        lgs, lgk = sx("flgs")
        tpa, tpak = sx("ftpa")
        tpb, tpbk = sx("ftpb")
        p8, p8k = sx("fp8s")
        m1, m1k = sx("fm1s")
        m2s, m2k = sx("fm2s")
        ohs, ohk = sx("fohs")
        fin, fink = sx("finds")
        finb, finbk = sx("findbs")
        scr, scrk = sx("fscrs")
        sel, selk = sx("fsels")
        psl, pslk = (fps1, "fps1") if s == 0 else (fps2, "fps2")
        emit_ln_tile(ctx, resid[:, ti, :], "resid", xls[:, :], xlk,
                     t["lmean"][:, ti:ti + 1], t["lrstd"][:, ti:ti + 1],
                     mk=f"lmean{s}", rk=f"lrstd{s}",
                     bneg_ap=t["lbb2"][:, ti:ti + 1], bneg_key=f"lbb2{s}",
                     sfx=f"_{s}", sq_ap=xls[:, :], sq_key=xlk)
        emit_tm2fm(ctx, xls, xlk, lambda c, fxs=fxs: fxs[:, c, :], fxk, eng="sync")
        for cc in range(NCHUNK):
            b.emit("tensor", lambda e, cc=cc, psl=psl, fxs=fxs: e.matmul(
                psl[0:8, 0:128], t["frwT"][:, cc, :], fxs[:, cc, :],
                start=(cc == 0), stop=(cc == NCHUNK - 1)),
                reads=["frwT", fxk], writes=[pslk], group=S + f"lg{ti}",
                group_end=(cc == NCHUNK - 1))
        b.emit("vector", lambda e, lgs=lgs, psl=psl: e.tensor_copy(lgs[0:8, :], psl[0:8, 0:128]),
               reads=[pslk], writes=[lgk])
        emit_tr_32x128(ctx, tpa, tpak, lgs, lgk)
        b.emit("vector", lambda e, p8=p8, tpa=tpa: e.tensor_add(p8[:, :], tpa[:, 0:8], t["frb"][:, :]),
               reads=[tpak, "frb"], writes=[p8k])
        b.emit("vector", lambda e, p8=p8, m1=m1: e.reduce_max(m1[:, :], p8[:, :], axis=AX.X, negate=True),
               reads=[p8k], writes=[m1k])
        b.emit("scalar", lambda e, p8=p8, m1=m1: e.activation(out=p8[:, :], in_=p8[:, :], func=AF.Exp, bias=m1[:, :]),
               reads=[p8k, m1k], writes=[p8k])
        b.emit("vector", lambda e, p8=p8, m1=m1: e.reduce_sum(m1[:, :], p8[:, :], axis=AX.X),
               reads=[p8k], writes=[m1k])
        b.emit("vector", lambda e, m1=m1: e.reciprocal(m1[:, :], m1[:, :]),
               reads=[m1k], writes=[m1k])
        b.emit("vector", lambda e, p8=p8, m1=m1: e.tensor_scalar(p8[:, :], p8[:, :], m1[:, :], None, ALU.mult),
               reads=[p8k, m1k], writes=[p8k])
        b.emit("vector", lambda e, p8=p8, m1=m1: e.reduce_max(m1[:, :], p8[:, :], axis=AX.X),
               reads=[p8k], writes=[m1k])
        b.emit("vector", lambda e, p8=p8, m1=m1, ohs=ohs: e.tensor_scalar(ohs[:, :], p8[:, :], m1[:, :], None, ALU.is_ge),
               reads=[p8k, m1k], writes=[ohk])
        b.emit("vector", lambda e, p8=p8, ohs=ohs, scr=scr: e.scalar_tensor_tensor(scr[:, :], ohs[:, :], -2.0, p8[:, :], ALU.mult, ALU.add),
               reads=[ohk, p8k], writes=[scrk])
        b.emit("vector", lambda e, scr=scr, m2s=m2s: e.reduce_max(m2s[:, :], scr[:, :], axis=AX.X),
               reads=[scrk], writes=[m2k])
        b.emit("vector", lambda e, p8=p8, m2s=m2s, fin=fin: e.tensor_scalar(fin[:, :], p8[:, :], m2s[:, :], None, ALU.is_ge),
               reads=[p8k, m2k], writes=[fink])
        b.emit("scalar", lambda e, fin=fin, finb=finb: e.activation(out=finb[:, :], in_=fin[:, :], func=AF.Copy),
               reads=[fink], writes=[finbk])
        b.emit("vector", lambda e, p8=p8, fin=fin, scr=scr: e.tensor_mul(scr[:, :], p8[:, :], fin[:, :]),
               reads=[p8k, fink], writes=[scrk])
        b.emit("vector", lambda e, ti=ti, scr=scr: e.tensor_copy(t["fwn"][:, ti:ti + 1], scr[:, 7:8]),
               reads=[scrk], writes=["fwn"])
        emit_select(ctx, scr[:, 0:7], scrk, fin[:, 0:7], [fink],
                    t["fiota"][:, 0:7], ["fiota"], t["f99"][:, 0:7], ["f99"], n=7, scr=sel, scrk=selk)
        b.emit("vector", lambda e, ti=ti, scr=scr: e.tensor_reduce(t["fe1"][:, ti:ti + 1], scr[:, 0:7], AX.X, ALU.min),
               reads=[scrk], writes=["fe1"])
        b.emit("vector", lambda e, ti=ti, ohs=ohs: e.tensor_scalar(ohs[:, 0:7], t["fiota"][:, 0:7], t["fe1"][:, ti:ti + 1], None, ALU.is_equal),
               reads=["fiota", "fe1"], writes=[ohk])
        b.emit("vector", lambda e, scr=scr, sel=sel: e.tensor_scalar(sel[:, 0:7], scr[:, 0:7], -99.0, None, ALU.add),
               reads=[scrk], writes=[selk])
        b.emit("vector", lambda e, sel=sel, ohs=ohs: e.tensor_mul(sel[:, 0:7], sel[:, 0:7], ohs[:, 0:7]),
               reads=[selk, ohk], writes=[selk])
        b.emit("vector", lambda e, scr=scr, sel=sel: e.tensor_sub(scr[:, 0:7], scr[:, 0:7], sel[:, 0:7]),
               reads=[scrk, selk], writes=[scrk])
        b.emit("vector", lambda e, ti=ti, scr=scr: e.tensor_reduce(t["fe2"][:, ti:ti + 1], scr[:, 0:7], AX.X, ALU.min),
               reads=[scrk], writes=["fe2"])
        b.emit("vector", lambda e, ohs=ohs, p8=p8, scr=scr: e.tensor_mul(scr[:, 0:7], ohs[:, 0:7], p8[:, 0:7]),
               reads=[ohk, p8k], writes=[scrk])
        b.emit("vector", lambda e, ti=ti, scr=scr: e.tensor_reduce(t["fw1"][:, ti:ti + 1], scr[:, 0:7], AX.X, ALU.add),
               reads=[scrk], writes=["fw1"])
        b.emit("vector", lambda e, ti=ti, scr=scr: e.tensor_scalar(scr[:, 0:7], t["fiota"][:, 0:7], t["fe2"][:, ti:ti + 1], None, ALU.is_equal),
               reads=["fiota", "fe2"], writes=[scrk])
        b.emit("vector", lambda e, scr=scr, p8=p8: e.tensor_mul(scr[:, 0:7], scr[:, 0:7], p8[:, 0:7]),
               reads=[scrk, p8k], writes=[scrk])
        b.emit("vector", lambda e, ti=ti, scr=scr: e.tensor_reduce(t["fw2"][:, ti:ti + 1], scr[:, 0:7], AX.X, ALU.add),
               reads=[scrk], writes=["fw2"])
        b.emit("vector", lambda e, tpb=tpb, finb=finb: e.tensor_copy(tpb[:, 0:8], finb[:, :]),
               reads=[finbk], writes=[tpbk])
        emit_tr_128x32(ctx, indfm[:, ti * 128:(ti + 1) * 128], "indfm", tpb, tpbk)
    b.emit("vector", lambda e: e.memset(t["frun"][:, :], 0.0), writes=["frun"])
    b.emit("vector", lambda e: e.tensor_tensor_scan(
        jrfm[0:8, :], t["fones8"][:, 0:1].broadcast_to((8, 4096)), indfm[0:8, :],
        t["frun"][:, 0:1], ALU.mult, ALU.add),
        reads=["fones8", "indfm", "frun"], writes=["jrfm"])
    b.emit("vector", lambda e: e.tensor_copy(t["fcnt"][:, 0:1], jrfm[0:8, 2047:2048]),
           reads=["jrfm"], writes=["fcnt"])
    b.emit("vector", lambda e: e.tensor_copy(t["fcnt"][:, 1:2], jrfm[0:8, 4095:4096]),
           reads=["jrfm"], writes=["fcnt"])
    b.emit("vector", lambda e: e.tensor_sub(t["fcnt"][:, 1:2], t["fcnt"][:, 1:2], t["fcnt"][:, 0:1]),
           reads=["fcnt"], writes=["fcnt"])
    b.emit("vector", lambda e: e.tensor_sub(jrfm[0:8, :], jrfm[0:8, :], indfm[0:8, :]),
           reads=["jrfm", "indfm"], writes=["jrfm"])
    for ti in range(NT):
        s = ti % 2
        def sx(name, s=s):
            return t[f"{name}{s}"], f"{name}{s}"
        tpc, tpck = sx("ftpc")
        ohs, ohk = sx("fohs")
        scr, scrk = sx("fscrs")
        sc1, sc1k = sx("fsc1c")
        b.emit("scalar", lambda e, ti=ti, s=s: e.activation(
            out=xlt[:, :], in_=resid[:, ti, :], func=AF.Identity,
            scale=t["lrstd"][:, ti:ti + 1], bias=t["lbb2"][:, ti:ti + 1]),
            reads=["resid", f"lrstd{s}", f"lbb2{s}"], writes=["xlt"])
        emit_tr_32x128(ctx, tpc, tpck, jrfm[:, ti * 128:(ti + 1) * 128], "jrfm")
        for (esel, jrn, gxn, gidxn) in (("fe1", "fjr1", "fgx1", "fgidx1"),
                                        ("fe2", "fjr2", "fgx2", "fgidx2")):
            gix, gixk = sx(gidxn)
            b.emit("vector", lambda e, ti=ti, esel=esel, ohs=ohs: e.tensor_scalar(
                ohs[:, :], t["fiota"][:, :], t[esel][:, ti:ti + 1], None, ALU.is_equal),
                reads=["fiota", esel], writes=[ohk])
            b.emit("vector", lambda e, ohs=ohs, scr=scr, tpc=tpc: e.tensor_mul(scr[:, :], ohs[:, :], tpc[:, 0:8]),
                   reads=[ohk, tpck], writes=[scrk])
            b.emit("vector", lambda e, ti=ti, jrn=jrn, scr=scr: e.tensor_reduce(
                t[jrn][:, ti:ti + 1], scr[:, :], AX.X, ALU.add),
                reads=[scrk], writes=[jrn])
            b.emit("vector", lambda e, ti=ti, esel=esel, jrn=jrn, sc1=sc1: e.scalar_tensor_tensor(
                sc1[:, :], t[esel][:, ti:ti + 1], 2.0 * HALF_PAD, t[jrn][:, ti:ti + 1],
                ALU.mult, ALU.add), reads=[esel, jrn], writes=[sc1k])
            b.emit("vector", lambda e, sc1=sc1: e.tensor_add(sc1[:, :], sc1[:, :], t["fhoff"][:, :]),
                   reads=[sc1k, "fhoff"], writes=[sc1k])
            b.emit("vector", lambda e, sc1=sc1: e.tensor_tensor(sc1[:, :], sc1[:, :], t["ftrash"][:, :], ALU.min),
                   reads=[sc1k, "ftrash"], writes=[sc1k])
            b.emit("vector", lambda e, ti=ti, gxn=gxn, sc1=sc1: e.tensor_copy(t[gxn][:, ti:ti + 1], sc1[:, :]),
                   reads=[sc1k], writes=[gxn])
            b.emit("vector", lambda e, gix=gix, sc1=sc1: e.tensor_copy(gix[:, :], sc1[:, :]),
                   reads=[sc1k], writes=[gixk])
            b.emit("gpsimd", lambda e, gix=gix: e.indirect_dma_start(
                mslab[:, :], IndirectOffsetOnAxis(gix[:, 0:1], 0), xlt[:, :], None),
                reads=[gixk, "xlt"], writes=[mkey], dma=True)
    for qq in range(4):
        b.emit("gpsimd", lambda e, qq=qq: e.dma_start(
            out=dr["respill"][qq * 8:(qq + 1) * 8].rearrange("T p f -> p T f"),
            in_=resid[:, qq * 8:(qq + 1) * 8, :]),
            reads=["resid"], writes=["respill"], dma=True)
    b.emit("gpsimd", lambda e: e.dma_start(out=dr["agin"][:, :], in_=t["fcnt"][:, :]),
           reads=["fcnt"], writes=["agin"], dma=True)
    b.emit("gpsimd", lambda e, ncores=ncores: e.collective_compute(
        "AllGather", ALU.bypass, replica_groups=[list(range(ncores))],
        ins=[dr["agin"][:, :]], outs=[dr["agout"][:, :]]),
        reads=["agin", mkey], writes=["agout"], tag="AG")
    b.emit("gpsimd", lambda e: e.dma_start(
        out=t["fagt"][:, :].rearrange("e (c r) -> e c r", r=2),
        in_=dr["agout"].rearrange("(c e) r -> e c r", e=8)),
        reads=["agout"], writes=["fagt"], dma=True)
    for row in range(2):
        b.emit("vector", lambda e, row=row, ncores=ncores: e.tensor_mul(
            t["fscr8w"][:, 0:2 * ncores], t["fagt"][:, :],
            t["fmsk"][:, row * 2 * ncores:(row + 1) * 2 * ncores]),
            reads=["fagt", "fmsk"], writes=["fscr8w"])
        b.emit("vector", lambda e, row=row, ncores=ncores: e.tensor_reduce(
            t["foff"][:, row:row + 1], t["fscr8w"][:, 0:2 * ncores], AX.X, ALU.add),
            reads=["fscr8w"], writes=["foff"])
    for row in range(2):
        b.emit("vector", lambda e: e.memset(flgT[0:32, 0:32], 0.0), writes=["flgT"])
        b.emit("vector", lambda e, row=row: e.tensor_copy(flgT[0:8, 0:1], t["foff"][:, row:row + 1]),
               reads=["foff"], writes=["flgT"])
        b.emit("vector", lambda e: e.transpose(ftp[0:32, 0:32], flgT[0:32, 0:32]),
               reads=["flgT"], writes=["ftp"])
        b.emit("tensor", lambda e: e.matmul(
            fps1[0:128, 0:8], t["fonesf"][0:1, :], ftp[0:1, 0:8],
            start=True, stop=True),
            reads=["fonesf", "ftp"], writes=["fps1"], group=S + f"offb{row}", group_end=True)
        b.emit("vector", lambda e, row=row: e.tensor_copy(t["foffb"][:, row, :], fps1[0:128, 0:8]),
               reads=["fps1"], writes=["foffb"])
    ekeys = EXPERT_KEYS + COMBINE_KEYS
    for k in EXPERT_KEYS:
        base = ["resid", "xfm"] if k in ("fwi", "fwo") else ["resid"]
        b.inherit(k + S, base + ([k + "F0_"] if call == 1 else []))
    for par in range(2):
        b.inherit("fwi" + S + f"_{par}", ["xfm", "cvt32", "cvt16"])
        b.inherit("fwo" + S + f"_{par}", ["xfm", "cvt32", "cvt16"])
    hk, xgk, yk, wik, wok, xgt6k, ytm6k, zk = [k + S for k in EXPERT_KEYS]
    fhfm, fxg, fyfm, fwi, fwo, fxgt6, fytm6, fzero = [t[k] for k in EXPERT_KEYS]
    b.emit("vector", lambda e: e.memset(fzero[:, :], 0.0), writes=[zk])
    b.emit("vector", lambda e: e.memset(fzero[:, :], 0.0), writes=[zk])
    emit_dyn_dma(ctx, ST_ZROW, yslab, [[1024, 1], [1, 1024]],
                 fzero[0:1, :], True, reads=[zk], writes=[ykey], eng="scalar")
    done_q = ctx.t.setdefault("_doneq", set())
    for j in range(NEXP):
        q = QSEQ[j]
        if call == 0 and q not in done_q:
            emit_ffn_convert_q(ctx, q)
            done_q.add(q)
        emit_dyn_dma(ctx, ST_SUB + j, mslab,
                     [[1024, 128], [131072, 6], [1, 1024]],
                     fxgt6[:, :, :], False, reads=[mkey], writes=[xgt6k])
        for kk in range(6):
            for cc in range(NCHUNK):
                b.emit("sync", lambda e, kk=kk, cc=cc: e.dma_start(
                    out=fxg[:, cc, kk * 128:(kk + 1) * 128],
                    in_=fxgt6[:, kk, cc * 128:(cc + 1) * 128], transpose=True),
                    reads=[xgt6k], writes=[xgk], dma=True)
        WQ = ("sync", "scalar", "gpsimd")
        for m in range(8):
            par = m % 2
            wkm = wik + f"_{par}"
            b.emit(WQ[m % 3], lambda e, m=m, par=par, q=q: e.dma_start(
                out=fwi[:, :, par * 512:(par + 1) * 512],
                in_=dr["wscr_i"][q, :, :, m * 512:(m + 1) * 512].rearrange("dc p i -> p dc i")),
                reads=["wscr_i"], writes=[wkm], dma=True)
            for icl in range(4):
                ic = m * 4 + icl
                pk_a, pk_b = (("fps1", "fps2") if ic % 2 == 0 else ("fps3", "fps4"))
                for (pk, lo, w) in ((pk_a, 0, 512), (pk_b, 512, 256)):
                    ps = t[pk]
                    for dc in range(NCHUNK):
                        b.emit("tensor", lambda e, ps=ps, par=par, icl=icl, dc=dc, lo=lo, w=w: e.matmul(
                            ps[:, 0:w], fwi[:, dc, par * 512 + icl * 128:par * 512 + (icl + 1) * 128],
                            fxg[:, dc, lo:lo + w],
                            start=(dc == 0), stop=(dc == NCHUNK - 1)),
                            reads=[wkm, xgk], writes=[pk], group=S + f"h{j}_{ic}_{lo}",
                            group_end=(dc == NCHUNK - 1))
                b.emit("scalar", lambda e, ic=ic, pa=t[pk_a]: e.activation(
                    out=fhfm[:, ic, 0:512], in_=pa[:, 0:512], func=AF.Relu),
                    reads=[pk_a], writes=[hk])
                b.emit("scalar", lambda e, ic=ic, pb=t[pk_b]: e.activation(
                    out=fhfm[:, ic, 512:768], in_=pb[:, 0:256], func=AF.Relu),
                    reads=[pk_b], writes=[hk])
        for oc in range(8):
            par = oc % 2
            wkm = wok + f"_{par}"
            b.emit(WQ[oc % 3], lambda e, oc=oc, par=par, q=q: e.dma_start(
                out=fwo[:, :, par * 128:(par + 1) * 128],
                in_=dr["wscr_o"][q, :, :, oc * 128:(oc + 1) * 128].rearrange("ic p o -> p ic o")),
                reads=["wscr_o"], writes=[wkm], dma=True)
            pk_a, pk_b = (("fps3", "fps4") if oc % 2 == 0 else ("fps1", "fps2"))
            for (pk, lo, w) in ((pk_a, 0, 512), (pk_b, 512, 256)):
                ps = t[pk]
                for ic in range(32):
                    b.emit("tensor", lambda e, ps=ps, par=par, ic=ic, lo=lo, w=w: e.matmul(
                        ps[:, 0:w], fwo[:, ic, par * 128:(par + 1) * 128],
                        fhfm[:, ic, lo:lo + w],
                        start=(ic == 0), stop=(ic == 31)),
                        reads=[wkm, hk], writes=[pk], group=S + f"y{j}_{oc}_{lo}",
                        group_end=(ic == 31))
            b.emit("scalar", lambda e, oc=oc, pa=t[pk_a]: e.activation(
                out=fyfm[:, oc, 0:512], in_=pa[:, 0:512], func=AF.Copy),
                reads=[pk_a], writes=[yk])
            b.emit("scalar", lambda e, oc=oc, pb=t[pk_b]: e.activation(
                out=fyfm[:, oc, 512:768], in_=pb[:, 0:256], func=AF.Copy),
                reads=[pk_b], writes=[yk])
        for kk in range(6):
            for cc in range(NCHUNK):
                b.emit("sync", lambda e, kk=kk, cc=cc: e.dma_start(
                    out=fytm6[:, kk, cc * 128:(cc + 1) * 128],
                    in_=fyfm[:, cc, kk * 128:(kk + 1) * 128], transpose=True),
                    reads=[yk], writes=[ytm6k], dma=True)
        emit_dyn_dma(ctx, ST_SUB + j, yslab,
                     [[1024, 128], [131072, 6], [1, 1024]],
                     fytm6[:, :, :], True, reads=[ytm6k], writes=[ykey])
    groups = [[2 * p, 2 * p + 1] for p in range(ncores // 2)]
    b.emit("gpsimd", lambda e, groups=groups: e.collective_compute(
        "AllReduce", ALU.add, replica_groups=groups,
        ins=[dr["barin"][:, :]], outs=[dr["barout"][:, :]]),
        reads=[ykey, "barin"], writes=["ybar" + S], tag="BAR")
    b.inherit("resid", [mkey] + [k + S for k in ekeys])
    for sname in ("frt0", "frt1", "fg1b0", "fg1b1", "fg2b0", "fg2b1"):
        b.inherit(sname + S, [wik + "_0", wik + "_1", "cvt32", "cvt16"])
    for sname in ("fdelta0", "fdelta1", "fxltd0", "fxltd1"):
        b.inherit(sname + S, [wok + "_0", wok + "_1", "cvt32", "cvt16"])
    for ti in range(NT):
        row = ti // 16
        s = ti % 2
        def sx(name, s=s):
            return t[f"{name}{s}"], f"{name}{s}"
        frt_s, rtk_s = t[f"frt{s}"], f"frt{s}" + S
        xld, xldk = t[f"fxltd{s}"], f"fxltd{s}" + S
        fdl, dk_s = t[f"fdelta{s}"], f"fdelta{s}" + S
        ohs, ohk = sx("fohs")
        scr, scrk = sx("fscrs")
        sc1, sc1k = sx("fsc1c")
        kpt, kptk = sx("fkeptc")
        sc2, sc2k = sx("fscr2c")
        sel, selk = sx("fsels")
        b.emit("sync", lambda e, ti=ti, frt_s=frt_s: e.dma_start(
            out=frt_s[:, :], in_=dr["respill"][ti]),
            reads=["respill"], writes=[rtk_s], dma=True)
        b.emit("scalar", lambda e, ti=ti, s=s, frt_s=frt_s, xld=xld: e.activation(
            out=xld[:, :], in_=frt_s[:, :], func=AF.Identity,
            scale=t["lrstd"][:, ti:ti + 1], bias=t["lbb2"][:, ti:ti + 1]),
            reads=[rtk_s, f"lrstd{s}", f"lbb2{s}"], writes=[xldk])
        for (esel, jrn, gxn, gi) in (("fe1", "fjr1", "fgx1", 1),
                                     ("fe2", "fjr2", "fgx2", 2)):
            gix, gixk = sx(f"fgidx{gi}")
            gbuf, gbk = t[f"fg{gi}b{s}"], f"fg{gi}b{s}" + S
            b.emit("vector", lambda e, ti=ti, esel=esel, ohs=ohs: e.tensor_scalar(
                ohs[:, :], t["fiota"][:, :], t[esel][:, ti:ti + 1], None, ALU.is_equal),
                reads=["fiota", esel], writes=[ohk])
            b.emit("vector", lambda e, row=row, ohs=ohs, scr=scr: e.tensor_mul(
                scr[:, :], ohs[:, :], t["foffb"][:, row, :]),
                reads=[ohk, "foffb"], writes=[scrk])
            b.emit("vector", lambda e, scr=scr, sc1=sc1: e.tensor_reduce(sc1[:, :], scr[:, :], AX.X, ALU.add),
                   reads=[scrk], writes=[sc1k])
            b.emit("vector", lambda e, ti=ti, jrn=jrn, sc1=sc1: e.tensor_add(sc1[:, :], sc1[:, :], t[jrn][:, ti:ti + 1]),
                   reads=[sc1k, jrn], writes=[sc1k])
            b.emit("vector", lambda e, cap=cap, sc1=sc1, kpt=kpt: e.tensor_single_scalar(
                kpt[:, :], sc1[:, :], float(cap), ALU.is_lt),
                reads=[sc1k], writes=[kptk])
            b.emit("vector", lambda e, ti=ti, esel=esel, sc2=sc2: e.tensor_single_scalar(
                sc2[:, :], t[esel][:, ti:ti + 1], 50.0, ALU.is_lt),
                reads=[esel], writes=[sc2k])
            b.emit("vector", lambda e, kpt=kpt, sc2=sc2: e.tensor_mul(kpt[:, :], kpt[:, :], sc2[:, :]),
                   reads=[kptk, sc2k], writes=[kptk])
            emit_select(ctx, sc1[:, :], sc1k, kpt[:, :], [kptk],
                        t[gxn][:, ti:ti + 1], [gxn], t["fzrowc"][:, :], ["fzrowc"], n=1,
                        scr=sel, scrk=selk)
            b.emit("vector", lambda e, gix=gix, sc1=sc1: e.tensor_copy(gix[:, :], sc1[:, :]),
                   reads=[sc1k], writes=[gixk])
            b.emit("gpsimd", lambda e, gix=gix, gbuf=gbuf: e.indirect_dma_start(
                gbuf[:, :], None, yslab[:, :], IndirectOffsetOnAxis(gix[:, 0:1], 0)),
                reads=[gixk, ykey, "ybar" + S], writes=[gbk], dma=True)
        g1b, g2b = t[f"fg1b{s}"], t[f"fg2b{s}"]
        g1bk, g2bk = f"fg1b{s}" + S, f"fg2b{s}" + S
        b.emit("scalar", lambda e, ti=ti, fdl=fdl, g1b=g1b: e.activation(
            out=fdl[:, :], in_=g1b[:, :], func=AF.Identity,
            scale=t["fw1"][:, ti:ti + 1], bias=t["fzero1"][:, 0:1]),
            reads=[g1bk, "fw1", "fzero1"], writes=[dk_s])
        b.emit("vector", lambda e, ti=ti, fdl=fdl, g2b=g2b: e.scalar_tensor_tensor(
            fdl[:, :], g2b[:, :], t["fw2"][:, ti:ti + 1], fdl[:, :], ALU.mult, ALU.add),
            reads=[g2bk, "fw2", dk_s], writes=[dk_s])
        b.emit("vector", lambda e, ti=ti, fdl=fdl, xld=xld: e.scalar_tensor_tensor(
            fdl[:, :], xld[:, :], t["fwn"][:, ti:ti + 1], fdl[:, :], ALU.mult, ALU.add),
            reads=[xldk, "fwn", dk_s], writes=[dk_s])
        b.emit("vector", lambda e, ti=ti, fdl=fdl, frt_s=frt_s: e.tensor_add(
            resid[:, ti, :], frt_s[:, :], fdl[:, :]),
            reads=[rtk_s, dk_s], writes=["resid"])
        if out_dram is not None:
            b.emit("scalar", lambda e, ti=ti: e.dma_start(
                out=out_dram[ti], in_=resid[:, ti, :]),
                reads=["resid"], writes=["outd"], dma=True)


def emit_ffn_consts(ctx):
    b, t, dr = ctx.b, ctx.t, ctx.dram
    b.emit("gpsimd", lambda e: e.dma_start(
        out=t["frwT"][:, :, :], in_=dr["rwT_d"].rearrange("(c p) e -> p c e", p=128)),
        reads=["rwT_d"], writes=["frwT"], dma=True)
    for name, dst in (("rb_bc", "frb"), ("iota_d", "fiota"), ("c99_d", "f99"),
                      ("hoff_d", "fhoff"), ("trash_d", "ftrash"), ("zrowc_d", "fzrowc"),
                      ("fmsk_d", "fmsk")):
        b.emit("gpsimd", lambda e, name=name, dst=dst: e.dma_start(
            out=t[dst][:], in_=dr[name][:]), reads=[name], writes=[dst], dma=True)
    b.emit("gpsimd", lambda e: e.dma_start(out=t["stab"][:, :], in_=dr["stab_d"][:, :]),
           reads=["stab_d"], writes=["stab"], dma=True)
    b.emit("gpsimd", lambda e: e.dma_start(out=dr["barin"][0:1, 0:8], in_=t["frb"][0:1, 0:8]),
           reads=["frb"], writes=["barin"], dma=True)


TRASH0 = NEXP * 2 * HALF_PAD
"""Host-side input prep for the FFN (and full kernel): per-core tables/consts.
Layout-only numpy (transpose/reshape/broadcast/constants) -- no arithmetic on
input VALUES."""

TRASH0 = NEXP * 2 * HALF_PAD


def core_rows(c, ncores):
    if ncores == 8:
        P, h = divmod(c, 2)
        return [P + 4 * h, P + 4 * h + 8]
    elif ncores == 2:
        return [c, c + 2]
    raise ValueError(ncores)


def ffn_core_inputs(core, ncores, router_w, router_b):
    half = core % 2
    nb = 2 * ncores
    rows = core_rows(core, ncores)
    d = {}
    d["rwT_d"] = np.ascontiguousarray(router_w.T)
    d["rb_bc"] = np.ascontiguousarray(np.broadcast_to(router_b[None, :], (128, 8)))
    d["iota_d"] = np.ascontiguousarray(np.broadcast_to(np.arange(8, dtype=np.float32)[None, :], (128, 8)))
    d["c99_d"] = np.full((128, 8), 99.0, np.float32)
    d["hoff_d"] = np.full((128, 1), float(half * HALF_PAD), np.float32)
    d["trash_d"] = np.full((128, 1), float(TRASH0 + half), np.float32)
    d["zrowc_d"] = np.full((128, 1), float(TRASH0 + 2 + half), np.float32)
    msk = np.zeros((2, ncores, 2), np.float32)
    for row in range(2):
        myb = rows[row]
        for c2 in range(ncores):
            r2rows = core_rows(c2, ncores)
            for r2 in range(2):
                if c2 != core and r2rows[r2] < myb:
                    msk[row, c2, r2] = 1.0
    d["fmsk_d"] = np.ascontiguousarray(
        np.broadcast_to(msk.reshape(1, -1), (8, 2 * ncores * 2)))
    stab = np.zeros(256, np.int32)
    conv = [0, 1, 2, 3] if half == 0 else [4, 5, 6, 3]
    for q, e in enumerate(conv):
        for dc in range(8):
            stab[q * 8 + dc] = e * 4194304 + dc * 524288
        for icg in range(8):
            stab[64 + q * 8 + icg] = e * 4194304 + icg * 524288
    SUB_E0 = [0, 1, 2, 0, 1, 2, 3]
    SUB_HP = [0, 0, 0, 1, 1, 1, half]
    for j in range(7):
        e = SUB_E0[j] + (4 * half if j < 6 else 0)
        hp = SUB_HP[j]
        stab[192 + j] = (e * 2 * HALF_PAD + hp * HALF_PAD) * 1024
    stab[199] = (TRASH0 + 2 + half) * 1024
    for ee in range(7):
        stab[200 + ee] = (ee * 2 * HALF_PAD + half * HALF_PAD) * 1024
    stab[207] = (TRASH0 + half) * 1024
    d["stab_d"] = stab.reshape(1, 256)
    return d




from concourse.bass_utils import run_bass_kernel_spmd

NCORES = 8
CAP = 4096


def _build():
    nc = bass.Bass("TRN2", target_bir_lowering=False)
    b = B(nc)
    ctx = Ctx(nc, b)
    dr = ctx.dram
    x = nc.dram_tensor("x", [2, 2048, 1024], F32, kind="ExternalInput").ap()
    y = nc.dram_tensor("y", [2, 2048, 1024], F32, kind="ExternalOutput").ap()
    dr["wqkvT"] = nc.dram_tensor("wqkvT", [3, 4, 1024, 128], F32, kind="ExternalInput").ap()
    for name, shape in [("woutTd", [512, 1024]), ("boutd", [128, 8]), ("masksd", [128, 4, 128])]:
        dr[name] = nc.dram_tensor(name, shape, F32, kind="ExternalInput").ap()
    for name, shape in [("bq_fm", [128, 8]), ("bk_fm", [128, 8]), ("bnw_fm", [128, 8]),
                        ("bld_d", [16, 1]), ("bpm_d", [16, 8]), ("bhsel_d", [128, 2]),
                        ("bE1_d", [16, 128]), ("bE2_d", [2, 128]),
                        ("bowT_d", [1024, 1024]), ("onesf_d", [1, 128])]:
        dr[name] = nc.dram_tensor(name, shape, F32, kind="ExternalInput").ap()
    dr["ones_d"] = nc.dram_tensor("ones_d", [128, 1], BF16, kind="ExternalInput").ap()
    dr["expw_i"] = nc.dram_tensor("expw_i", [7, 1024, 4096], F32, kind="ExternalInput").ap()
    dr["expw_o"] = nc.dram_tensor("expw_o", [7, 4096, 1024], F32, kind="ExternalInput").ap()
    dr["rwT_d"] = nc.dram_tensor("rwT_d", [1024, 8], F32, kind="ExternalInput").ap()
    for name, shape, dt in [("rb_bc", [128, 8], F32), ("iota_d", [128, 8], F32),
                            ("c99_d", [128, 8], F32), ("hoff_d", [128, 1], F32),
                            ("trash_d", [128, 1], F32), ("zrowc_d", [128, 1], F32),
                            ("fmsk_d", [8, 2 * NCORES * 2], F32), ("stab_d", [1, 256], I32)]:
        dr[name] = nc.dram_tensor(name, shape, dt, kind="ExternalInput").ap()
    dr["mslab"] = nc.dram_tensor("mslab", [MROWS, D], BF16, addr_space="Shared").ap()
    dr["yslab"] = nc.dram_tensor("yslab", [MROWS, D], BF16, addr_space="Shared").ap()
    dr["wscr_i"] = nc.dram_tensor("wscr_i", [4, 8, 128, 4096], BF16).ap()
    dr["wscr_o"] = nc.dram_tensor("wscr_o", [4, 32, 128, 1024], BF16).ap()
    dr["respill"] = nc.dram_tensor("respill", [NT, 128, D], F32).ap()
    dr["agin"] = nc.dram_tensor("agin", [8, 2], F32).ap()
    dr["agout"] = nc.dram_tensor("agout", [NCORES * 8, 2], F32).ap()
    dr["barin"] = nc.dram_tensor("barin", [1, 8], F32).ap()
    dr["barout"] = nc.dram_tensor("barout", [1, 8], F32).ap()

    alloc_common(ctx)
    alloc_ffn(ctx, NCORES)
    alloc_ffn_expert(ctx)
    alloc_convert(ctx)
    t = ctx.t
    with ctx.stack:
        for r in range(2):
            b.emit("sync", lambda e, r=r: e.dma_start(
                out=t["resid"][:, 16 * r:16 * (r + 1), :],
                in_=x[r].rearrange("(T p) f -> p T f", p=128)),
                reads=["x"], writes=["resid"], dma=True)
        emit_ffn_consts(ctx)
        ctx.stage_begin()
        alloc_attn_a(ctx)
        b.emit("gpsimd", lambda e: e.dma_start(
            out=t["woutT"][:, :, :], in_=dr["woutTd"].rearrange("(h p) o -> p h o", p=128)),
            writes=["woutT"], dma=True)
        b.emit("gpsimd", lambda e: e.dma_start(out=t["bout"][:, :], in_=dr["boutd"][:, :]),
               writes=["bout"], dma=True)
        b.emit("gpsimd", lambda e: e.dma_start(out=t["amasks"][:, :, :], in_=dr["masksd"][:, :, :]),
               writes=["amasks"], dma=True)
        emit_attn_a(ctx)
        ctx.stage_end()
        ctx.stage_begin()
        alloc_ffn_stage(ctx)
        emit_ffn(ctx, 0, NCORES, CAP)
        ctx.stage_end()
        ctx.stage_begin()
        alloc_attn_b(ctx)
        emit_attn_b_tables(ctx)
        emit_attn_b(ctx)
        ctx.stage_end()
        ctx.stage_begin()
        alloc_ffn_stage(ctx)
        ytiles = [y[ti // 16].rearrange("(T p) f -> p T f", p=128)[:, ti % 16, :]
                  for ti in range(NT)]
        emit_ffn(ctx, 1, NCORES, CAP, out_dram=ytiles)
        ctx.stage_end()
        b.emit("gpsimd", None, reads=["outd"])
        b.finalize()
    return nc


def _host_tables():
    pm = np.zeros((16, 8), np.float32)
    for h in range(16):
        pm[h, h // 2] = 1
    hsel = np.zeros((128, 2), np.float32)
    for p in range(128):
        hsel[p, p // 64] = 1
    E1 = np.zeros((16, 128), np.float32)
    for h in range(16):
        for p in range(128):
            if h % 2 == p // 64:
                E1[h, p] = 1
    return pm, hsel, E1


def kernel(**inputs):
    f32 = lambda a: np.ascontiguousarray(np.asarray(a, dtype=np.float32))
    x = f32(inputs["L0_attn_a_x"])
    Wk = f32(inputs["L0_attn_a_Wk"])
    Wq = f32(inputs["L0_attn_a_Wq"])
    Wv = f32(inputs["L0_attn_a_Wv"])
    W_out = f32(inputs["L0_attn_a_W_out"])
    b_out = f32(inputs["L0_attn_a_b_out"])
    bq = f32(inputs["L0_attn_b_q"])
    bk = f32(inputs["L0_attn_b_k"])
    log_decay = f32(inputs["L0_attn_b_log_decay"])
    bnw = f32(inputs["L0_attn_b_norm_weight"])
    bow = f32(inputs["L0_attn_b_o_w"])
    router_w = f32(inputs["L0_ffn_router_w"])
    router_b = f32(inputs["L0_ffn_router_b"])
    wi = f32(inputs["L0_ffn_experts_inter"])
    wo = f32(inputs["L0_ffn_experts_out"])

    wqkvT = np.ascontiguousarray(np.stack([
        Wq.transpose(0, 2, 1), Wk.transpose(0, 2, 1), Wv.transpose(0, 2, 1)]))
    pm, hsel, E1 = _host_tables()
    common = {
        "wqkvT": wqkvT,
        "woutTd": np.ascontiguousarray(W_out.T),
        "boutd": np.ascontiguousarray(b_out.reshape(8, 128).T),
        "masksd": make_attn_a_masks(),
        "bq_fm": np.ascontiguousarray(bq.reshape(8, 128).T),
        "bk_fm": np.ascontiguousarray(bk.reshape(8, 128).T),
        "bnw_fm": np.ascontiguousarray(bnw.reshape(8, 128).T),
        "bld_d": np.ascontiguousarray(log_decay.reshape(16, 1)),
        "bpm_d": pm, "bhsel_d": hsel, "bE1_d": E1,
        "bE2_d": np.ascontiguousarray(hsel.T),
        "bowT_d": np.ascontiguousarray(bow.T),
        "onesf_d": np.ones((1, 128), np.float32),
        "ones_d": np.ones((128, 1), ml_dtypes.bfloat16),
        "expw_i": wi, "expw_o": wo,
    }
    in_maps = []
    for c in range(NCORES):
        rows = core_rows(c, NCORES)
        d = {"x": np.ascontiguousarray(x[rows])}
        d.update(common)
        d.update(ffn_core_inputs(c, NCORES, router_w, router_b))
        in_maps.append(d)

    nc = _build()
    res = run_bass_kernel_spmd(nc, in_maps, core_ids=list(range(NCORES)))
    out = np.empty((16, 2048, 1024), np.float32)
    for c in range(NCORES):
        out[core_rows(c, NCORES)] = np.asarray(res.results[c]["y"], dtype=np.float32)
    return out


# version 06
# speedup vs baseline: 1.1043x; 1.0779x over previous
import numpy as np
import ml_dtypes
import contextlib

"""Recording/replay emit framework with automatic semaphore dependency edges.

Buffers tracked by tensor name (or explicit string key). Every RAW/WAR/WAW
gets a semaphore edge (race-detector-sound, incl. same-engine). DMA ops use
per-queue round-robin semaphore lanes with at most one outstanding DMA per
lane (issuer pre-waits its lane), so lane_sem >= 16*k soundly means the k-th
use of that lane completed.
"""
import concourse.bass as bass
import concourse.mybir as mybir

ENGINES = ("sync", "scalar", "vector", "tensor", "gpsimd")
NLANES = 8


def bufkey(x):
    if isinstance(x, str):
        return x
    t = getattr(x, "tensor", x)
    return t.name.split("__s")[0]


class Op:
    __slots__ = ("i", "eng", "fn", "reads", "writes", "dma", "group",
                 "deps", "seq", "need_inc", "sem_val_after", "lane", "tag", "bar",
                 "noinstr", "local")

    def __init__(self, i, eng, fn, reads, writes, dma, group, tag):
        self.i = i
        self.eng = eng
        self.fn = fn
        self.reads = reads
        self.writes = writes
        self.dma = dma
        self.group = group
        self.deps = set()
        self.seq = None
        self.need_inc = False
        self.sem_val_after = None
        self.lane = None
        self.tag = tag
        self.bar = False
        self.noinstr = False
        self.local = False


class B:
    def __init__(self, nc):
        self.nc = nc
        self.ops = []
        self.last_write = {}
        self.readers = {}
        self.acc_writers = {}
        self.open_groups = {}
        self.last_bar = None

    def emit(self, eng, fn=None, reads=(), writes=(), dma=False, group=None,
             group_end=False, tag=None, noinstr=False, local=False):
        assert eng in ENGINES
        op = Op(len(self.ops), eng, fn, [bufkey(r) for r in reads],
                [bufkey(w) for w in writes], dma, group, tag)
        op.noinstr = noinstr
        op.local = local
        deps = set()
        for k in op.reads:
            lw = self.last_write.get(k)
            if lw is not None:
                deps.add(lw)
            deps.update(self.acc_writers.get(k, ()))
        for k in op.writes:
            lw = self.last_write.get(k)
            if lw is not None:
                deps.add(lw)
            for r in self.readers.get(k, ()):
                deps.add(r)
        if self.last_bar is not None:
            for k in op.reads + op.writes:
                if k not in self.last_write and k not in self.readers:
                    deps.add(self.last_bar)
                    break
        if group is not None:
            grp = self.open_groups.setdefault(group, [])
            deps -= set(grp)
            grp.append(op.i)
        op.deps = deps
        self.ops.append(op)
        for k in op.reads:
            self.readers.setdefault(k, []).append(op.i)
        if group is None or group_end:
            if group is not None:
                self.open_groups.pop(group)
            for k in op.writes:
                self.last_write[k] = op.i
                self.readers[k] = []
        for k in op.writes:
            self.acc_writers.setdefault(k, []).append(op.i)
        return op

    def inherit(self, new_key, old_keys):
        lw = sorted({self.last_write[k] for k in old_keys if k in self.last_write})
        rd = [i for k in old_keys for i in self.readers.get(k, ())]
        aw = [i for k in old_keys for i in self.acc_writers.get(k, ())]
        if lw:
            self.last_write[new_key] = lw[-1]
        extra = sorted(set(rd) | set(lw[:-1]) | set(aw))
        if extra:
            self.readers[new_key] = extra
        return new_key

    def barrier(self, eng="vector", tag="bar"):
        op = self.emit(eng, None, tag=tag)
        deps = set(self.last_write.values())
        for rl in self.readers.values():
            deps.update(rl)
        deps.discard(op.i)
        deps = {d for d in deps if not self.ops[d].local}
        op.deps = deps
        op.bar = True
        for k in list(self.last_write):
            self.last_write[k] = op.i
            self.readers[k] = []
        for k in list(self.acc_writers):
            self.acc_writers[k] = []
        self.last_bar = op.i
        return op

    def finalize(self):
        nc = self.nc
        ops = self.ops
        seqs = {e: [] for e in ENGINES}
        for op in ops:
            op.seq = len(seqs[op.eng])
            seqs[op.eng].append(op)
        for op in ops:
            for d in op.deps:
                dop = ops[d]
                if dop.local and dop.eng == op.eng:
                    continue
                dop.need_inc = True
        comp_count = {e: 0 for e in ENGINES}
        lane_count = {}
        lane_rr = {e: 0 for e in ENGINES}
        pre_waits = {}
        for op in ops:
            if op.dma:
                lane = lane_rr[op.eng]
                lane_rr[op.eng] = (lane + 1) % NLANES
                op.lane = lane
                key = (op.eng, lane)
                prior = lane_count.get(key, 0)
                if prior > 0:
                    pre_waits[op.i] = (("l", op.eng, lane), 16 * prior)
                lane_count[key] = prior + 1
                op.sem_val_after = 16 * (prior + 1)
                op.need_inc = True
            elif op.need_inc:
                comp_count[op.eng] += 1
                op.sem_val_after = comp_count[op.eng]
        import contextlib
        self._semstack = contextlib.ExitStack()
        csem = {e: self._semstack.enter_context(nc.semaphore(f"c_{e}"))
                for e in ENGINES if comp_count[e] > 0}
        lsem = {k: self._semstack.enter_context(nc.semaphore(f"l_{k[0]}_{k[1]}"))
                for k in lane_count}
        known = {e: {} for e in ENGINES}
        waits = []
        for op in ops:
            need = {}
            for d in sorted(op.deps):
                dop = ops[d]
                if dop.sem_val_after is None:
                    continue
                key = (("l", dop.eng, dop.lane) if dop.dma else ("c", dop.eng))
                v = dop.sem_val_after
                if need.get(key, 0) < v:
                    need[key] = v
            if op.i in pre_waits:
                key, v = pre_waits[op.i]
                if need.get(key, 0) < v:
                    need[key] = v
            k = known[op.eng]
            todo = []
            for key, v in sorted(need.items(), key=lambda kv: str(kv[0])):
                if k.get(key, 0) < v:
                    todo.append((key, v))
                    k[key] = v
            waits.append(todo)
        self.waits = waits
        self._csem, self._lsem = csem, lsem

        def sem_of(key):
            if key[0] == "c":
                return csem[key[1]]
            return lsem[(key[1], key[2])]

        block_cm = nc.Block()
        block = block_cm.__enter__()

        def run_engine(engine_name):
            def runner(eobj):
                try:
                    eobj.get_value_cache().nocache = True
                except Exception:
                    pass
                for op in seqs[engine_name]:
                    todo = waits[op.i]
                    for key, v in todo:
                        eobj.wait_ge(sem_of(key), v)
                    try:
                        instr = op.fn(eobj) if op.fn is not None else eobj.nop()
                    except Exception:
                        import traceback as _tb
                        _tb.print_exc()
                        raise
                    if op.need_inc and op.sem_val_after is not None:
                        sem = (lsem[(op.eng, op.lane)] if op.dma else csem[op.eng])
                        amt = 16 if op.dma else 1
                        if instr is not None:
                            instr.then_inc(sem, amt)
                        else:
                            eobj.sem_inc(sem, amt)
                    if op.dma:
                        eobj.get_value_cache().barrier()
            return runner

        block.sync(run_engine("sync"))
        block.scalar(run_engine("scalar"))
        block.vector(run_engine("vector"))
        block.tensor(run_engine("tensor"))
        block.gpsimd(run_engine("gpsimd"))
        block_cm.__exit__(None, None, None)
        self._semstack.close()


"""Megakernel builder for mk_hybrid_d1 (see DESIGN.md).

Per-core program; 8 cores SPMD. Core c owns batch rows core_rows(c)
(stride-4 pair layout). All emitters share a Ctx.
"""
import numpy as np
import contextlib
import concourse.bass as bass
import concourse.mybir as mybir

F32 = mybir.dt.float32
BF16 = mybir.dt.bfloat16
I32 = mybir.dt.int32
AF = mybir.ActivationFunctionType
ALU = mybir.AluOpType
AX = mybir.AxisListType

D = 1024
NCHUNK = 8
TCORE = 4096
NT = 32
TROW = 2048
NTR = 16
DILS = [1, 2, 4, 8]
NORM = 1.0 / np.sqrt(128.0)
HOFF = [0, 2048, 3072, 3584]




class Ctx:
    def __init__(self, nc, b):
        self.nc = nc
        self.b = b
        self.stack = contextlib.ExitStack()
        self.t = {}
        self.dram = {}
        self.stage_stack = None
        self.stage_id = 0

    def _name(self, name):
        return name if self.stage_stack is None else f"{name}__s{self.stage_id}"

    def _stk(self):
        return self.stack if self.stage_stack is None else self.stage_stack

    def sbuf(self, name, shape, dtype):
        h = self._stk().enter_context(self.nc.sbuf_tensor(self._name(name), shape, dtype))
        self.t[name] = h
        return h

    def psum(self, name, shape, dtype=F32):
        h = self._stk().enter_context(self.nc.psum_tensor(self._name(name), shape, dtype))
        self.t[name] = h
        return h

    def stage_begin(self):
        assert self.stage_stack is None
        self.stage_id += 1
        self.stage_stack = contextlib.ExitStack()

    def stage_end(self):
        self.stage_stack.close()
        self.stage_stack = None
        self.b.barrier()


def emit_ln_tile(ctx, resid_tile, resid_key, dst_tm, dst_key, mean_ap, rstd_ap,
                 mk="lmean", rk="lrstd", bneg_ap=None, bneg_key=None, sfx="",
                 sq_ap=None, sq_key=None):
    b = ctx.b
    t = ctx.t
    ssq, s, m2, bb = t["lssq" + sfx], t["ls" + sfx], t["lm2" + sfx], t["lbb" + sfx]
    if sq_ap is None:
        sq, sqk = t["lsq"][:], "lsq"
    else:
        sq, sqk = sq_ap, sq_key

    b.emit("scalar", lambda e: e.activation(out=sq, in_=resid_tile, func=AF.Square, accum_out=ssq[:]),
           reads=[resid_key], writes=[sqk, ssq])
    b.emit("vector", lambda e: e.reduce_sum(s[:], resid_tile, axis=AX.X),
           reads=[resid_key], writes=[s])
    b.emit("vector", lambda e: e.tensor_scalar_mul(mean_ap, s[:], 1.0 / D),
           reads=[s], writes=[mk])
    b.emit("scalar", lambda e: e.activation(out=m2[:], in_=mean_ap, func=AF.Square),
           reads=[mk], writes=[m2])
    b.emit("scalar", lambda e: e.activation(out=bb[:], in_=m2[:], func=AF.Copy, scale=-1.0, bias=1e-5),
           reads=[m2], writes=[bb])
    b.emit("scalar", lambda e: e.activation(out=rstd_ap, in_=ssq[:], func=AF.Sqrt, scale=1.0 / D, bias=bb[:]),
           reads=[ssq, bb], writes=[rk])
    b.emit("vector", lambda e: e.reciprocal(rstd_ap, rstd_ap), reads=[rk], writes=[rk])
    bbk = bneg_key if bneg_key is not None else "lbb"
    bba = bb[:, :] if bneg_ap is None else bneg_ap
    b.emit("vector", lambda e: e.tensor_mul(bba, mean_ap, rstd_ap), reads=[mk, rk], writes=[bbk])
    b.emit("vector", lambda e: e.tensor_scalar_mul(bba, bba, -1.0), reads=[bbk], writes=[bbk])
    b.emit("scalar", lambda e: e.activation(out=dst_tm, in_=resid_tile, func=AF.Identity, scale=rstd_ap, bias=bba),
           reads=[resid_key, rk, bbk], writes=[dst_key])


def emit_tm2fm(ctx, src_tm, src_key, dst_fm_fn, dst_key, eng="sync"):
    b = ctx.b
    for c in range(NCHUNK):
        b.emit(eng, lambda e, c=c: e.dma_start(
            out=dst_fm_fn(c), in_=src_tm[:, c * 128:(c + 1) * 128], transpose=True),
            reads=[src_key], writes=[dst_key], dma=True)


def dil_ap(xfm_chunk, dil, s0, cnt_s):
    base = xfm_chunk.rearrange("p (seg s) -> p seg s", s=128)
    return base[:, ::dil, s0:s0 + cnt_s].rearrange("p n s -> p s n")


def emit_attn_a(ctx):
    nc, b, t = ctx.nc, ctx.b, ctx.t
    resid = t["resid"]
    xfm = t["xfm"]
    xlt = t["xlt"]
    q, k, v = t["aQ"], t["aK"], t["aV"]
    vst = t["aVs"]
    att = t["aAtt"]
    sraw = t["aSraw"]
    sm = t["aSm"]
    rowmax, rowsum = t["aRmax"], t["aRsum"]
    ysb = t["aY"]
    dfm = t["aDfm"]
    dtm = t["aDtm"]
    masks = t["amasks"]
    bout = t["bout"]
    woutT = t["woutT"]
    wq = t["awq"]; wk = t["awk"]; wv = t["awv"]
    wqkvT = ctx.dram["wqkvT"]
    ps_qkv = t["ps_qkv"]; ps_s = t["ps_s"]; ps_av = t["ps_av"]; ps_y = t["ps_y"]

    def load_head_w(h, buf):
        for wi, wsb in enumerate((wq, wk, wv)):
            src = wqkvT[wi, h].rearrange("(c p) k -> p c k", p=128)
            b.emit("gpsimd", lambda e, wsb=wsb, src=src, buf=buf: e.dma_start(
                out=wsb[:, buf], in_=src), reads=["wqkvT"], writes=[wsb], dma=True)

    load_head_w(0, 0)
    for r in range(2):
        for i in range(NTR):
            ti = 16 * r + i
            emit_ln_tile(ctx, resid[:, ti, :], "resid", xlt[:, :], "xlt",
                         t["lmean"][:, 0:1], t["lrstd"][:, 0:1])
            emit_tm2fm(ctx, xlt, "xlt", lambda c, i=i: xfm[:, c, i * 128:(i + 1) * 128], "xfm",
                       eng="sync")
        for h, dil in enumerate(DILS):
            L = 16 // dil
            ncols = 128 * L
            buf = 0
            if r == 0 and h == 0:
                pass
            if h > 0 or r > 0:
                load_head_w(h, 0)
            for j0 in range(0, ncols, 512):
              w = min(512, ncols - j0)
              for (wsb, dst, tag) in ((wq, q, "q"), (wk, k, "k")):
                    cs = w // L
                    s0 = j0 // L
                    for cc in range(NCHUNK):
                        ap = dil_ap(xfm[:, cc, :], dil, s0, cs)
                        b.emit("tensor", lambda e, wsb=wsb, buf=buf, cc=cc, ap=ap, w=w: e.matmul(
                            ps_qkv[:, 0:w], wsb[:, buf, cc, :], ap,
                            start=(cc == 0), stop=(cc == NCHUNK - 1)),
                            reads=[wsb, "xfm"], writes=[ps_qkv], group=f"qk{r}{h}{tag}{j0}",
                            group_end=(cc == NCHUNK - 1))
                    b.emit("vector", lambda e, dst=dst, w=w: e.tensor_copy(
                        dst[:, 0:w], ps_qkv[:, 0:w]),
                        reads=[ps_qkv], writes=[dst])
              for gl in range(w // 128):
                g = j0 // 128 + gl
                s0 = (g * 128) // L
                cs = 128 // L
                for cc in range(NCHUNK):
                    ap = dil_ap(xfm[:, cc, :], dil, s0, cs)
                    b.emit("scalar", lambda e, ap=ap: e.activation(
                        out=vst[:, :], in_=ap, func=AF.Copy),
                        reads=["xfm"], writes=["aVs"])
                    b.emit("tensor", lambda e, cc=cc, buf=buf: e.matmul(
                        ps_qkv[:, 0:128], vst[:, :], wv[:, buf, cc, :],
                        start=(cc == 0), stop=(cc == NCHUNK - 1)),
                        reads=[wv, "aVs"], writes=[ps_qkv], group=f"v{r}{h}{g}",
                        group_end=(cc == NCHUNK - 1))
                b.emit("vector", lambda e, gl=gl: e.tensor_copy(
                    v[:, gl * 128:(gl + 1) * 128], ps_qkv[:, 0:128]),
                    reads=[ps_qkv], writes=[v])
              for gl in range(w // 128):
                g = j0 // 128 + gl
                o = gl * 128
                b.emit("tensor", lambda e, o=o: e.matmul(
                    ps_s[:, :], k[:, o:o + 128], q[:, o:o + 128], start=True, stop=True),
                    reads=[k, q], writes=[ps_s], group=f"s{r}{h}{g}", group_end=True)
                b.emit("vector", lambda e, h=h: e.scalar_tensor_tensor(
                    sraw[:], ps_s[:, :], NORM, masks[:, h, :], ALU.mult, ALU.add),
                    reads=[ps_s, masks], writes=[sraw])
                b.emit("vector", lambda e: e.reduce_max(rowmax[:], sraw[:], axis=AX.X, negate=True),
                       reads=[sraw], writes=[rowmax])
                b.emit("scalar", lambda e: e.activation(out=sraw[:], in_=sraw[:], func=AF.Exp, bias=rowmax[:]),
                       reads=[sraw, rowmax], writes=[sraw])
                b.emit("vector", lambda e: e.reduce_sum(rowsum[:], sraw[:], axis=AX.X),
                       reads=[sraw], writes=[rowsum])
                b.emit("vector", lambda e: e.reciprocal(rowsum[:], rowsum[:]),
                       reads=[rowsum], writes=[rowsum])
                b.emit("vector", lambda e: e.tensor_scalar(sm[:], sraw[:], rowsum[:], None, ALU.mult),
                       reads=[sraw, rowsum], writes=[sm])
                b.emit("tensor", lambda e, o=o: e.matmul(
                    ps_av[:, :], v[:, o:o + 128], sm[:, :], start=True, stop=True),
                    reads=[v, sm], writes=[ps_av], group=f"av{r}{h}{g}", group_end=True)
                b.emit("vector", lambda e, h=h, g=g: e.tensor_copy(
                    att[:, HOFF[h] + g * 128:HOFF[h] + (g + 1) * 128], ps_av[:, :]),
                    reads=[ps_av], writes=[att])
        PS_OFF = [0, 512, 768, 896]
        SS = [(True, True), (True, False), (False, False), (False, True)]
        for tau in range(4):
          for half in range(2):
            for oc4 in range(4):
                oc = 4 * half + oc4
                for h, dil in enumerate(DILS):
                    wsl = 512 // dil
                    a0 = HOFF[h] + tau * wsl
                    p0 = PS_OFF[h]
                    st, sp = SS[h]
                    b.emit("tensor", lambda e, h=h, oc=oc, a0=a0, p0=p0, w=wsl, st=st, sp=sp: e.matmul(
                        ps_y[:, p0:p0 + w], woutT[:, h, oc * 128:(oc + 1) * 128], att[:, a0:a0 + w],
                        start=st, stop=sp),
                        reads=[woutT, att], writes=[ps_y], group=f"y{r}{tau}{oc}",
                        group_end=(h == 3))
                b.emit("vector", lambda e: e.tensor_copy(ysb[:], ps_y[:, 0:512]),
                       reads=[ps_y], writes=[ysb])
                for h, dil in enumerate(DILS[1:], start=1):
                    p0 = PS_OFF[h]
                    w = 512 // dil
                    b.emit("vector", lambda e, p0=p0, w=w, dil=dil: e.tensor_add(
                        ysb[:, ::dil], ysb[:, ::dil], ps_y[:, p0:p0 + w]),
                        reads=[ps_y, ysb], writes=[ysb])
                b.emit("scalar", lambda e, oc=oc, oc4=oc4: e.activation(
                    out=dfm[:, oc4, :], in_=ysb[:], func=AF.Identity, scale=1.0,
                    bias=bout[:, oc:oc + 1]),
                    reads=[ysb, bout], writes=[dfm])
            for tt in range(4):
                ti = 16 * r + tau * 4 + tt
                for cc4 in range(4):
                    b.emit("sync", lambda e, cc4=cc4, tt=tt: e.dma_start(
                        out=dtm[:, cc4 * 128:(cc4 + 1) * 128],
                        in_=dfm[:, cc4, tt * 128:(tt + 1) * 128], transpose=True),
                        reads=[dfm], writes=[dtm], dma=True)
                b.emit("vector", lambda e, ti=ti, half=half: e.tensor_add(
                    resid[:, ti, 512 * half:512 * (half + 1)],
                    resid[:, ti, 512 * half:512 * (half + 1)], dtm[:, :]),
                    reads=[dtm, "resid"], writes=["resid"])


def alloc_attn_a(ctx):
    ctx.sbuf("woutT", [128, 4, D], BF16)
    ctx.sbuf("bout", [128, NCHUNK], F32)
    ctx.sbuf("amasks", [128, 4, 128], BF16)
    ctx.sbuf("aQ", [128, 512], BF16)
    ctx.sbuf("aVs", [128, 128], BF16)
    ctx.sbuf("aK", [128, 512], BF16)
    ctx.sbuf("aV", [128, 512], BF16)
    ctx.sbuf("aAtt", [128, 3840], BF16)
    ctx.sbuf("aSraw", [128, 128], F32)
    ctx.sbuf("aSm", [128, 128], BF16)
    ctx.sbuf("aRmax", [128, 1], F32)
    ctx.sbuf("aRsum", [128, 1], F32)
    ctx.sbuf("aY", [128, 512], F32)
    ctx.sbuf("aDfm", [128, 4, 512], BF16)
    ctx.sbuf("aDtm", [128, 512], BF16)
    ctx.sbuf("awq", [128, 1, NCHUNK, 128], BF16)
    ctx.sbuf("awk", [128, 1, NCHUNK, 128], BF16)
    ctx.sbuf("awv", [128, 1, NCHUNK, 128], BF16)
    ctx.psum("ps_qkv", [128, 512])
    ctx.psum("ps_s", [128, 128])
    ctx.psum("ps_av", [128, 128])
    ctx.psum("ps_y", [128, 1024])


def alloc_common(ctx):
    ctx.sbuf("resid", [128, NT, D], F32)
    ctx.sbuf("xfm", [128, NCHUNK, TROW], BF16)
    ctx.sbuf("xlt", [128, D], BF16)
    ctx.sbuf("lsq", [128, D], BF16)
    ctx.sbuf("lssq", [128, 1], F32)
    ctx.sbuf("ls", [128, 1], F32)
    ctx.sbuf("lm2", [128, 1], F32)
    ctx.sbuf("lbb", [128, 1], F32)
    ctx.sbuf("lmean", [128, NT], F32)
    ctx.sbuf("lrstd", [128, NT], F32)


def make_attn_a_masks():
    m = np.full((128, 4, 128), -1e30, np.float32)
    for h, dil in enumerate(DILS):
        L = 16 // dil
        for p in range(128):
            s1, n = divmod(p, L)
            for f in range(128):
                s2, mm = divmod(f, L)
                if s1 == s2 and n >= mm:
                    m[p, h, f] = 0.0
    return m


def alloc_attn_b(ctx):
    ctx.sbuf("bqf", [128, 8], F32)
    ctx.sbuf("bkf", [128, 8], F32)
    ctx.sbuf("bqk", [128, 8], F32)
    ctx.sbuf("bld", [16, 1], F32)
    ctx.sbuf("blam", [16, 1], F32)
    ctx.sbuf("bpm", [16, 8], F32)
    ctx.sbuf("blamp", [16, 8], F32)
    ctx.sbuf("bhsel", [128, 2], F32)
    ctx.sbuf("bE1", [16, 128], F32)
    ctx.sbuf("bE2", [2, 128], F32)
    ctx.sbuf("bnw", [128, 8], F32)
    ctx.sbuf("blamd", [128, 8], F32)
    ctx.sbuf("bcd", [128, 8], F32)
    ctx.sbuf("bscale", [128, 8], F32)
    ctx.sbuf("bones", [128, 1], BF16)
    ctx.sbuf("bonesf", [1, 128], F32)
    ctx.sbuf("byscan", [128, 1024], F32)
    ctx.sbuf("bsq", [128, 1024], BF16)
    ctx.sbuf("browrms", [1, 512], F32)
    ctx.sbuf("beps", [1, 1], F32)
    ctx.sbuf("browssb", [128, 512], F32)
    ctx.sbuf("bowT", [128, NCHUNK, D], BF16)
    ctx.sbuf("bY", [128, 512], F32)
    ctx.sbuf("bDfm", [128, 4, 512], BF16)
    ctx.sbuf("bDtm", [128, 512], BF16)
    ctx.psum("ps_b1", [128, 512])
    ctx.psum("ps_b2", [128, 512])
    ctx.psum("ps_rms", [1, TROW])
    ctx.psum("ps_tab", [16, 8])


def emit_attn_b_tables(ctx):
    b, t = ctx.b, ctx.t
    dr = ctx.dram
    for name, dst in (("bq_fm", "bqf"), ("bk_fm", "bkf"), ("bnw_fm", "bnw")):
        b.emit("gpsimd", lambda e, name=name, dst=dst: e.dma_start(
            out=t[dst][:, :], in_=dr[name][:, :]), reads=[name], writes=[dst], dma=True)
    for name, dst in (("bld_d", "bld"), ("bpm_d", "bpm"), ("bhsel_d", "bhsel"),
                      ("bE1_d", "bE1"), ("bE2_d", "bE2")):
        b.emit("gpsimd", lambda e, name=name, dst=dst: e.dma_start(
            out=t[dst][:, :], in_=dr[name][:, :]), reads=[name], writes=[dst], dma=True)
    b.emit("vector", lambda e: e.tensor_mul(t["bqk"][:], t["bqf"][:], t["bkf"][:]),
           reads=["bqf", "bkf"], writes=["bqk"])
    b.emit("tensor", lambda e: e.matmul(t["ps_tab"][0:2, :], t["bhsel"][:, :], t["bqk"][:, :],
                                        start=True, stop=True),
           reads=["bhsel", "bqk"], writes=["ps_tab"], group="btab1", group_end=True)
    b.emit("vector", lambda e: e.tensor_copy(t["bqk"][0:2, :], t["ps_tab"][0:2, :]),
           reads=["ps_tab"], writes=["bqk"])
    b.emit("tensor", lambda e: e.matmul(t["ps_b1"][:, 0:8], t["bE2"][0:2, :], t["bqk"][0:2, :], start=True, stop=True),
           reads=["bE2", "bqk"], writes=["ps_b1"], group="btab2", group_end=True)
    b.emit("vector", lambda e: e.tensor_copy(t["bcd"][:, :], t["ps_b1"][:, 0:8]),
           reads=["ps_b1"], writes=["bcd"])
    b.emit("scalar", lambda e: e.activation(out=t["blam"][:, :], in_=t["bld"][:, :], func=AF.Sigmoid),
           reads=["bld"], writes=["blam"])
    b.emit("vector", lambda e: e.tensor_scalar(t["blamp"][:, :], t["bpm"][:, :], t["blam"][:, 0:1], None, ALU.mult),
           reads=["bpm", "blam"], writes=["blamp"])
    b.emit("tensor", lambda e: e.matmul(t["ps_b1"][:, 0:8], t["bE1"][:, :], t["blamp"][:, :], start=True, stop=True),
           reads=["bE1", "blamp"], writes=["ps_b1"], group="btab3", group_end=True)
    b.emit("vector", lambda e: e.tensor_copy(t["blamd"][:, :], t["ps_b1"][:, 0:8]),
           reads=["ps_b1"], writes=["blamd"])
    b.emit("vector", lambda e: e.tensor_mul(t["bscale"][:, :], t["bcd"][:, :], t["bnw"][:, :]),
           reads=["bcd", "bnw"], writes=["bscale"])


def emit_attn_b(ctx):
    nc, b, t = ctx.nc, ctx.b, ctx.t
    resid = t["resid"]
    xfm = t["xfm"]; xlt = t["xlt"]
    yscan = t["byscan"]; sq = t["bsq"]
    rowrms = t["browrms"]; rmsb = t["browssb"]
    owT = t["bowT"]; ysb = t["bY"]; dfm = t["bDfm"]; dtm = t["bDtm"]
    ps1 = t["ps_b1"]; ps2 = t["ps_b2"]; psr = t["ps_rms"]
    b.emit("gpsimd", lambda e: e.dma_start(
        out=owT[:, :, :], in_=ctx.dram["bowT_d"].rearrange("(c p) od -> p c od", p=128)),
        reads=["bowT_d"], writes=["bowT"], dma=True)
    b.emit("gpsimd", lambda e: e.dma_start(
        out=t["bones"][:, :], in_=ctx.dram["ones_d"][:, 0:1]), reads=["ones_d"], writes=["bones"], dma=True)
    b.emit("gpsimd", lambda e: e.dma_start(
        out=t["bonesf"][:, :], in_=ctx.dram["onesf_d"][0:1, :]), reads=["onesf_d"], writes=["bonesf"], dma=True)
    b.emit("vector", lambda e: e.memset(t["beps"][:, :], 1e-6), writes=["beps"])
    for r in range(2):
        for i in range(NTR):
            ti = 16 * r + i
            emit_ln_tile(ctx, resid[:, ti, :], "resid", xlt[:, :], "xlt",
                         t["lmean"][:, 0:1], t["lrstd"][:, 0:1])
            emit_tm2fm(ctx, xlt, "xlt", lambda c, i=i: xfm[:, c, i * 128:(i + 1) * 128], "xfm",
                       eng="sync")
        for cc in range(NCHUNK):
            lam_b = t["blamd"][:, cc:cc + 1].broadcast_to((128, 1024))
            for hh in range(2):
                init = 0.0 if hh == 0 else yscan[:, 1023:1024]
                b.emit("vector", lambda e, cc=cc, hh=hh, init=init, lam_b=lam_b: e.tensor_tensor_scan(
                    yscan[:, :], lam_b, xfm[:, cc, hh * 1024:(hh + 1) * 1024], init, ALU.mult, ALU.add),
                    reads=["blamd", "xfm", "byscan"], writes=["byscan"])
                b.emit("scalar", lambda e, cc=cc: e.activation(
                    out=sq[:, :], in_=yscan[:, :], func=AF.Square, scale=t["bcd"][:, cc:cc + 1]),
                    reads=["byscan", "bcd"], writes=["bsq"])
                b.emit("scalar", lambda e, cc=cc, hh=hh: e.activation(
                    out=xfm[:, cc, hh * 1024:(hh + 1) * 1024], in_=yscan[:, :], func=AF.Identity, scale=t["bscale"][:, cc:cc + 1]),
                    reads=["byscan", "bscale"], writes=["xfm"])
                for tl in range(2):
                    tt = hh * 2 + tl
                    b.emit("tensor", lambda e, tt=tt, tl=tl, cc=cc: e.matmul(
                        psr[0:1, tt * 512:(tt + 1) * 512], t["bones"][:, :], sq[:, tl * 512:(tl + 1) * 512],
                        start=(cc == 0), stop=(cc == NCHUNK - 1)),
                        reads=["bones", "bsq"], writes=["ps_rms"], group=f"brms{r}{tt}",
                        group_end=(cc == NCHUNK - 1))
        for tt in range(4):
            b.emit("vector", lambda e, tt=tt: e.tensor_copy(rowrms[:, :], psr[0:1, tt * 512:(tt + 1) * 512]),
                   reads=["ps_rms"], writes=["browrms"])
            b.emit("scalar", lambda e: e.activation(out=rowrms[:, :], in_=rowrms[:, :],
                                                    func=AF.Sqrt, scale=1.0 / D, bias=t["beps"][:, :]),
                   reads=["browrms", "beps"], writes=["browrms"])
            b.emit("vector", lambda e: e.reciprocal(rowrms[:, :], rowrms[:, :]),
                   reads=["browrms"], writes=["browrms"])
            b.emit("tensor", lambda e: e.matmul(
                ps2[:, :], t["bonesf"][0:1, :], rowrms[0:1, :],
                start=True, stop=True),
                reads=["bonesf", "browrms"], writes=["ps_b2"], group=f"brb{r}{tt}", group_end=True)
            b.emit("vector", lambda e: e.tensor_copy(rmsb[:, :], ps2[:, :]),
                   reads=["ps_b2"], writes=["browssb"])
            for half in range(2):
                for oc4 in range(4):
                    oc = 4 * half + oc4
                    for cc in range(NCHUNK):
                        b.emit("tensor", lambda e, oc=oc, cc=cc, tt=tt: e.matmul(
                            ps1[:, :], owT[:, cc, oc * 128:(oc + 1) * 128],
                            xfm[:, cc, tt * 512:(tt + 1) * 512],
                            start=(cc == 0), stop=(cc == NCHUNK - 1)),
                            reads=["bowT", "xfm"], writes=["ps_b1"], group=f"bow{r}{tt}{oc}",
                            group_end=(cc == NCHUNK - 1))
                    b.emit("vector", lambda e: e.tensor_mul(ysb[:, :], ps1[:, :], rmsb[:, :]),
                           reads=["ps_b1", "browssb"], writes=["bY"])
                    b.emit("scalar", lambda e, oc4=oc4: e.activation(
                        out=dfm[:, oc4, :], in_=ysb[:], func=AF.Copy, scale=1.0, bias=0.0),
                        reads=["bY"], writes=["bDfm"])
                for tloc in range(4):
                    ti = 16 * r + tt * 4 + tloc
                    for cc4 in range(4):
                        b.emit("sync", lambda e, cc4=cc4, tloc=tloc: e.dma_start(
                            out=dtm[:, cc4 * 128:(cc4 + 1) * 128],
                            in_=dfm[:, cc4, tloc * 128:(tloc + 1) * 128], transpose=True),
                            reads=["bDfm"], writes=["bDtm"], dma=True)
                    b.emit("vector", lambda e, ti=ti, half=half: e.tensor_add(
                        resid[:, ti, 512 * half:512 * (half + 1)],
                        resid[:, ti, 512 * half:512 * (half + 1)], dtm[:, :]),
                        reads=["bDtm", "resid"], writes=["resid"])





def emit_tr_32x128(ctx, dst, dst_key, src, src_key):
    b = ctx.b
    for i in range(4):
        b.emit("vector", lambda e, i=i: e.transpose(
            dst[32 * i:32 * (i + 1), 0:32], src[0:32, 32 * i:32 * (i + 1)]),
            reads=[src_key], writes=[dst_key])


def emit_tr_128x32(ctx, dst, dst_key, src, src_key):
    b = ctx.b
    for i in range(4):
        b.emit("vector", lambda e, i=i: e.transpose(
            dst[0:32, 32 * i:32 * (i + 1)], src[32 * i:32 * (i + 1), 0:32]),
            reads=[src_key], writes=[dst_key])



def emit_select(ctx, out_ap, out_key, mask_ap, mask_keys, true_ap, true_keys, false_ap, false_keys, n=8,
                scr=None, scrk=None):
    b, t = ctx.b, ctx.t
    if scr is None:
        scr, scrk = t["fsel"], "fsel"
    sa = scr[:, 0:n]
    b.emit("vector", lambda e: e.tensor_sub(sa, false_ap, true_ap),
           reads=list(false_keys) + list(true_keys), writes=[scrk])
    b.emit("vector", lambda e: e.tensor_mul(sa, sa, mask_ap),
           reads=list(mask_keys) + [scrk], writes=[scrk])
    b.emit("vector", lambda e: e.tensor_sub(out_ap, false_ap, sa),
           reads=list(false_keys) + [scrk], writes=[out_key])

from concourse.bass_core_rust import IndirectOffsetOnAxis

NEXP = 7
HALF_PAD = 1280
EXEC_ROWS = 768
MROWS = NEXP * 2 * HALF_PAD + 4


def alloc_ffn(ctx, ncores):
    ctx.sbuf("lbb2", [128, NT], F32)
    ctx.sbuf("frwT", [128, NCHUNK, 8], BF16)
    ctx.sbuf("frb", [128, 8], F32)
    ctx.sbuf("fiota", [128, 8], F32)
    ctx.sbuf("f99", [128, 8], F32)
    ctx.sbuf("fp8", [128, 8], F32)
    ctx.sbuf("fsel", [128, 8], F32)
    ctx.sbuf("fm1", [128, 1], F32)
    ctx.sbuf("fm2", [128, 1], F32)
    ctx.sbuf("foh", [128, 8], F32)
    ctx.sbuf("find", [128, 8], F32)
    ctx.sbuf("findb", [128, 8], BF16)
    ctx.sbuf("fe1", [128, NT], F32)
    ctx.sbuf("fe2", [128, NT], F32)
    ctx.sbuf("fw1", [128, NT], F32)
    ctx.sbuf("fw2", [128, NT], F32)
    ctx.sbuf("fwn", [128, NT], F32)
    ctx.sbuf("fjr1", [128, NT], F32)
    ctx.sbuf("fjr2", [128, NT], F32)
    ctx.sbuf("fgx1", [128, NT], F32)
    ctx.sbuf("fgx2", [128, NT], F32)
    ctx.sbuf("fzero1", [128, 1], F32)
    ctx.sbuf("frun", [8, 1], F32)
    ctx.sbuf("fscr8w", [8, max(16, 2 * ncores)], F32)
    ctx.sbuf("fones8", [8, 1], F32)
    ctx.sbuf("fonesf", [1, 128], F32)
    ctx.sbuf("fcnt", [8, 2], F32)
    ctx.sbuf("fagt", [8, ncores * 2], F32)
    ctx.sbuf("fmsk", [8, 2 * ncores * 2], F32)
    ctx.sbuf("foff", [8, 2], F32)
    ctx.sbuf("foffb", [128, 2, 8], F32)
    ctx.sbuf("fhoff", [128, 1], F32)
    ctx.sbuf("ftrash", [128, 1], F32)
    ctx.sbuf("fzrowc", [128, 1], F32)
    ctx.sbuf("fgidx", [128, 1], I32)
    ctx.sbuf("fgidx2", [128, 1], I32)
    ctx.sbuf("fscr", [128, 8], F32)
    ctx.sbuf("fscr1", [128, 1], F32)
    ctx.sbuf("fscr2", [128, 1], F32)
    ctx.sbuf("fkept", [128, 1], F32)
    ctx.sbuf("stab", [1, 256], I32)


def alloc_ffn_stage(ctx):
    ctx.sbuf("flgT", [32, 128], F32)
    ctx.sbuf("ftp", [128, 32], F32)
    ctx.sbuf("fz0", [128, D], BF16)
    for s in range(2):
        ctx.sbuf(f"fxlt{s}", [128, D], BF16)
        ctx.sbuf(f"fxfs{s}", [128, NCHUNK, 128], BF16)
        ctx.sbuf(f"flgs{s}", [32, 128], F32)
        ctx.sbuf(f"ftpa{s}", [128, 32], F32)
        ctx.sbuf(f"ftpb{s}", [128, 32], BF16)
        ctx.sbuf(f"ftpc{s}", [128, 32], F32)
        for nm in ("fp8s", "fohs", "finds", "fscrs", "fsels"):
            ctx.sbuf(f"{nm}{s}", [128, 8], F32)
        ctx.sbuf(f"findbs{s}", [128, 8], BF16)
        for nm in ("fm1s", "fm2s", "fsc1c", "fkeptc", "fscr2c"):
            ctx.sbuf(f"{nm}{s}", [128, 1], F32)
        ctx.sbuf(f"fgidx1{s}", [128, 1], I32)
        ctx.sbuf(f"fgidx2{s}", [128, 1], I32)
        for nm in ("lssq", "ls", "lm2", "lbb"):
            ctx.sbuf(f"{nm}_{s}", [128, 1], F32)
    ctx.sbuf("indfm", [32, 4096], BF16)
    ctx.sbuf("jrfm", [32, 4096], F32)
    ctx.psum("fps1", [128, 512])
    ctx.psum("fps2", [128, 512])
    ctx.psum("fps3", [128, 512])
    ctx.psum("fps4", [128, 512])


def alloc_ffn_expert(ctx):
    nc = ctx.nc
    resid_off = nc.lookup_mloc(ctx.t["resid"]).addr
    xfm_off = nc.lookup_mloc(ctx.t["xfm"]).addr
    offs = [0]
    def at(name, shape, dtype, base=None):
        nbytes = int(np.prod(shape[1:])) * (2 if dtype == BF16 else 4)
        if base is None:
            h = nc.alloc_sbuf_tensor_at(name, shape, dtype, offset=resid_off + offs[0])
            offs[0] += nbytes
        else:
            h = nc.alloc_sbuf_tensor_at(name, shape, dtype, offset=base)
        ctx.t[name] = h
        return h
    at("fhfm", [128, 32, EXEC_ROWS], BF16)
    at("fxg0", [128, NCHUNK, EXEC_ROWS], BF16)
    at("fxg1", [128, NCHUNK, EXEC_ROWS], BF16)
    at("fyfm", [128, NCHUNK, EXEC_ROWS], BF16)
    at("fxgt6", [128, 6, D], BF16)
    at("fytm6", [128, 6, D], BF16)
    at("fzero", [128, D], BF16)
    assert offs[0] <= NT * D * 4, f"expert alias overflow {offs[0]}"
    at("fwi", [128, NCHUNK, 1024], BF16, base=xfm_off)
    at("fwo", [128, 32, 256], BF16, base=xfm_off + NCHUNK * 1024 * 2)
    at("frt0", [128, D], F32, base=xfm_off)
    at("frt1", [128, D], F32, base=xfm_off + 4096)
    at("fg1b0", [128, D], BF16, base=xfm_off + 8192)
    at("fg1b1", [128, D], BF16, base=xfm_off + 10240)
    at("fg2b0", [128, D], BF16, base=xfm_off + 12288)
    at("fg2b1", [128, D], BF16, base=xfm_off + 14336)
    at("fdelta0", [128, D], F32, base=xfm_off + 16384)
    at("fdelta1", [128, D], F32, base=xfm_off + 20480)
    at("fxltd0", [128, D], BF16, base=xfm_off + 24576)
    at("fxltd1", [128, D], BF16, base=xfm_off + 26624)


EXPERT_KEYS = ["fhfm", "fyfm", "fwi", "fwo", "fxgt6", "fytm6", "fzero", "fxg0", "fxg1"]
COMBINE_KEYS = ["frt0", "frt1", "fg1b0", "fg1b1", "fg2b0", "fg2b1",
                "fdelta0", "fdelta1", "fxltd0", "fxltd1"]

ST_CI = 0
ST_CO = 64
ST_SUB = 192
ST_ZROW = 199
ST_ZF = 200
ST_YTRASH = 207
QSEQ = [0, 1, 2, 0, 1, 2, 3]




def _dyn_reg(ctx, e, eng):
    import bass_rust
    regs = ctx.t.setdefault("_regs", {})
    key = "dyn_" + eng
    if key not in regs:
        h = e.register("dynreg").__enter__()
        regs[key] = (h, bass_rust.RuntimeValue(h, min_val=0, max_val=1 << 27))
    return regs[key]


def emit_dyn_dma(ctx, stab_idx, dram_base, dims, sbuf_ap, write_to_dram,
                 reads, writes, tag=None, eng="gpsimd"):
    b, t = ctx.b, ctx.t
    from concourse.ap import AP
    rk = "dynreg_" + eng
    chk = "dynch_" + eng
    def ld(e, stab_idx=stab_idx, eng=eng):
        h, rv = _dyn_reg(ctx, e, eng)
        e.reg_load(h, t["stab"][0:1, stab_idx:stab_idx + 1])
    b.emit(eng, ld, reads=["stab"], writes=[rk], noinstr=True, local=True)
    def go(e, eng=eng):
        h, rv = ctx.t["_regs"]["dyn_" + eng]
        dap = AP(dram_base.tensor, rv, [list(d) for d in dims])
        if write_to_dram:
            return e.dma_start(out=dap, in_=sbuf_ap)
        return e.dma_start(out=sbuf_ap, in_=dap)
    b.emit(eng, go, reads=list(reads) + [rk, chk],
           writes=list(writes) + [chk], dma=True, tag=tag)

def alloc_convert(ctx):
    nc = ctx.nc
    xfm_off = nc.lookup_mloc(ctx.t["xfm"]).addr
    ctx.t["cvt32"] = nc.alloc_sbuf_tensor_at("cvt32", [128, 4096], F32, offset=xfm_off)
    ctx.t["cvt16"] = nc.alloc_sbuf_tensor_at("cvt16", [128, 4096], BF16, offset=xfm_off + 4096 * 4)


def emit_ffn_convert_q(ctx, q):
    nc, b, t = ctx.nc, ctx.b, ctx.t
    dr = ctx.dram
    c32, c16 = t["cvt32"], t["cvt16"]
    b.inherit("cvt32", ["fwiF0__0", "fwiF0__1", "fwoF0__0", "fwoF0__1"])
    b.inherit("cvt16", ["fwiF0__0", "fwiF0__1", "fwoF0__0", "fwoF0__1"])
    for dc in range(NCHUNK):
        emit_dyn_dma(ctx, ST_CI + q * 8 + dc, dr["expw_i"],
                     [[4096, 128], [1, 4096]], c32[:, :], False,
                     reads=["expw_i"], writes=["cvt32"], eng="sync")
        b.emit("vector", lambda e: e.tensor_copy(c16[:, :], c32[:, :]),
               reads=["cvt32"], writes=["cvt16"])
        b.emit("gpsimd", lambda e, q=q, dc=dc: e.dma_start(
            out=dr["wscr_i"][q, dc, :, :], in_=c16[:, :]),
            reads=["cvt16"], writes=["wscr_i"], dma=True)
    for icg in range(8):
        emit_dyn_dma(ctx, ST_CO + q * 8 + icg, dr["expw_o"],
                     [[1024, 128], [131072, 4], [1, 1024]],
                     c32[:, :].rearrange("p (g f) -> p g f", g=4), False,
                     reads=["expw_o"], writes=["cvt32"], eng="scalar")
        b.emit("scalar", lambda e: e.activation(out=c16[:, :], in_=c32[:, :], func=AF.Copy),
               reads=["cvt32"], writes=["cvt16"])
        b.emit("gpsimd", lambda e, q=q, icg=icg: e.dma_start(
            out=dr["wscr_o"][q, icg * 4:(icg + 1) * 4].rearrange("g p f -> p g f"),
            in_=c16[:, :].rearrange("p (g f) -> p g f", g=4)),
            reads=["cvt16"], writes=["wscr_o"], dma=True)


def emit_ffn(ctx, call, ncores, cap, out_dram=None):
    nc, b, t = ctx.nc, ctx.b, ctx.t
    import bass_rust
    S = f"F{call}_"
    resid = t["resid"]
    dr = ctx.dram
    mslab, yslab = dr["mslab"], dr["yslab"]
    mkey, ykey = "mslab" + S, "yslab" + S
    b.inherit(mkey, ["mslab", "mslabF0_", "yslabF0_"])
    b.inherit(ykey, ["yslab", "mslabF0_", "yslabF0_"])
    xlt = t["xlt"]
    flgT, ftp, fz0 = t["flgT"], t["ftp"], t["fz0"]
    fps1, fps2, fps3, fps4 = t["fps1"], t["fps2"], t["fps3"], t["fps4"]
    b.emit("vector", lambda e: e.memset(flgT[:, :], 0.0), writes=["flgT"])
    b.emit("vector", lambda e: e.memset(ftp[:, :], 0.0), writes=["ftp"])
    b.emit("vector", lambda e: e.memset(t["fonesf"][:, :], 1.0), writes=["fonesf"])
    b.emit("vector", lambda e: e.memset(t["frun"][:, :], 0.0), writes=["frun"])
    b.emit("vector", lambda e: e.memset(t["fones8"][:, :], 1.0), writes=["fones8"])
    b.emit("vector", lambda e: e.memset(t["fzero1"][:, :], 0.0), writes=["fzero1"])
    if call == 0:
        b.emit("vector", lambda e: e.memset(fz0[:, :], 0.0), writes=["fz0"])
        for ee in range(NEXP):
            emit_dyn_dma(ctx, ST_ZF + ee, mslab,
                         [[1024, 128], [131072, 6], [1, 1024]],
                         fz0[:, None, :].broadcast_to((128, 6, D)), True,
                         reads=["fz0"], writes=[mkey])
            emit_dyn_dma(ctx, ST_ZF + ee, yslab,
                         [[1024, 128], [131072, 10], [1, 1024]],
                         fz0[:, None, :].broadcast_to((128, 10, D)), True,
                         reads=["fz0"], writes=[ykey])
        emit_dyn_dma(ctx, ST_YTRASH, yslab, [[1024, 1], [1, 1024]],
                     fz0[0:1, :], True, reads=["fz0"], writes=[ykey])
    for s in range(2):
        b.emit("vector", lambda e, h=t[f"ftpb{s}"]: e.memset(h[:, :], 0.0), writes=[f"ftpb{s}"])
        b.emit("vector", lambda e, h=t[f"flgs{s}"]: e.memset(h[:, :], 0.0), writes=[f"flgs{s}"])
    b.emit("vector", lambda e, h=t["jrfm"]: e.memset(h[:, :], 0.0), writes=["jrfm"])
    indfm, jrfm = t["indfm"], t["jrfm"]
    for ti in range(NT):
        s = ti % 2
        def sx(name, s=s):
            return t[f"{name}{s}"], f"{name}{s}"
        xls, xlk = sx("fxlt")
        fxs, fxk = sx("fxfs")
        lgs, lgk = sx("flgs")
        tpa, tpak = sx("ftpa")
        tpb, tpbk = sx("ftpb")
        p8, p8k = sx("fp8s")
        m1, m1k = sx("fm1s")
        m2s, m2k = sx("fm2s")
        ohs, ohk = sx("fohs")
        fin, fink = sx("finds")
        finb, finbk = sx("findbs")
        scr, scrk = sx("fscrs")
        sel, selk = sx("fsels")
        psl, pslk = (fps1, "fps1") if s == 0 else (fps2, "fps2")
        emit_ln_tile(ctx, resid[:, ti, :], "resid", xls[:, :], xlk,
                     t["lmean"][:, ti:ti + 1], t["lrstd"][:, ti:ti + 1],
                     mk=f"lmean{s}", rk=f"lrstd{s}",
                     bneg_ap=t["lbb2"][:, ti:ti + 1], bneg_key=f"lbb2{s}",
                     sfx=f"_{s}", sq_ap=xls[:, :], sq_key=xlk)
        emit_tm2fm(ctx, xls, xlk, lambda c, fxs=fxs: fxs[:, c, :], fxk, eng="sync")
        for cc in range(NCHUNK):
            b.emit("tensor", lambda e, cc=cc, psl=psl, fxs=fxs: e.matmul(
                psl[0:8, 0:128], t["frwT"][:, cc, :], fxs[:, cc, :],
                start=(cc == 0), stop=(cc == NCHUNK - 1)),
                reads=["frwT", fxk], writes=[pslk], group=S + f"lg{ti}",
                group_end=(cc == NCHUNK - 1))
        b.emit("vector", lambda e, lgs=lgs, psl=psl: e.tensor_copy(lgs[0:8, :], psl[0:8, 0:128]),
               reads=[pslk], writes=[lgk])
        emit_tr_32x128(ctx, tpa, tpak, lgs, lgk)
        b.emit("vector", lambda e, p8=p8, tpa=tpa: e.tensor_add(p8[:, :], tpa[:, 0:8], t["frb"][:, :]),
               reads=[tpak, "frb"], writes=[p8k])
        b.emit("vector", lambda e, p8=p8, m1=m1: e.reduce_max(m1[:, :], p8[:, :], axis=AX.X, negate=True),
               reads=[p8k], writes=[m1k])
        b.emit("scalar", lambda e, p8=p8, m1=m1: e.activation(out=p8[:, :], in_=p8[:, :], func=AF.Exp, bias=m1[:, :]),
               reads=[p8k, m1k], writes=[p8k])
        b.emit("vector", lambda e, p8=p8, m1=m1: e.reduce_sum(m1[:, :], p8[:, :], axis=AX.X),
               reads=[p8k], writes=[m1k])
        b.emit("vector", lambda e, m1=m1: e.reciprocal(m1[:, :], m1[:, :]),
               reads=[m1k], writes=[m1k])
        b.emit("vector", lambda e, p8=p8, m1=m1: e.tensor_scalar(p8[:, :], p8[:, :], m1[:, :], None, ALU.mult),
               reads=[p8k, m1k], writes=[p8k])
        b.emit("vector", lambda e, p8=p8, m1=m1: e.reduce_max(m1[:, :], p8[:, :], axis=AX.X),
               reads=[p8k], writes=[m1k])
        b.emit("vector", lambda e, p8=p8, m1=m1, ohs=ohs: e.tensor_scalar(ohs[:, :], p8[:, :], m1[:, :], None, ALU.is_ge),
               reads=[p8k, m1k], writes=[ohk])
        b.emit("vector", lambda e, p8=p8, ohs=ohs, scr=scr: e.scalar_tensor_tensor(scr[:, :], ohs[:, :], -2.0, p8[:, :], ALU.mult, ALU.add),
               reads=[ohk, p8k], writes=[scrk])
        b.emit("vector", lambda e, scr=scr, m2s=m2s: e.reduce_max(m2s[:, :], scr[:, :], axis=AX.X),
               reads=[scrk], writes=[m2k])
        b.emit("vector", lambda e, p8=p8, m2s=m2s, fin=fin: e.tensor_scalar(fin[:, :], p8[:, :], m2s[:, :], None, ALU.is_ge),
               reads=[p8k, m2k], writes=[fink])
        b.emit("scalar", lambda e, fin=fin, finb=finb: e.activation(out=finb[:, :], in_=fin[:, :], func=AF.Copy),
               reads=[fink], writes=[finbk])
        b.emit("vector", lambda e, p8=p8, fin=fin, scr=scr: e.tensor_mul(scr[:, :], p8[:, :], fin[:, :]),
               reads=[p8k, fink], writes=[scrk])
        b.emit("vector", lambda e, ti=ti, scr=scr: e.tensor_copy(t["fwn"][:, ti:ti + 1], scr[:, 7:8]),
               reads=[scrk], writes=["fwn"])
        emit_select(ctx, scr[:, 0:7], scrk, fin[:, 0:7], [fink],
                    t["fiota"][:, 0:7], ["fiota"], t["f99"][:, 0:7], ["f99"], n=7, scr=sel, scrk=selk)
        b.emit("vector", lambda e, ti=ti, scr=scr: e.tensor_reduce(t["fe1"][:, ti:ti + 1], scr[:, 0:7], AX.X, ALU.min),
               reads=[scrk], writes=["fe1"])
        b.emit("vector", lambda e, ti=ti, ohs=ohs: e.tensor_scalar(ohs[:, 0:7], t["fiota"][:, 0:7], t["fe1"][:, ti:ti + 1], None, ALU.is_equal),
               reads=["fiota", "fe1"], writes=[ohk])
        b.emit("vector", lambda e, scr=scr, sel=sel: e.tensor_scalar(sel[:, 0:7], scr[:, 0:7], -99.0, None, ALU.add),
               reads=[scrk], writes=[selk])
        b.emit("vector", lambda e, sel=sel, ohs=ohs: e.tensor_mul(sel[:, 0:7], sel[:, 0:7], ohs[:, 0:7]),
               reads=[selk, ohk], writes=[selk])
        b.emit("vector", lambda e, scr=scr, sel=sel: e.tensor_sub(scr[:, 0:7], scr[:, 0:7], sel[:, 0:7]),
               reads=[scrk, selk], writes=[scrk])
        b.emit("vector", lambda e, ti=ti, scr=scr: e.tensor_reduce(t["fe2"][:, ti:ti + 1], scr[:, 0:7], AX.X, ALU.min),
               reads=[scrk], writes=["fe2"])
        b.emit("vector", lambda e, ohs=ohs, p8=p8, scr=scr: e.tensor_mul(scr[:, 0:7], ohs[:, 0:7], p8[:, 0:7]),
               reads=[ohk, p8k], writes=[scrk])
        b.emit("vector", lambda e, ti=ti, scr=scr: e.tensor_reduce(t["fw1"][:, ti:ti + 1], scr[:, 0:7], AX.X, ALU.add),
               reads=[scrk], writes=["fw1"])
        b.emit("vector", lambda e, ti=ti, scr=scr: e.tensor_scalar(scr[:, 0:7], t["fiota"][:, 0:7], t["fe2"][:, ti:ti + 1], None, ALU.is_equal),
               reads=["fiota", "fe2"], writes=[scrk])
        b.emit("vector", lambda e, scr=scr, p8=p8: e.tensor_mul(scr[:, 0:7], scr[:, 0:7], p8[:, 0:7]),
               reads=[scrk, p8k], writes=[scrk])
        b.emit("vector", lambda e, ti=ti, scr=scr: e.tensor_reduce(t["fw2"][:, ti:ti + 1], scr[:, 0:7], AX.X, ALU.add),
               reads=[scrk], writes=["fw2"])
        b.emit("vector", lambda e, tpb=tpb, finb=finb: e.tensor_copy(tpb[:, 0:8], finb[:, :]),
               reads=[finbk], writes=[tpbk])
        emit_tr_128x32(ctx, indfm[:, ti * 128:(ti + 1) * 128], "indfm", tpb, tpbk)
    b.emit("vector", lambda e: e.memset(t["frun"][:, :], 0.0), writes=["frun"])
    b.emit("vector", lambda e: e.tensor_tensor_scan(
        jrfm[0:8, :], t["fones8"][:, 0:1].broadcast_to((8, 4096)), indfm[0:8, :],
        t["frun"][:, 0:1], ALU.mult, ALU.add),
        reads=["fones8", "indfm", "frun"], writes=["jrfm"])
    b.emit("vector", lambda e: e.tensor_copy(t["fcnt"][:, 0:1], jrfm[0:8, 2047:2048]),
           reads=["jrfm"], writes=["fcnt"])
    b.emit("vector", lambda e: e.tensor_copy(t["fcnt"][:, 1:2], jrfm[0:8, 4095:4096]),
           reads=["jrfm"], writes=["fcnt"])
    b.emit("vector", lambda e: e.tensor_sub(t["fcnt"][:, 1:2], t["fcnt"][:, 1:2], t["fcnt"][:, 0:1]),
           reads=["fcnt"], writes=["fcnt"])
    b.emit("vector", lambda e: e.tensor_sub(jrfm[0:8, :], jrfm[0:8, :], indfm[0:8, :]),
           reads=["jrfm", "indfm"], writes=["jrfm"])
    for ti in range(NT):
        s = ti % 2
        def sx(name, s=s):
            return t[f"{name}{s}"], f"{name}{s}"
        tpc, tpck = sx("ftpc")
        ohs, ohk = sx("fohs")
        scr, scrk = sx("fscrs")
        sc1, sc1k = sx("fsc1c")
        b.emit("scalar", lambda e, ti=ti, s=s: e.activation(
            out=xlt[:, :], in_=resid[:, ti, :], func=AF.Identity,
            scale=t["lrstd"][:, ti:ti + 1], bias=t["lbb2"][:, ti:ti + 1]),
            reads=["resid", f"lrstd{s}", f"lbb2{s}"], writes=["xlt"])
        emit_tr_32x128(ctx, tpc, tpck, jrfm[:, ti * 128:(ti + 1) * 128], "jrfm")
        for (esel, jrn, gxn, gidxn) in (("fe1", "fjr1", "fgx1", "fgidx1"),
                                        ("fe2", "fjr2", "fgx2", "fgidx2")):
            gix, gixk = sx(gidxn)
            b.emit("vector", lambda e, ti=ti, esel=esel, ohs=ohs: e.tensor_scalar(
                ohs[:, :], t["fiota"][:, :], t[esel][:, ti:ti + 1], None, ALU.is_equal),
                reads=["fiota", esel], writes=[ohk])
            b.emit("vector", lambda e, ohs=ohs, scr=scr, tpc=tpc: e.tensor_mul(scr[:, :], ohs[:, :], tpc[:, 0:8]),
                   reads=[ohk, tpck], writes=[scrk])
            b.emit("vector", lambda e, ti=ti, jrn=jrn, scr=scr: e.tensor_reduce(
                t[jrn][:, ti:ti + 1], scr[:, :], AX.X, ALU.add),
                reads=[scrk], writes=[jrn])
            b.emit("vector", lambda e, ti=ti, esel=esel, jrn=jrn, sc1=sc1: e.scalar_tensor_tensor(
                sc1[:, :], t[esel][:, ti:ti + 1], 2.0 * HALF_PAD, t[jrn][:, ti:ti + 1],
                ALU.mult, ALU.add), reads=[esel, jrn], writes=[sc1k])
            b.emit("vector", lambda e, sc1=sc1: e.tensor_add(sc1[:, :], sc1[:, :], t["fhoff"][:, :]),
                   reads=[sc1k, "fhoff"], writes=[sc1k])
            b.emit("vector", lambda e, sc1=sc1: e.tensor_tensor(sc1[:, :], sc1[:, :], t["ftrash"][:, :], ALU.min),
                   reads=[sc1k, "ftrash"], writes=[sc1k])
            b.emit("vector", lambda e, ti=ti, gxn=gxn, sc1=sc1: e.tensor_copy(t[gxn][:, ti:ti + 1], sc1[:, :]),
                   reads=[sc1k], writes=[gxn])
            b.emit("vector", lambda e, gix=gix, sc1=sc1: e.tensor_copy(gix[:, :], sc1[:, :]),
                   reads=[sc1k], writes=[gixk])
            b.emit("gpsimd", lambda e, gix=gix: e.indirect_dma_start(
                mslab[:, :], IndirectOffsetOnAxis(gix[:, 0:1], 0), xlt[:, :], None),
                reads=[gixk, "xlt"], writes=[mkey], dma=True)
    for qq in range(4):
        b.emit("gpsimd", lambda e, qq=qq: e.dma_start(
            out=dr["respill"][qq * 8:(qq + 1) * 8].rearrange("T p f -> p T f"),
            in_=resid[:, qq * 8:(qq + 1) * 8, :]),
            reads=["resid"], writes=["respill"], dma=True)
    b.emit("gpsimd", lambda e: e.dma_start(out=dr["agin"][:, :], in_=t["fcnt"][:, :]),
           reads=["fcnt"], writes=["agin"], dma=True)
    b.emit("gpsimd", lambda e, ncores=ncores: e.collective_compute(
        "AllGather", ALU.bypass, replica_groups=[list(range(ncores))],
        ins=[dr["agin"][:, :]], outs=[dr["agout"][:, :]]),
        reads=["agin", mkey], writes=["agout"], tag="AG")
    b.emit("gpsimd", lambda e: e.dma_start(
        out=t["fagt"][:, :].rearrange("e (c r) -> e c r", r=2),
        in_=dr["agout"].rearrange("(c e) r -> e c r", e=8)),
        reads=["agout"], writes=["fagt"], dma=True)
    for row in range(2):
        b.emit("vector", lambda e, row=row, ncores=ncores: e.tensor_mul(
            t["fscr8w"][:, 0:2 * ncores], t["fagt"][:, :],
            t["fmsk"][:, row * 2 * ncores:(row + 1) * 2 * ncores]),
            reads=["fagt", "fmsk"], writes=["fscr8w"])
        b.emit("vector", lambda e, row=row, ncores=ncores: e.tensor_reduce(
            t["foff"][:, row:row + 1], t["fscr8w"][:, 0:2 * ncores], AX.X, ALU.add),
            reads=["fscr8w"], writes=["foff"])
    for row in range(2):
        b.emit("vector", lambda e: e.memset(flgT[0:32, 0:32], 0.0), writes=["flgT"])
        b.emit("vector", lambda e, row=row: e.tensor_copy(flgT[0:8, 0:1], t["foff"][:, row:row + 1]),
               reads=["foff"], writes=["flgT"])
        b.emit("vector", lambda e: e.transpose(ftp[0:32, 0:32], flgT[0:32, 0:32]),
               reads=["flgT"], writes=["ftp"])
        b.emit("tensor", lambda e: e.matmul(
            fps1[0:128, 0:8], t["fonesf"][0:1, :], ftp[0:1, 0:8],
            start=True, stop=True),
            reads=["fonesf", "ftp"], writes=["fps1"], group=S + f"offb{row}", group_end=True)
        b.emit("vector", lambda e, row=row: e.tensor_copy(t["foffb"][:, row, :], fps1[0:128, 0:8]),
               reads=["fps1"], writes=["foffb"])
    ekeys = EXPERT_KEYS + COMBINE_KEYS
    for k in EXPERT_KEYS:
        base = ["resid", "xfm"] if k in ("fwi", "fwo") else ["resid"]
        b.inherit(k + S, base + ([k + "F0_"] if call == 1 else []))
    for par in range(2):
        b.inherit("fwi" + S + f"_{par}", ["xfm", "cvt32", "cvt16"])
        b.inherit("fwo" + S + f"_{par}", ["xfm", "cvt32", "cvt16"])
    hk, yk, wik, wok, xgt6k, ytm6k, zk, _, _ = [k + S for k in EXPERT_KEYS]
    fhfm, fyfm, fwi, fwo, fxgt6, fytm6, fzero = [t[k] for k in EXPERT_KEYS[:7]]
    b.emit("vector", lambda e: e.memset(fzero[:, :], 0.0), writes=[zk])
    b.emit("vector", lambda e: e.memset(fzero[:, :], 0.0), writes=[zk])
    emit_dyn_dma(ctx, ST_ZROW, yslab, [[1024, 1], [1, 1024]],
                 fzero[0:1, :], True, reads=[zk], writes=[ykey], eng="scalar")
    done_q = ctx.t.setdefault("_doneq", set())

    def ld_wi(jj, m):
        par = m % 2
        wkm = wik + f"_{par}"
        qq = QSEQ[jj]
        b.emit("sync", lambda e, m=m, par=par, qq=qq: e.dma_start(
            out=fwi[:, :, par * 512:(par + 1) * 512],
            in_=dr["wscr_i"][qq, :, :, m * 512:(m + 1) * 512].rearrange("dc p i -> p dc i")),
            reads=["wscr_i"], writes=[wkm], dma=True)

    def gather_xg(jj):
        emit_dyn_dma(ctx, ST_SUB + jj, mslab,
                     [[1024, 128], [131072, 6], [1, 1024]],
                     fxgt6[:, :, :], False, reads=[mkey], writes=[xgt6k])

    def xg_tr(jj):
        fxg_t, xgk_t = (t["fxg0"], "fxg0" + S) if jj % 2 == 0 else (t["fxg1"], "fxg1" + S)
        for kk in range(6):
            for cc in range(NCHUNK):
                b.emit("sync", lambda e, kk=kk, cc=cc, fxg_t=fxg_t: e.dma_start(
                    out=fxg_t[:, cc, kk * 128:(kk + 1) * 128],
                    in_=fxgt6[:, kk, cc * 128:(cc + 1) * 128], transpose=True),
                    reads=[xgt6k], writes=[xgk_t], dma=True)

    for j in range(NEXP):
        q = QSEQ[j]
        if call == 0 and j == 0 and q not in done_q:
            emit_ffn_convert_q(ctx, q)
            done_q.add(q)
        fxg_j, xgk_j = (t["fxg0"], "fxg0" + S) if j % 2 == 0 else (t["fxg1"], "fxg1" + S)
        if j == 0:
            gather_xg(0)
            xg_tr(0)
            ld_wi(0, 0)
            ld_wi(0, 1)
        for m in range(8):
            par = m % 2
            wkm = wik + f"_{par}"
            if m >= 2:
                ld_wi(j, m)
            for icl in range(4):
                ic = m * 4 + icl
                pk_a, pk_b = (("fps1", "fps2") if ic % 2 == 0 else ("fps3", "fps4"))
                for (pk, lo, w) in ((pk_a, 0, 512), (pk_b, 512, 256)):
                    ps = t[pk]
                    for dc in range(NCHUNK):
                        b.emit("tensor", lambda e, ps=ps, par=par, icl=icl, dc=dc, lo=lo, w=w, fxg_j=fxg_j: e.matmul(
                            ps[:, 0:w], fwi[:, dc, par * 512 + icl * 128:par * 512 + (icl + 1) * 128],
                            fxg_j[:, dc, lo:lo + w],
                            start=(dc == 0), stop=(dc == NCHUNK - 1)),
                            reads=[wkm, xgk_j], writes=[pk], group=S + f"h{j}_{ic}_{lo}",
                            group_end=(dc == NCHUNK - 1))
                b.emit("scalar", lambda e, ic=ic, pa=t[pk_a]: e.activation(
                    out=fhfm[:, ic, 0:512], in_=pa[:, 0:512], func=AF.Relu),
                    reads=[pk_a], writes=[hk])
                b.emit("scalar", lambda e, ic=ic, pb=t[pk_b]: e.activation(
                    out=fhfm[:, ic, 512:768], in_=pb[:, 0:256], func=AF.Relu),
                    reads=[pk_b], writes=[hk])
        for oc in range(8):
            par = oc % 2
            wkm = wok + f"_{par}"
            b.emit("sync", lambda e, oc=oc, par=par, q=q: e.dma_start(
                out=fwo[:, :, par * 128:(par + 1) * 128],
                in_=dr["wscr_o"][q, :, :, oc * 128:(oc + 1) * 128].rearrange("ic p o -> p ic o")),
                reads=["wscr_o"], writes=[wkm], dma=True)
            pk_a, pk_b = (("fps3", "fps4") if oc % 2 == 0 else ("fps1", "fps2"))
            for (pk, lo, w) in ((pk_a, 0, 512), (pk_b, 512, 256)):
                ps = t[pk]
                for ic in range(32):
                    b.emit("tensor", lambda e, ps=ps, par=par, ic=ic, lo=lo, w=w: e.matmul(
                        ps[:, 0:w], fwo[:, ic, par * 128:(par + 1) * 128],
                        fhfm[:, ic, lo:lo + w],
                        start=(ic == 0), stop=(ic == 31)),
                        reads=[wkm, hk], writes=[pk], group=S + f"y{j}_{oc}_{lo}",
                        group_end=(ic == 31))
            b.emit("scalar", lambda e, oc=oc, pa=t[pk_a]: e.activation(
                out=fyfm[:, oc, 0:512], in_=pa[:, 0:512], func=AF.Copy),
                reads=[pk_a], writes=[yk])
            b.emit("scalar", lambda e, oc=oc, pb=t[pk_b]: e.activation(
                out=fyfm[:, oc, 512:768], in_=pb[:, 0:256], func=AF.Copy),
                reads=[pk_b], writes=[yk])
        if j + 1 < NEXP:
            qn = QSEQ[j + 1]
            if call == 0 and qn not in done_q:
                emit_ffn_convert_q(ctx, qn)
                done_q.add(qn)
            gather_xg(j + 1)
            xg_tr(j + 1)
            ld_wi(j + 1, 0)
            ld_wi(j + 1, 1)
        for kk in range(6):
            for cc in range(NCHUNK):
                b.emit("sync", lambda e, kk=kk, cc=cc: e.dma_start(
                    out=fytm6[:, kk, cc * 128:(cc + 1) * 128],
                    in_=fyfm[:, cc, kk * 128:(kk + 1) * 128], transpose=True),
                    reads=[yk], writes=[ytm6k], dma=True)
        emit_dyn_dma(ctx, ST_SUB + j, yslab,
                     [[1024, 128], [131072, 6], [1, 1024]],
                     fytm6[:, :, :], True, reads=[ytm6k], writes=[ykey])
    groups = [[2 * p, 2 * p + 1] for p in range(ncores // 2)]
    b.emit("gpsimd", lambda e, groups=groups: e.collective_compute(
        "AllReduce", ALU.add, replica_groups=groups,
        ins=[dr["barin"][:, :]], outs=[dr["barout"][:, :]]),
        reads=[ykey, "barin"], writes=["ybar" + S], tag="BAR")
    b.inherit("resid", [mkey] + [k + S for k in ekeys])
    for sname in ("frt0", "frt1", "fg1b0", "fg1b1", "fg2b0", "fg2b1"):
        b.inherit(sname + S, [wik + "_0", wik + "_1", "cvt32", "cvt16"])
    for sname in ("fdelta0", "fdelta1", "fxltd0", "fxltd1"):
        b.inherit(sname + S, [wok + "_0", wok + "_1", "cvt32", "cvt16"])
    for ti in range(NT):
        row = ti // 16
        s = ti % 2
        def sx(name, s=s):
            return t[f"{name}{s}"], f"{name}{s}"
        frt_s, rtk_s = t[f"frt{s}"], f"frt{s}" + S
        xld, xldk = t[f"fxltd{s}"], f"fxltd{s}" + S
        fdl, dk_s = t[f"fdelta{s}"], f"fdelta{s}" + S
        ohs, ohk = sx("fohs")
        scr, scrk = sx("fscrs")
        sc1, sc1k = sx("fsc1c")
        kpt, kptk = sx("fkeptc")
        sc2, sc2k = sx("fscr2c")
        sel, selk = sx("fsels")
        b.emit("sync", lambda e, ti=ti, frt_s=frt_s: e.dma_start(
            out=frt_s[:, :], in_=dr["respill"][ti]),
            reads=["respill"], writes=[rtk_s], dma=True)
        b.emit("scalar", lambda e, ti=ti, s=s, frt_s=frt_s, xld=xld: e.activation(
            out=xld[:, :], in_=frt_s[:, :], func=AF.Identity,
            scale=t["lrstd"][:, ti:ti + 1], bias=t["lbb2"][:, ti:ti + 1]),
            reads=[rtk_s, f"lrstd{s}", f"lbb2{s}"], writes=[xldk])
        for (esel, jrn, gxn, gi) in (("fe1", "fjr1", "fgx1", 1),
                                     ("fe2", "fjr2", "fgx2", 2)):
            gix, gixk = sx(f"fgidx{gi}")
            gbuf, gbk = t[f"fg{gi}b{s}"], f"fg{gi}b{s}" + S
            b.emit("vector", lambda e, ti=ti, esel=esel, ohs=ohs: e.tensor_scalar(
                ohs[:, :], t["fiota"][:, :], t[esel][:, ti:ti + 1], None, ALU.is_equal),
                reads=["fiota", esel], writes=[ohk])
            b.emit("vector", lambda e, row=row, ohs=ohs, scr=scr: e.tensor_mul(
                scr[:, :], ohs[:, :], t["foffb"][:, row, :]),
                reads=[ohk, "foffb"], writes=[scrk])
            b.emit("vector", lambda e, scr=scr, sc1=sc1: e.tensor_reduce(sc1[:, :], scr[:, :], AX.X, ALU.add),
                   reads=[scrk], writes=[sc1k])
            b.emit("vector", lambda e, ti=ti, jrn=jrn, sc1=sc1: e.tensor_add(sc1[:, :], sc1[:, :], t[jrn][:, ti:ti + 1]),
                   reads=[sc1k, jrn], writes=[sc1k])
            b.emit("vector", lambda e, cap=cap, sc1=sc1, kpt=kpt: e.tensor_single_scalar(
                kpt[:, :], sc1[:, :], float(cap), ALU.is_lt),
                reads=[sc1k], writes=[kptk])
            b.emit("vector", lambda e, ti=ti, esel=esel, sc2=sc2: e.tensor_single_scalar(
                sc2[:, :], t[esel][:, ti:ti + 1], 50.0, ALU.is_lt),
                reads=[esel], writes=[sc2k])
            b.emit("vector", lambda e, kpt=kpt, sc2=sc2: e.tensor_mul(kpt[:, :], kpt[:, :], sc2[:, :]),
                   reads=[kptk, sc2k], writes=[kptk])
            emit_select(ctx, sc1[:, :], sc1k, kpt[:, :], [kptk],
                        t[gxn][:, ti:ti + 1], [gxn], t["fzrowc"][:, :], ["fzrowc"], n=1,
                        scr=sel, scrk=selk)
            b.emit("vector", lambda e, gix=gix, sc1=sc1: e.tensor_copy(gix[:, :], sc1[:, :]),
                   reads=[sc1k], writes=[gixk])
            b.emit("gpsimd", lambda e, gix=gix, gbuf=gbuf: e.indirect_dma_start(
                gbuf[:, :], None, yslab[:, :], IndirectOffsetOnAxis(gix[:, 0:1], 0)),
                reads=[gixk, ykey, "ybar" + S], writes=[gbk], dma=True)
        g1b, g2b = t[f"fg1b{s}"], t[f"fg2b{s}"]
        g1bk, g2bk = f"fg1b{s}" + S, f"fg2b{s}" + S
        b.emit("scalar", lambda e, ti=ti, fdl=fdl, g1b=g1b: e.activation(
            out=fdl[:, :], in_=g1b[:, :], func=AF.Identity,
            scale=t["fw1"][:, ti:ti + 1], bias=t["fzero1"][:, 0:1]),
            reads=[g1bk, "fw1", "fzero1"], writes=[dk_s])
        b.emit("vector", lambda e, ti=ti, fdl=fdl, g2b=g2b: e.scalar_tensor_tensor(
            fdl[:, :], g2b[:, :], t["fw2"][:, ti:ti + 1], fdl[:, :], ALU.mult, ALU.add),
            reads=[g2bk, "fw2", dk_s], writes=[dk_s])
        b.emit("vector", lambda e, ti=ti, fdl=fdl, xld=xld: e.scalar_tensor_tensor(
            fdl[:, :], xld[:, :], t["fwn"][:, ti:ti + 1], fdl[:, :], ALU.mult, ALU.add),
            reads=[xldk, "fwn", dk_s], writes=[dk_s])
        b.emit("vector", lambda e, ti=ti, fdl=fdl, frt_s=frt_s: e.tensor_add(
            resid[:, ti, :], frt_s[:, :], fdl[:, :]),
            reads=[rtk_s, dk_s], writes=["resid"])
        if out_dram is not None:
            b.emit("scalar", lambda e, ti=ti: e.dma_start(
                out=out_dram[ti], in_=resid[:, ti, :]),
                reads=["resid"], writes=["outd"], dma=True)


def emit_ffn_consts(ctx):
    b, t, dr = ctx.b, ctx.t, ctx.dram
    b.emit("gpsimd", lambda e: e.dma_start(
        out=t["frwT"][:, :, :], in_=dr["rwT_d"].rearrange("(c p) e -> p c e", p=128)),
        reads=["rwT_d"], writes=["frwT"], dma=True)
    for name, dst in (("rb_bc", "frb"), ("iota_d", "fiota"), ("c99_d", "f99"),
                      ("hoff_d", "fhoff"), ("trash_d", "ftrash"), ("zrowc_d", "fzrowc"),
                      ("fmsk_d", "fmsk")):
        b.emit("gpsimd", lambda e, name=name, dst=dst: e.dma_start(
            out=t[dst][:], in_=dr[name][:]), reads=[name], writes=[dst], dma=True)
    b.emit("gpsimd", lambda e: e.dma_start(out=t["stab"][:, :], in_=dr["stab_d"][:, :]),
           reads=["stab_d"], writes=["stab"], dma=True)
    b.emit("gpsimd", lambda e: e.dma_start(out=dr["barin"][0:1, 0:8], in_=t["frb"][0:1, 0:8]),
           reads=["frb"], writes=["barin"], dma=True)


TRASH0 = NEXP * 2 * HALF_PAD
"""Host-side input prep for the FFN (and full kernel): per-core tables/consts.
Layout-only numpy (transpose/reshape/broadcast/constants) -- no arithmetic on
input VALUES."""

TRASH0 = NEXP * 2 * HALF_PAD


def core_rows(c, ncores):
    if ncores == 8:
        P, h = divmod(c, 2)
        return [P + 4 * h, P + 4 * h + 8]
    elif ncores == 2:
        return [c, c + 2]
    raise ValueError(ncores)


def ffn_core_inputs(core, ncores, router_w, router_b):
    half = core % 2
    nb = 2 * ncores
    rows = core_rows(core, ncores)
    d = {}
    d["rwT_d"] = np.ascontiguousarray(router_w.T)
    d["rb_bc"] = np.ascontiguousarray(np.broadcast_to(router_b[None, :], (128, 8)))
    d["iota_d"] = np.ascontiguousarray(np.broadcast_to(np.arange(8, dtype=np.float32)[None, :], (128, 8)))
    d["c99_d"] = np.full((128, 8), 99.0, np.float32)
    d["hoff_d"] = np.full((128, 1), float(half * HALF_PAD), np.float32)
    d["trash_d"] = np.full((128, 1), float(TRASH0 + half), np.float32)
    d["zrowc_d"] = np.full((128, 1), float(TRASH0 + 2 + half), np.float32)
    msk = np.zeros((2, ncores, 2), np.float32)
    for row in range(2):
        myb = rows[row]
        for c2 in range(ncores):
            r2rows = core_rows(c2, ncores)
            for r2 in range(2):
                if c2 != core and r2rows[r2] < myb:
                    msk[row, c2, r2] = 1.0
    d["fmsk_d"] = np.ascontiguousarray(
        np.broadcast_to(msk.reshape(1, -1), (8, 2 * ncores * 2)))
    stab = np.zeros(256, np.int32)
    conv = [0, 1, 2, 3] if half == 0 else [4, 5, 6, 3]
    for q, e in enumerate(conv):
        for dc in range(8):
            stab[q * 8 + dc] = e * 4194304 + dc * 524288
        for icg in range(8):
            stab[64 + q * 8 + icg] = e * 4194304 + icg * 524288
    SUB_E0 = [0, 1, 2, 0, 1, 2, 3]
    SUB_HP = [0, 0, 0, 1, 1, 1, half]
    for j in range(7):
        e = SUB_E0[j] + (4 * half if j < 6 else 0)
        hp = SUB_HP[j]
        stab[192 + j] = (e * 2 * HALF_PAD + hp * HALF_PAD) * 1024
    stab[199] = (TRASH0 + 2 + half) * 1024
    for ee in range(7):
        stab[200 + ee] = (ee * 2 * HALF_PAD + half * HALF_PAD) * 1024
    stab[207] = (TRASH0 + half) * 1024
    d["stab_d"] = stab.reshape(1, 256)
    return d




from concourse.bass_utils import run_bass_kernel_spmd

NCORES = 8
CAP = 4096


def _build():
    nc = bass.Bass("TRN2", target_bir_lowering=False)
    b = B(nc)
    ctx = Ctx(nc, b)
    dr = ctx.dram
    x = nc.dram_tensor("x", [2, 2048, 1024], F32, kind="ExternalInput").ap()
    y = nc.dram_tensor("y", [2, 2048, 1024], F32, kind="ExternalOutput").ap()
    dr["wqkvT"] = nc.dram_tensor("wqkvT", [3, 4, 1024, 128], F32, kind="ExternalInput").ap()
    for name, shape in [("woutTd", [512, 1024]), ("boutd", [128, 8]), ("masksd", [128, 4, 128])]:
        dr[name] = nc.dram_tensor(name, shape, F32, kind="ExternalInput").ap()
    for name, shape in [("bq_fm", [128, 8]), ("bk_fm", [128, 8]), ("bnw_fm", [128, 8]),
                        ("bld_d", [16, 1]), ("bpm_d", [16, 8]), ("bhsel_d", [128, 2]),
                        ("bE1_d", [16, 128]), ("bE2_d", [2, 128]),
                        ("bowT_d", [1024, 1024]), ("onesf_d", [1, 128])]:
        dr[name] = nc.dram_tensor(name, shape, F32, kind="ExternalInput").ap()
    dr["ones_d"] = nc.dram_tensor("ones_d", [128, 1], BF16, kind="ExternalInput").ap()
    dr["expw_i"] = nc.dram_tensor("expw_i", [7, 1024, 4096], F32, kind="ExternalInput").ap()
    dr["expw_o"] = nc.dram_tensor("expw_o", [7, 4096, 1024], F32, kind="ExternalInput").ap()
    dr["rwT_d"] = nc.dram_tensor("rwT_d", [1024, 8], F32, kind="ExternalInput").ap()
    for name, shape, dt in [("rb_bc", [128, 8], F32), ("iota_d", [128, 8], F32),
                            ("c99_d", [128, 8], F32), ("hoff_d", [128, 1], F32),
                            ("trash_d", [128, 1], F32), ("zrowc_d", [128, 1], F32),
                            ("fmsk_d", [8, 2 * NCORES * 2], F32), ("stab_d", [1, 256], I32)]:
        dr[name] = nc.dram_tensor(name, shape, dt, kind="ExternalInput").ap()
    dr["mslab"] = nc.dram_tensor("mslab", [MROWS, D], BF16, addr_space="Shared").ap()
    dr["yslab"] = nc.dram_tensor("yslab", [MROWS, D], BF16, addr_space="Shared").ap()
    dr["wscr_i"] = nc.dram_tensor("wscr_i", [4, 8, 128, 4096], BF16).ap()
    dr["wscr_o"] = nc.dram_tensor("wscr_o", [4, 32, 128, 1024], BF16).ap()
    dr["respill"] = nc.dram_tensor("respill", [NT, 128, D], F32).ap()
    dr["agin"] = nc.dram_tensor("agin", [8, 2], F32).ap()
    dr["agout"] = nc.dram_tensor("agout", [NCORES * 8, 2], F32).ap()
    dr["barin"] = nc.dram_tensor("barin", [1, 8], F32).ap()
    dr["barout"] = nc.dram_tensor("barout", [1, 8], F32).ap()

    alloc_common(ctx)
    alloc_ffn(ctx, NCORES)
    alloc_ffn_expert(ctx)
    alloc_convert(ctx)
    t = ctx.t
    with ctx.stack:
        for r in range(2):
            b.emit("sync", lambda e, r=r: e.dma_start(
                out=t["resid"][:, 16 * r:16 * (r + 1), :],
                in_=x[r].rearrange("(T p) f -> p T f", p=128)),
                reads=["x"], writes=["resid"], dma=True)
        emit_ffn_consts(ctx)
        ctx.stage_begin()
        alloc_attn_a(ctx)
        b.emit("gpsimd", lambda e: e.dma_start(
            out=t["woutT"][:, :, :], in_=dr["woutTd"].rearrange("(h p) o -> p h o", p=128)),
            writes=["woutT"], dma=True)
        b.emit("gpsimd", lambda e: e.dma_start(out=t["bout"][:, :], in_=dr["boutd"][:, :]),
               writes=["bout"], dma=True)
        b.emit("gpsimd", lambda e: e.dma_start(out=t["amasks"][:, :, :], in_=dr["masksd"][:, :, :]),
               writes=["amasks"], dma=True)
        emit_attn_a(ctx)
        ctx.stage_end()
        ctx.stage_begin()
        alloc_ffn_stage(ctx)
        emit_ffn(ctx, 0, NCORES, CAP)
        ctx.stage_end()
        ctx.stage_begin()
        alloc_attn_b(ctx)
        emit_attn_b_tables(ctx)
        emit_attn_b(ctx)
        ctx.stage_end()
        ctx.stage_begin()
        alloc_ffn_stage(ctx)
        ytiles = [y[ti // 16].rearrange("(T p) f -> p T f", p=128)[:, ti % 16, :]
                  for ti in range(NT)]
        emit_ffn(ctx, 1, NCORES, CAP, out_dram=ytiles)
        ctx.stage_end()
        b.emit("gpsimd", None, reads=["outd"])
        b.finalize()
    return nc


def _host_tables():
    pm = np.zeros((16, 8), np.float32)
    for h in range(16):
        pm[h, h // 2] = 1
    hsel = np.zeros((128, 2), np.float32)
    for p in range(128):
        hsel[p, p // 64] = 1
    E1 = np.zeros((16, 128), np.float32)
    for h in range(16):
        for p in range(128):
            if h % 2 == p // 64:
                E1[h, p] = 1
    return pm, hsel, E1


def kernel(**inputs):
    f32 = lambda a: np.ascontiguousarray(np.asarray(a, dtype=np.float32))
    x = f32(inputs["L0_attn_a_x"])
    Wk = f32(inputs["L0_attn_a_Wk"])
    Wq = f32(inputs["L0_attn_a_Wq"])
    Wv = f32(inputs["L0_attn_a_Wv"])
    W_out = f32(inputs["L0_attn_a_W_out"])
    b_out = f32(inputs["L0_attn_a_b_out"])
    bq = f32(inputs["L0_attn_b_q"])
    bk = f32(inputs["L0_attn_b_k"])
    log_decay = f32(inputs["L0_attn_b_log_decay"])
    bnw = f32(inputs["L0_attn_b_norm_weight"])
    bow = f32(inputs["L0_attn_b_o_w"])
    router_w = f32(inputs["L0_ffn_router_w"])
    router_b = f32(inputs["L0_ffn_router_b"])
    wi = f32(inputs["L0_ffn_experts_inter"])
    wo = f32(inputs["L0_ffn_experts_out"])

    wqkvT = np.ascontiguousarray(np.stack([
        Wq.transpose(0, 2, 1), Wk.transpose(0, 2, 1), Wv.transpose(0, 2, 1)]))
    pm, hsel, E1 = _host_tables()
    common = {
        "wqkvT": wqkvT,
        "woutTd": np.ascontiguousarray(W_out.T),
        "boutd": np.ascontiguousarray(b_out.reshape(8, 128).T),
        "masksd": make_attn_a_masks(),
        "bq_fm": np.ascontiguousarray(bq.reshape(8, 128).T),
        "bk_fm": np.ascontiguousarray(bk.reshape(8, 128).T),
        "bnw_fm": np.ascontiguousarray(bnw.reshape(8, 128).T),
        "bld_d": np.ascontiguousarray(log_decay.reshape(16, 1)),
        "bpm_d": pm, "bhsel_d": hsel, "bE1_d": E1,
        "bE2_d": np.ascontiguousarray(hsel.T),
        "bowT_d": np.ascontiguousarray(bow.T),
        "onesf_d": np.ones((1, 128), np.float32),
        "ones_d": np.ones((128, 1), ml_dtypes.bfloat16),
        "expw_i": wi, "expw_o": wo,
    }
    in_maps = []
    for c in range(NCORES):
        rows = core_rows(c, NCORES)
        d = {"x": np.ascontiguousarray(x[rows])}
        d.update(common)
        d.update(ffn_core_inputs(c, NCORES, router_w, router_b))
        in_maps.append(d)

    nc = _build()
    res = run_bass_kernel_spmd(nc, in_maps, core_ids=list(range(NCORES)))
    out = np.empty((16, 2048, 1024), np.float32)
    for c in range(NCORES):
        out[core_rows(c, NCORES)] = np.asarray(res.results[c]["y"], dtype=np.float32)
    return out


# version 07 (final)
# speedup vs baseline: 1.1055x; 1.0011x over previous
import numpy as np
import ml_dtypes
import contextlib

"""Recording/replay emit framework with automatic semaphore dependency edges.

Buffers tracked by tensor name (or explicit string key). Every RAW/WAR/WAW
gets a semaphore edge (race-detector-sound, incl. same-engine). DMA ops use
per-queue round-robin semaphore lanes with at most one outstanding DMA per
lane (issuer pre-waits its lane), so lane_sem >= 16*k soundly means the k-th
use of that lane completed.
"""
import concourse.bass as bass
import concourse.mybir as mybir

ENGINES = ("sync", "scalar", "vector", "tensor", "gpsimd")
NLANES = 8


def bufkey(x):
    if isinstance(x, str):
        return x
    t = getattr(x, "tensor", x)
    return t.name.split("__s")[0]


class Op:
    __slots__ = ("i", "eng", "fn", "reads", "writes", "dma", "group",
                 "deps", "seq", "need_inc", "sem_val_after", "lane", "tag", "bar",
                 "noinstr", "local")

    def __init__(self, i, eng, fn, reads, writes, dma, group, tag):
        self.i = i
        self.eng = eng
        self.fn = fn
        self.reads = reads
        self.writes = writes
        self.dma = dma
        self.group = group
        self.deps = set()
        self.seq = None
        self.need_inc = False
        self.sem_val_after = None
        self.lane = None
        self.tag = tag
        self.bar = False
        self.noinstr = False
        self.local = False


class B:
    def __init__(self, nc):
        self.nc = nc
        self.ops = []
        self.last_write = {}
        self.readers = {}
        self.acc_writers = {}
        self.open_groups = {}
        self.last_bar = None

    def emit(self, eng, fn=None, reads=(), writes=(), dma=False, group=None,
             group_end=False, tag=None, noinstr=False, local=False):
        assert eng in ENGINES
        op = Op(len(self.ops), eng, fn, [bufkey(r) for r in reads],
                [bufkey(w) for w in writes], dma, group, tag)
        op.noinstr = noinstr
        op.local = local
        deps = set()
        for k in op.reads:
            lw = self.last_write.get(k)
            if lw is not None:
                deps.add(lw)
            deps.update(self.acc_writers.get(k, ()))
        for k in op.writes:
            lw = self.last_write.get(k)
            if lw is not None:
                deps.add(lw)
            for r in self.readers.get(k, ()):
                deps.add(r)
        if self.last_bar is not None:
            for k in op.reads + op.writes:
                if k not in self.last_write and k not in self.readers:
                    deps.add(self.last_bar)
                    break
        if group is not None:
            grp = self.open_groups.setdefault(group, [])
            deps -= set(grp)
            grp.append(op.i)
        op.deps = deps
        self.ops.append(op)
        for k in op.reads:
            self.readers.setdefault(k, []).append(op.i)
        if group is None or group_end:
            if group is not None:
                self.open_groups.pop(group)
            for k in op.writes:
                self.last_write[k] = op.i
                self.readers[k] = []
        for k in op.writes:
            self.acc_writers.setdefault(k, []).append(op.i)
        return op

    def inherit(self, new_key, old_keys):
        lw = sorted({self.last_write[k] for k in old_keys if k in self.last_write})
        rd = [i for k in old_keys for i in self.readers.get(k, ())]
        aw = [i for k in old_keys for i in self.acc_writers.get(k, ())]
        if lw:
            self.last_write[new_key] = lw[-1]
        extra = sorted(set(rd) | set(lw[:-1]) | set(aw))
        if extra:
            self.readers[new_key] = extra
        return new_key

    def barrier(self, eng="vector", tag="bar"):
        op = self.emit(eng, None, tag=tag)
        deps = set(self.last_write.values())
        for rl in self.readers.values():
            deps.update(rl)
        deps.discard(op.i)
        deps = {d for d in deps if not self.ops[d].local}
        op.deps = deps
        op.bar = True
        for k in list(self.last_write):
            self.last_write[k] = op.i
            self.readers[k] = []
        for k in list(self.acc_writers):
            self.acc_writers[k] = []
        self.last_bar = op.i
        return op

    def finalize(self):
        nc = self.nc
        ops = self.ops
        seqs = {e: [] for e in ENGINES}
        for op in ops:
            op.seq = len(seqs[op.eng])
            seqs[op.eng].append(op)
        for op in ops:
            for d in op.deps:
                dop = ops[d]
                if dop.local and dop.eng == op.eng:
                    continue
                dop.need_inc = True
        comp_count = {e: 0 for e in ENGINES}
        lane_count = {}
        lane_rr = {e: 0 for e in ENGINES}
        pre_waits = {}
        for op in ops:
            if op.dma:
                lane = lane_rr[op.eng]
                lane_rr[op.eng] = (lane + 1) % NLANES
                op.lane = lane
                key = (op.eng, lane)
                prior = lane_count.get(key, 0)
                if prior > 0:
                    pre_waits[op.i] = (("l", op.eng, lane), 16 * prior)
                lane_count[key] = prior + 1
                op.sem_val_after = 16 * (prior + 1)
                op.need_inc = True
            elif op.need_inc:
                comp_count[op.eng] += 1
                op.sem_val_after = comp_count[op.eng]
        import contextlib
        self._semstack = contextlib.ExitStack()
        csem = {e: self._semstack.enter_context(nc.semaphore(f"c_{e}"))
                for e in ENGINES if comp_count[e] > 0}
        lsem = {k: self._semstack.enter_context(nc.semaphore(f"l_{k[0]}_{k[1]}"))
                for k in lane_count}
        known = {e: {} for e in ENGINES}
        waits = []
        for op in ops:
            need = {}
            for d in sorted(op.deps):
                dop = ops[d]
                if dop.sem_val_after is None:
                    continue
                key = (("l", dop.eng, dop.lane) if dop.dma else ("c", dop.eng))
                v = dop.sem_val_after
                if need.get(key, 0) < v:
                    need[key] = v
            if op.i in pre_waits:
                key, v = pre_waits[op.i]
                if need.get(key, 0) < v:
                    need[key] = v
            k = known[op.eng]
            todo = []
            for key, v in sorted(need.items(), key=lambda kv: str(kv[0])):
                if k.get(key, 0) < v:
                    todo.append((key, v))
                    k[key] = v
            waits.append(todo)
        self.waits = waits
        self._csem, self._lsem = csem, lsem

        def sem_of(key):
            if key[0] == "c":
                return csem[key[1]]
            return lsem[(key[1], key[2])]

        block_cm = nc.Block()
        block = block_cm.__enter__()

        def run_engine(engine_name):
            def runner(eobj):
                try:
                    eobj.get_value_cache().nocache = True
                except Exception:
                    pass
                for op in seqs[engine_name]:
                    todo = waits[op.i]
                    for key, v in todo:
                        eobj.wait_ge(sem_of(key), v)
                    try:
                        instr = op.fn(eobj) if op.fn is not None else eobj.nop()
                    except Exception:
                        import traceback as _tb
                        _tb.print_exc()
                        raise
                    if op.need_inc and op.sem_val_after is not None:
                        sem = (lsem[(op.eng, op.lane)] if op.dma else csem[op.eng])
                        amt = 16 if op.dma else 1
                        if instr is not None:
                            instr.then_inc(sem, amt)
                        else:
                            eobj.sem_inc(sem, amt)
                    if op.dma:
                        eobj.get_value_cache().barrier()
            return runner

        block.sync(run_engine("sync"))
        block.scalar(run_engine("scalar"))
        block.vector(run_engine("vector"))
        block.tensor(run_engine("tensor"))
        block.gpsimd(run_engine("gpsimd"))
        block_cm.__exit__(None, None, None)
        self._semstack.close()


"""Megakernel builder for mk_hybrid_d1 (see DESIGN.md).

Per-core program; 8 cores SPMD. Core c owns batch rows core_rows(c)
(stride-4 pair layout). All emitters share a Ctx.
"""
import numpy as np
import contextlib
import concourse.bass as bass
import concourse.mybir as mybir

F32 = mybir.dt.float32
BF16 = mybir.dt.bfloat16
I32 = mybir.dt.int32
AF = mybir.ActivationFunctionType
ALU = mybir.AluOpType
AX = mybir.AxisListType

D = 1024
NCHUNK = 8
TCORE = 4096
NT = 32
TROW = 2048
NTR = 16
DILS = [1, 2, 4, 8]
NORM = 1.0 / np.sqrt(128.0)
HOFF = [0, 2048, 3072, 3584]




class Ctx:
    def __init__(self, nc, b):
        self.nc = nc
        self.b = b
        self.stack = contextlib.ExitStack()
        self.t = {}
        self.dram = {}
        self.stage_stack = None
        self.stage_id = 0

    def _name(self, name):
        return name if self.stage_stack is None else f"{name}__s{self.stage_id}"

    def _stk(self):
        return self.stack if self.stage_stack is None else self.stage_stack

    def sbuf(self, name, shape, dtype):
        h = self._stk().enter_context(self.nc.sbuf_tensor(self._name(name), shape, dtype))
        self.t[name] = h
        return h

    def psum(self, name, shape, dtype=F32):
        h = self._stk().enter_context(self.nc.psum_tensor(self._name(name), shape, dtype))
        self.t[name] = h
        return h

    def stage_begin(self):
        assert self.stage_stack is None
        self.stage_id += 1
        self.stage_stack = contextlib.ExitStack()

    def stage_end(self):
        self.stage_stack.close()
        self.stage_stack = None
        self.b.barrier()


def emit_ln_tile(ctx, resid_tile, resid_key, dst_tm, dst_key, mean_ap, rstd_ap,
                 mk="lmean", rk="lrstd", bneg_ap=None, bneg_key=None, sfx="",
                 sq_ap=None, sq_key=None):
    b = ctx.b
    t = ctx.t
    ssq, s, m2, bb = t["lssq" + sfx], t["ls" + sfx], t["lm2" + sfx], t["lbb" + sfx]
    if sq_ap is None:
        sq, sqk = t["lsq"][:], "lsq"
    else:
        sq, sqk = sq_ap, sq_key

    b.emit("scalar", lambda e: e.activation(out=sq, in_=resid_tile, func=AF.Square, accum_out=ssq[:]),
           reads=[resid_key], writes=[sqk, ssq])
    b.emit("vector", lambda e: e.reduce_sum(s[:], resid_tile, axis=AX.X),
           reads=[resid_key], writes=[s])
    b.emit("vector", lambda e: e.tensor_scalar_mul(mean_ap, s[:], 1.0 / D),
           reads=[s], writes=[mk])
    b.emit("scalar", lambda e: e.activation(out=m2[:], in_=mean_ap, func=AF.Square),
           reads=[mk], writes=[m2])
    b.emit("scalar", lambda e: e.activation(out=bb[:], in_=m2[:], func=AF.Copy, scale=-1.0, bias=1e-5),
           reads=[m2], writes=[bb])
    b.emit("scalar", lambda e: e.activation(out=rstd_ap, in_=ssq[:], func=AF.Sqrt, scale=1.0 / D, bias=bb[:]),
           reads=[ssq, bb], writes=[rk])
    b.emit("vector", lambda e: e.reciprocal(rstd_ap, rstd_ap), reads=[rk], writes=[rk])
    bbk = bneg_key if bneg_key is not None else "lbb"
    bba = bb[:, :] if bneg_ap is None else bneg_ap
    b.emit("vector", lambda e: e.tensor_mul(bba, mean_ap, rstd_ap), reads=[mk, rk], writes=[bbk])
    b.emit("vector", lambda e: e.tensor_scalar_mul(bba, bba, -1.0), reads=[bbk], writes=[bbk])
    b.emit("scalar", lambda e: e.activation(out=dst_tm, in_=resid_tile, func=AF.Identity, scale=rstd_ap, bias=bba),
           reads=[resid_key, rk, bbk], writes=[dst_key])


def emit_tm2fm(ctx, src_tm, src_key, dst_fm_fn, dst_key, eng="sync"):
    b = ctx.b
    for c in range(NCHUNK):
        b.emit(eng, lambda e, c=c: e.dma_start(
            out=dst_fm_fn(c), in_=src_tm[:, c * 128:(c + 1) * 128], transpose=True),
            reads=[src_key], writes=[dst_key], dma=True)


def dil_ap(xfm_chunk, dil, s0, cnt_s):
    base = xfm_chunk.rearrange("p (seg s) -> p seg s", s=128)
    return base[:, ::dil, s0:s0 + cnt_s].rearrange("p n s -> p s n")


def emit_attn_a(ctx):
    nc, b, t = ctx.nc, ctx.b, ctx.t
    resid = t["resid"]
    xfm = t["xfm"]
    xlt = t["xlt"]
    q, k, v = t["aQ"], t["aK"], t["aV"]
    vst = t["aVs"]
    att = t["aAtt"]
    sraw = t["aSraw"]
    sm = t["aSm"]
    rowmax, rowsum = t["aRmax"], t["aRsum"]
    ysb = t["aY"]
    dfm = t["aDfm"]
    dtm = t["aDtm"]
    masks = t["amasks"]
    bout = t["bout"]
    woutT = t["woutT"]
    wq = t["awq"]; wk = t["awk"]; wv = t["awv"]
    wqkvT = ctx.dram["wqkvT"]
    ps_qkv = t["ps_qkv"]; ps_s = t["ps_s"]; ps_av = t["ps_av"]; ps_y = t["ps_y"]

    def load_head_w(h, buf):
        for wi, wsb in enumerate((wq, wk, wv)):
            src = wqkvT[wi, h].rearrange("(c p) k -> p c k", p=128)
            b.emit("gpsimd", lambda e, wsb=wsb, src=src, buf=buf: e.dma_start(
                out=wsb[:, buf], in_=src), reads=["wqkvT"], writes=[wsb], dma=True)

    load_head_w(0, 0)
    for r in range(2):
        for i in range(NTR):
            ti = 16 * r + i
            emit_ln_tile(ctx, resid[:, ti, :], "resid", xlt[:, :], "xlt",
                         t["lmean"][:, 0:1], t["lrstd"][:, 0:1])
            emit_tm2fm(ctx, xlt, "xlt", lambda c, i=i: xfm[:, c, i * 128:(i + 1) * 128], "xfm",
                       eng="sync")
        for h, dil in enumerate(DILS):
            L = 16 // dil
            ncols = 128 * L
            buf = 0
            if r == 0 and h == 0:
                pass
            if h > 0 or r > 0:
                load_head_w(h, 0)
            for j0 in range(0, ncols, 512):
              w = min(512, ncols - j0)
              for (wsb, dst, tag) in ((wq, q, "q"), (wk, k, "k")):
                    cs = w // L
                    s0 = j0 // L
                    for cc in range(NCHUNK):
                        ap = dil_ap(xfm[:, cc, :], dil, s0, cs)
                        b.emit("tensor", lambda e, wsb=wsb, buf=buf, cc=cc, ap=ap, w=w: e.matmul(
                            ps_qkv[:, 0:w], wsb[:, buf, cc, :], ap,
                            start=(cc == 0), stop=(cc == NCHUNK - 1)),
                            reads=[wsb, "xfm"], writes=[ps_qkv], group=f"qk{r}{h}{tag}{j0}",
                            group_end=(cc == NCHUNK - 1))
                    b.emit("vector", lambda e, dst=dst, w=w: e.tensor_copy(
                        dst[:, 0:w], ps_qkv[:, 0:w]),
                        reads=[ps_qkv], writes=[dst])
              for gl in range(w // 128):
                g = j0 // 128 + gl
                s0 = (g * 128) // L
                cs = 128 // L
                for cc in range(NCHUNK):
                    ap = dil_ap(xfm[:, cc, :], dil, s0, cs)
                    b.emit("scalar", lambda e, ap=ap: e.activation(
                        out=vst[:, :], in_=ap, func=AF.Copy),
                        reads=["xfm"], writes=["aVs"])
                    b.emit("tensor", lambda e, cc=cc, buf=buf: e.matmul(
                        ps_qkv[:, 0:128], vst[:, :], wv[:, buf, cc, :],
                        start=(cc == 0), stop=(cc == NCHUNK - 1)),
                        reads=[wv, "aVs"], writes=[ps_qkv], group=f"v{r}{h}{g}",
                        group_end=(cc == NCHUNK - 1))
                b.emit("vector", lambda e, gl=gl: e.tensor_copy(
                    v[:, gl * 128:(gl + 1) * 128], ps_qkv[:, 0:128]),
                    reads=[ps_qkv], writes=[v])
              for gl in range(w // 128):
                g = j0 // 128 + gl
                o = gl * 128
                b.emit("tensor", lambda e, o=o: e.matmul(
                    ps_s[:, :], k[:, o:o + 128], q[:, o:o + 128], start=True, stop=True),
                    reads=[k, q], writes=[ps_s], group=f"s{r}{h}{g}", group_end=True)
                b.emit("vector", lambda e, h=h: e.scalar_tensor_tensor(
                    sraw[:], ps_s[:, :], NORM, masks[:, h, :], ALU.mult, ALU.add),
                    reads=[ps_s, masks], writes=[sraw])
                b.emit("vector", lambda e: e.reduce_max(rowmax[:], sraw[:], axis=AX.X, negate=True),
                       reads=[sraw], writes=[rowmax])
                b.emit("scalar", lambda e: e.activation(out=sraw[:], in_=sraw[:], func=AF.Exp, bias=rowmax[:]),
                       reads=[sraw, rowmax], writes=[sraw])
                b.emit("vector", lambda e: e.reduce_sum(rowsum[:], sraw[:], axis=AX.X),
                       reads=[sraw], writes=[rowsum])
                b.emit("vector", lambda e: e.reciprocal(rowsum[:], rowsum[:]),
                       reads=[rowsum], writes=[rowsum])
                b.emit("vector", lambda e: e.tensor_scalar(sm[:], sraw[:], rowsum[:], None, ALU.mult),
                       reads=[sraw, rowsum], writes=[sm])
                b.emit("tensor", lambda e, o=o: e.matmul(
                    ps_av[:, :], v[:, o:o + 128], sm[:, :], start=True, stop=True),
                    reads=[v, sm], writes=[ps_av], group=f"av{r}{h}{g}", group_end=True)
                b.emit("vector", lambda e, h=h, g=g: e.tensor_copy(
                    att[:, HOFF[h] + g * 128:HOFF[h] + (g + 1) * 128], ps_av[:, :]),
                    reads=[ps_av], writes=[att])
        PS_OFF = [0, 512, 768, 896]
        SS = [(True, True), (True, False), (False, False), (False, True)]
        for tau in range(4):
          for half in range(2):
            for oc4 in range(4):
                oc = 4 * half + oc4
                for h, dil in enumerate(DILS):
                    wsl = 512 // dil
                    a0 = HOFF[h] + tau * wsl
                    p0 = PS_OFF[h]
                    st, sp = SS[h]
                    b.emit("tensor", lambda e, h=h, oc=oc, a0=a0, p0=p0, w=wsl, st=st, sp=sp: e.matmul(
                        ps_y[:, p0:p0 + w], woutT[:, h, oc * 128:(oc + 1) * 128], att[:, a0:a0 + w],
                        start=st, stop=sp),
                        reads=[woutT, att], writes=[ps_y], group=f"y{r}{tau}{oc}",
                        group_end=(h == 3))
                b.emit("vector", lambda e: e.tensor_copy(ysb[:], ps_y[:, 0:512]),
                       reads=[ps_y], writes=[ysb])
                for h, dil in enumerate(DILS[1:], start=1):
                    p0 = PS_OFF[h]
                    w = 512 // dil
                    b.emit("vector", lambda e, p0=p0, w=w, dil=dil: e.tensor_add(
                        ysb[:, ::dil], ysb[:, ::dil], ps_y[:, p0:p0 + w]),
                        reads=[ps_y, ysb], writes=[ysb])
                b.emit("scalar", lambda e, oc=oc, oc4=oc4: e.activation(
                    out=dfm[:, oc4, :], in_=ysb[:], func=AF.Identity, scale=1.0,
                    bias=bout[:, oc:oc + 1]),
                    reads=[ysb, bout], writes=[dfm])
            for tt in range(4):
                ti = 16 * r + tau * 4 + tt
                for cc4 in range(4):
                    b.emit("sync", lambda e, cc4=cc4, tt=tt: e.dma_start(
                        out=dtm[:, cc4 * 128:(cc4 + 1) * 128],
                        in_=dfm[:, cc4, tt * 128:(tt + 1) * 128], transpose=True),
                        reads=[dfm], writes=[dtm], dma=True)
                b.emit("vector", lambda e, ti=ti, half=half: e.tensor_add(
                    resid[:, ti, 512 * half:512 * (half + 1)],
                    resid[:, ti, 512 * half:512 * (half + 1)], dtm[:, :]),
                    reads=[dtm, "resid"], writes=["resid"])


def alloc_attn_a(ctx):
    ctx.sbuf("woutT", [128, 4, D], BF16)
    ctx.sbuf("bout", [128, NCHUNK], F32)
    ctx.sbuf("amasks", [128, 4, 128], BF16)
    ctx.sbuf("aQ", [128, 512], BF16)
    ctx.sbuf("aVs", [128, 128], BF16)
    ctx.sbuf("aK", [128, 512], BF16)
    ctx.sbuf("aV", [128, 512], BF16)
    ctx.sbuf("aAtt", [128, 3840], BF16)
    ctx.sbuf("aSraw", [128, 128], F32)
    ctx.sbuf("aSm", [128, 128], BF16)
    ctx.sbuf("aRmax", [128, 1], F32)
    ctx.sbuf("aRsum", [128, 1], F32)
    ctx.sbuf("aY", [128, 512], F32)
    ctx.sbuf("aDfm", [128, 4, 512], BF16)
    ctx.sbuf("aDtm", [128, 512], BF16)
    ctx.sbuf("awq", [128, 1, NCHUNK, 128], BF16)
    ctx.sbuf("awk", [128, 1, NCHUNK, 128], BF16)
    ctx.sbuf("awv", [128, 1, NCHUNK, 128], BF16)
    ctx.psum("ps_qkv", [128, 512])
    ctx.psum("ps_s", [128, 128])
    ctx.psum("ps_av", [128, 128])
    ctx.psum("ps_y", [128, 1024])


def alloc_common(ctx):
    ctx.sbuf("resid", [128, NT, D], F32)
    ctx.sbuf("xfm", [128, NCHUNK, TROW], BF16)
    ctx.sbuf("xlt", [128, D], BF16)
    ctx.sbuf("lsq", [128, D], BF16)
    ctx.sbuf("lssq", [128, 1], F32)
    ctx.sbuf("ls", [128, 1], F32)
    ctx.sbuf("lm2", [128, 1], F32)
    ctx.sbuf("lbb", [128, 1], F32)
    ctx.sbuf("lmean", [128, NT], F32)
    ctx.sbuf("lrstd", [128, NT], F32)


def make_attn_a_masks():
    m = np.full((128, 4, 128), -1e30, np.float32)
    for h, dil in enumerate(DILS):
        L = 16 // dil
        for p in range(128):
            s1, n = divmod(p, L)
            for f in range(128):
                s2, mm = divmod(f, L)
                if s1 == s2 and n >= mm:
                    m[p, h, f] = 0.0
    return m


def alloc_attn_b(ctx):
    ctx.sbuf("bqf", [128, 8], F32)
    ctx.sbuf("bkf", [128, 8], F32)
    ctx.sbuf("bqk", [128, 8], F32)
    ctx.sbuf("bld", [16, 1], F32)
    ctx.sbuf("blam", [16, 1], F32)
    ctx.sbuf("bpm", [16, 8], F32)
    ctx.sbuf("blamp", [16, 8], F32)
    ctx.sbuf("bhsel", [128, 2], F32)
    ctx.sbuf("bE1", [16, 128], F32)
    ctx.sbuf("bE2", [2, 128], F32)
    ctx.sbuf("bnw", [128, 8], F32)
    ctx.sbuf("blamd", [128, 8], F32)
    ctx.sbuf("bcd", [128, 8], F32)
    ctx.sbuf("bscale", [128, 8], F32)
    ctx.sbuf("bones", [128, 1], BF16)
    ctx.sbuf("bonesf", [1, 128], F32)
    ctx.sbuf("byscan", [128, 1024], F32)
    ctx.sbuf("bsq", [128, 1024], BF16)
    ctx.sbuf("browrms", [1, 512], F32)
    ctx.sbuf("beps", [1, 1], F32)
    ctx.sbuf("browssb", [128, 512], F32)
    ctx.sbuf("bowT", [128, NCHUNK, D], BF16)
    ctx.sbuf("bY", [128, 512], F32)
    ctx.sbuf("bDfm", [128, 4, 512], BF16)
    ctx.sbuf("bDtm", [128, 512], BF16)
    ctx.psum("ps_b1", [128, 512])
    ctx.psum("ps_b2", [128, 512])
    ctx.psum("ps_rms", [1, TROW])
    ctx.psum("ps_tab", [16, 8])


def emit_attn_b_tables(ctx):
    b, t = ctx.b, ctx.t
    dr = ctx.dram
    for name, dst in (("bq_fm", "bqf"), ("bk_fm", "bkf"), ("bnw_fm", "bnw")):
        b.emit("gpsimd", lambda e, name=name, dst=dst: e.dma_start(
            out=t[dst][:, :], in_=dr[name][:, :]), reads=[name], writes=[dst], dma=True)
    for name, dst in (("bld_d", "bld"), ("bpm_d", "bpm"), ("bhsel_d", "bhsel"),
                      ("bE1_d", "bE1"), ("bE2_d", "bE2")):
        b.emit("gpsimd", lambda e, name=name, dst=dst: e.dma_start(
            out=t[dst][:, :], in_=dr[name][:, :]), reads=[name], writes=[dst], dma=True)
    b.emit("vector", lambda e: e.tensor_mul(t["bqk"][:], t["bqf"][:], t["bkf"][:]),
           reads=["bqf", "bkf"], writes=["bqk"])
    b.emit("tensor", lambda e: e.matmul(t["ps_tab"][0:2, :], t["bhsel"][:, :], t["bqk"][:, :],
                                        start=True, stop=True),
           reads=["bhsel", "bqk"], writes=["ps_tab"], group="btab1", group_end=True)
    b.emit("vector", lambda e: e.tensor_copy(t["bqk"][0:2, :], t["ps_tab"][0:2, :]),
           reads=["ps_tab"], writes=["bqk"])
    b.emit("tensor", lambda e: e.matmul(t["ps_b1"][:, 0:8], t["bE2"][0:2, :], t["bqk"][0:2, :], start=True, stop=True),
           reads=["bE2", "bqk"], writes=["ps_b1"], group="btab2", group_end=True)
    b.emit("vector", lambda e: e.tensor_copy(t["bcd"][:, :], t["ps_b1"][:, 0:8]),
           reads=["ps_b1"], writes=["bcd"])
    b.emit("scalar", lambda e: e.activation(out=t["blam"][:, :], in_=t["bld"][:, :], func=AF.Sigmoid),
           reads=["bld"], writes=["blam"])
    b.emit("vector", lambda e: e.tensor_scalar(t["blamp"][:, :], t["bpm"][:, :], t["blam"][:, 0:1], None, ALU.mult),
           reads=["bpm", "blam"], writes=["blamp"])
    b.emit("tensor", lambda e: e.matmul(t["ps_b1"][:, 0:8], t["bE1"][:, :], t["blamp"][:, :], start=True, stop=True),
           reads=["bE1", "blamp"], writes=["ps_b1"], group="btab3", group_end=True)
    b.emit("vector", lambda e: e.tensor_copy(t["blamd"][:, :], t["ps_b1"][:, 0:8]),
           reads=["ps_b1"], writes=["blamd"])
    b.emit("vector", lambda e: e.tensor_mul(t["bscale"][:, :], t["bcd"][:, :], t["bnw"][:, :]),
           reads=["bcd", "bnw"], writes=["bscale"])


def emit_attn_b(ctx):
    nc, b, t = ctx.nc, ctx.b, ctx.t
    resid = t["resid"]
    xfm = t["xfm"]; xlt = t["xlt"]
    yscan = t["byscan"]; sq = t["bsq"]
    rowrms = t["browrms"]; rmsb = t["browssb"]
    owT = t["bowT"]; ysb = t["bY"]; dfm = t["bDfm"]; dtm = t["bDtm"]
    ps1 = t["ps_b1"]; ps2 = t["ps_b2"]; psr = t["ps_rms"]
    b.emit("gpsimd", lambda e: e.dma_start(
        out=owT[:, :, :], in_=ctx.dram["bowT_d"].rearrange("(c p) od -> p c od", p=128)),
        reads=["bowT_d"], writes=["bowT"], dma=True)
    b.emit("gpsimd", lambda e: e.dma_start(
        out=t["bones"][:, :], in_=ctx.dram["ones_d"][:, 0:1]), reads=["ones_d"], writes=["bones"], dma=True)
    b.emit("gpsimd", lambda e: e.dma_start(
        out=t["bonesf"][:, :], in_=ctx.dram["onesf_d"][0:1, :]), reads=["onesf_d"], writes=["bonesf"], dma=True)
    b.emit("vector", lambda e: e.memset(t["beps"][:, :], 1e-6), writes=["beps"])
    for r in range(2):
        for i in range(NTR):
            ti = 16 * r + i
            emit_ln_tile(ctx, resid[:, ti, :], "resid", xlt[:, :], "xlt",
                         t["lmean"][:, 0:1], t["lrstd"][:, 0:1])
            emit_tm2fm(ctx, xlt, "xlt", lambda c, i=i: xfm[:, c, i * 128:(i + 1) * 128], "xfm",
                       eng="sync")
        for cc in range(NCHUNK):
            lam_b = t["blamd"][:, cc:cc + 1].broadcast_to((128, 1024))
            for hh in range(2):
                init = 0.0 if hh == 0 else yscan[:, 1023:1024]
                b.emit("vector", lambda e, cc=cc, hh=hh, init=init, lam_b=lam_b: e.tensor_tensor_scan(
                    yscan[:, :], lam_b, xfm[:, cc, hh * 1024:(hh + 1) * 1024], init, ALU.mult, ALU.add),
                    reads=["blamd", "xfm", "byscan"], writes=["byscan"])
                b.emit("scalar", lambda e, cc=cc: e.activation(
                    out=sq[:, :], in_=yscan[:, :], func=AF.Square, scale=t["bcd"][:, cc:cc + 1]),
                    reads=["byscan", "bcd"], writes=["bsq"])
                b.emit("scalar", lambda e, cc=cc, hh=hh: e.activation(
                    out=xfm[:, cc, hh * 1024:(hh + 1) * 1024], in_=yscan[:, :], func=AF.Identity, scale=t["bscale"][:, cc:cc + 1]),
                    reads=["byscan", "bscale"], writes=["xfm"])
                for tl in range(2):
                    tt = hh * 2 + tl
                    b.emit("tensor", lambda e, tt=tt, tl=tl, cc=cc: e.matmul(
                        psr[0:1, tt * 512:(tt + 1) * 512], t["bones"][:, :], sq[:, tl * 512:(tl + 1) * 512],
                        start=(cc == 0), stop=(cc == NCHUNK - 1)),
                        reads=["bones", "bsq"], writes=["ps_rms"], group=f"brms{r}{tt}",
                        group_end=(cc == NCHUNK - 1))
        for tt in range(4):
            b.emit("vector", lambda e, tt=tt: e.tensor_copy(rowrms[:, :], psr[0:1, tt * 512:(tt + 1) * 512]),
                   reads=["ps_rms"], writes=["browrms"])
            b.emit("scalar", lambda e: e.activation(out=rowrms[:, :], in_=rowrms[:, :],
                                                    func=AF.Sqrt, scale=1.0 / D, bias=t["beps"][:, :]),
                   reads=["browrms", "beps"], writes=["browrms"])
            b.emit("vector", lambda e: e.reciprocal(rowrms[:, :], rowrms[:, :]),
                   reads=["browrms"], writes=["browrms"])
            b.emit("tensor", lambda e: e.matmul(
                ps2[:, :], t["bonesf"][0:1, :], rowrms[0:1, :],
                start=True, stop=True),
                reads=["bonesf", "browrms"], writes=["ps_b2"], group=f"brb{r}{tt}", group_end=True)
            b.emit("vector", lambda e: e.tensor_copy(rmsb[:, :], ps2[:, :]),
                   reads=["ps_b2"], writes=["browssb"])
            for half in range(2):
                for oc4 in range(4):
                    oc = 4 * half + oc4
                    for cc in range(NCHUNK):
                        b.emit("tensor", lambda e, oc=oc, cc=cc, tt=tt: e.matmul(
                            ps1[:, :], owT[:, cc, oc * 128:(oc + 1) * 128],
                            xfm[:, cc, tt * 512:(tt + 1) * 512],
                            start=(cc == 0), stop=(cc == NCHUNK - 1)),
                            reads=["bowT", "xfm"], writes=["ps_b1"], group=f"bow{r}{tt}{oc}",
                            group_end=(cc == NCHUNK - 1))
                    b.emit("vector", lambda e: e.tensor_mul(ysb[:, :], ps1[:, :], rmsb[:, :]),
                           reads=["ps_b1", "browssb"], writes=["bY"])
                    b.emit("scalar", lambda e, oc4=oc4: e.activation(
                        out=dfm[:, oc4, :], in_=ysb[:], func=AF.Copy, scale=1.0, bias=0.0),
                        reads=["bY"], writes=["bDfm"])
                for tloc in range(4):
                    ti = 16 * r + tt * 4 + tloc
                    for cc4 in range(4):
                        b.emit("sync", lambda e, cc4=cc4, tloc=tloc: e.dma_start(
                            out=dtm[:, cc4 * 128:(cc4 + 1) * 128],
                            in_=dfm[:, cc4, tloc * 128:(tloc + 1) * 128], transpose=True),
                            reads=["bDfm"], writes=["bDtm"], dma=True)
                    b.emit("vector", lambda e, ti=ti, half=half: e.tensor_add(
                        resid[:, ti, 512 * half:512 * (half + 1)],
                        resid[:, ti, 512 * half:512 * (half + 1)], dtm[:, :]),
                        reads=["bDtm", "resid"], writes=["resid"])





def emit_tr_32x128(ctx, dst, dst_key, src, src_key):
    b = ctx.b
    for i in range(4):
        b.emit("vector", lambda e, i=i: e.transpose(
            dst[32 * i:32 * (i + 1), 0:32], src[0:32, 32 * i:32 * (i + 1)]),
            reads=[src_key], writes=[dst_key])


def emit_tr_128x32(ctx, dst, dst_key, src, src_key):
    b = ctx.b
    for i in range(4):
        b.emit("vector", lambda e, i=i: e.transpose(
            dst[0:32, 32 * i:32 * (i + 1)], src[32 * i:32 * (i + 1), 0:32]),
            reads=[src_key], writes=[dst_key])



def emit_select(ctx, out_ap, out_key, mask_ap, mask_keys, true_ap, true_keys, false_ap, false_keys, n=8,
                scr=None, scrk=None):
    b, t = ctx.b, ctx.t
    if scr is None:
        scr, scrk = t["fsel"], "fsel"
    sa = scr[:, 0:n]
    b.emit("vector", lambda e: e.tensor_sub(sa, false_ap, true_ap),
           reads=list(false_keys) + list(true_keys), writes=[scrk])
    b.emit("vector", lambda e: e.tensor_mul(sa, sa, mask_ap),
           reads=list(mask_keys) + [scrk], writes=[scrk])
    b.emit("vector", lambda e: e.tensor_sub(out_ap, false_ap, sa),
           reads=list(false_keys) + [scrk], writes=[out_key])

from concourse.bass_core_rust import IndirectOffsetOnAxis

NEXP = 7
HALF_PAD = 1280
EXEC_ROWS = 768
MROWS = NEXP * 2 * HALF_PAD + 4


def alloc_ffn(ctx, ncores):
    ctx.sbuf("lbb2", [128, NT], F32)
    ctx.sbuf("frwT", [128, NCHUNK, 8], BF16)
    ctx.sbuf("frb", [128, 8], F32)
    ctx.sbuf("fiota", [128, 8], F32)
    ctx.sbuf("f99", [128, 8], F32)
    ctx.sbuf("fp8", [128, 8], F32)
    ctx.sbuf("fsel", [128, 8], F32)
    ctx.sbuf("fm1", [128, 1], F32)
    ctx.sbuf("fm2", [128, 1], F32)
    ctx.sbuf("foh", [128, 8], F32)
    ctx.sbuf("find", [128, 8], F32)
    ctx.sbuf("findb", [128, 8], BF16)
    ctx.sbuf("fe1", [128, NT], F32)
    ctx.sbuf("fe2", [128, NT], F32)
    ctx.sbuf("fw1", [128, NT], F32)
    ctx.sbuf("fw2", [128, NT], F32)
    ctx.sbuf("fwn", [128, NT], F32)
    ctx.sbuf("fjr1", [128, NT], F32)
    ctx.sbuf("fjr2", [128, NT], F32)
    ctx.sbuf("fgx1", [128, NT], F32)
    ctx.sbuf("fgx2", [128, NT], F32)
    ctx.sbuf("fzero1", [128, 1], F32)
    ctx.sbuf("frun", [8, 1], F32)
    ctx.sbuf("fscr8w", [8, max(16, 2 * ncores)], F32)
    ctx.sbuf("fones8", [8, 1], F32)
    ctx.sbuf("fonesf", [1, 128], F32)
    ctx.sbuf("fcnt", [8, 2], F32)
    ctx.sbuf("fagt", [8, ncores * 2], F32)
    ctx.sbuf("fmsk", [8, 2 * ncores * 2], F32)
    ctx.sbuf("foff", [8, 2], F32)
    ctx.sbuf("foffb", [128, 2, 8], F32)
    ctx.sbuf("fhoff", [128, 1], F32)
    ctx.sbuf("ftrash", [128, 1], F32)
    ctx.sbuf("fzrowc", [128, 1], F32)
    ctx.sbuf("fgidx", [128, 1], I32)
    ctx.sbuf("fgidx2", [128, 1], I32)
    ctx.sbuf("fscr", [128, 8], F32)
    ctx.sbuf("fscr1", [128, 1], F32)
    ctx.sbuf("fscr2", [128, 1], F32)
    ctx.sbuf("fkept", [128, 1], F32)
    ctx.sbuf("stab", [1, 256], I32)


def alloc_ffn_stage(ctx):
    ctx.sbuf("flgT", [32, 128], F32)
    ctx.sbuf("ftp", [128, 32], F32)
    ctx.sbuf("fz0", [128, D], BF16)
    for s in range(2):
        ctx.sbuf(f"fxlt{s}", [128, D], BF16)
        ctx.sbuf(f"fxfs{s}", [128, NCHUNK, 128], BF16)
        ctx.sbuf(f"flgs{s}", [32, 128], F32)
        ctx.sbuf(f"ftpa{s}", [128, 32], F32)
        ctx.sbuf(f"ftpb{s}", [128, 32], BF16)
        ctx.sbuf(f"ftpc{s}", [128, 32], F32)
        for nm in ("fp8s", "fohs", "finds", "fscrs", "fsels"):
            ctx.sbuf(f"{nm}{s}", [128, 8], F32)
        ctx.sbuf(f"findbs{s}", [128, 8], BF16)
        for nm in ("fm1s", "fm2s", "fsc1c", "fkeptc", "fscr2c"):
            ctx.sbuf(f"{nm}{s}", [128, 1], F32)
        ctx.sbuf(f"fgidx1{s}", [128, 1], I32)
        ctx.sbuf(f"fgidx2{s}", [128, 1], I32)
        for nm in ("lssq", "ls", "lm2", "lbb"):
            ctx.sbuf(f"{nm}_{s}", [128, 1], F32)
    ctx.sbuf("indfm", [32, 4096], BF16)
    ctx.sbuf("jrfm", [32, 4096], F32)
    ctx.psum("fps1", [128, 512])
    ctx.psum("fps2", [128, 512])
    ctx.psum("fps3", [128, 512])
    ctx.psum("fps4", [128, 512])


def alloc_ffn_expert(ctx):
    nc = ctx.nc
    resid_off = nc.lookup_mloc(ctx.t["resid"]).addr
    xfm_off = nc.lookup_mloc(ctx.t["xfm"]).addr
    offs = [0]
    def at(name, shape, dtype, base=None):
        nbytes = int(np.prod(shape[1:])) * (2 if dtype == BF16 else 4)
        if base is None:
            h = nc.alloc_sbuf_tensor_at(name, shape, dtype, offset=resid_off + offs[0])
            offs[0] += nbytes
        else:
            h = nc.alloc_sbuf_tensor_at(name, shape, dtype, offset=base)
        ctx.t[name] = h
        return h
    at("fhfm", [128, 32, EXEC_ROWS], BF16)
    at("fxg0", [128, NCHUNK, EXEC_ROWS], BF16)
    at("fxg1", [128, NCHUNK, EXEC_ROWS], BF16)
    at("fyfm", [128, NCHUNK, EXEC_ROWS], BF16)
    at("fxgt6", [128, 6, D], BF16)
    at("fytm6", [128, 6, D], BF16)
    at("fzero", [128, D], BF16)
    assert offs[0] <= NT * D * 4, f"expert alias overflow {offs[0]}"
    at("fwi", [128, 2, NCHUNK, 512], BF16, base=xfm_off)
    at("fwo", [128, 2, 32, 128], BF16, base=xfm_off + NCHUNK * 1024 * 2)
    at("frt0", [128, D], F32, base=xfm_off)
    at("frt1", [128, D], F32, base=xfm_off + 4096)
    at("fg1b0", [128, D], BF16, base=xfm_off + 8192)
    at("fg1b1", [128, D], BF16, base=xfm_off + 10240)
    at("fg2b0", [128, D], BF16, base=xfm_off + 12288)
    at("fg2b1", [128, D], BF16, base=xfm_off + 14336)
    at("fdelta0", [128, D], F32, base=xfm_off + 16384)
    at("fdelta1", [128, D], F32, base=xfm_off + 20480)
    at("fxltd0", [128, D], BF16, base=xfm_off + 24576)
    at("fxltd1", [128, D], BF16, base=xfm_off + 26624)


EXPERT_KEYS = ["fhfm", "fyfm", "fwi", "fwo", "fxgt6", "fytm6", "fzero", "fxg0", "fxg1"]
COMBINE_KEYS = ["frt0", "frt1", "fg1b0", "fg1b1", "fg2b0", "fg2b1",
                "fdelta0", "fdelta1", "fxltd0", "fxltd1"]

ST_CI = 0
ST_CO = 64
ST_SUB = 192
ST_ZROW = 199
ST_ZF = 200
ST_YTRASH = 207
QSEQ = [0, 1, 2, 0, 1, 2, 3]




def _dyn_reg(ctx, e, eng):
    import bass_rust
    regs = ctx.t.setdefault("_regs", {})
    key = "dyn_" + eng
    if key not in regs:
        h = e.register("dynreg").__enter__()
        regs[key] = (h, bass_rust.RuntimeValue(h, min_val=0, max_val=1 << 27))
    return regs[key]


def emit_dyn_dma(ctx, stab_idx, dram_base, dims, sbuf_ap, write_to_dram,
                 reads, writes, tag=None, eng="gpsimd"):
    b, t = ctx.b, ctx.t
    from concourse.ap import AP
    rk = "dynreg_" + eng
    chk = "dynch_" + eng
    def ld(e, stab_idx=stab_idx, eng=eng):
        h, rv = _dyn_reg(ctx, e, eng)
        e.reg_load(h, t["stab"][0:1, stab_idx:stab_idx + 1])
    b.emit(eng, ld, reads=["stab"], writes=[rk], noinstr=True, local=True)
    def go(e, eng=eng):
        h, rv = ctx.t["_regs"]["dyn_" + eng]
        dap = AP(dram_base.tensor, rv, [list(d) for d in dims])
        if write_to_dram:
            return e.dma_start(out=dap, in_=sbuf_ap)
        return e.dma_start(out=sbuf_ap, in_=dap)
    b.emit(eng, go, reads=list(reads) + [rk, chk],
           writes=list(writes) + [chk], dma=True, tag=tag)

def alloc_convert(ctx):
    nc = ctx.nc
    xfm_off = nc.lookup_mloc(ctx.t["xfm"]).addr
    ctx.t["cvt32"] = nc.alloc_sbuf_tensor_at("cvt32", [128, 4096], F32, offset=xfm_off)
    ctx.t["cvt16"] = nc.alloc_sbuf_tensor_at("cvt16", [128, 4096], BF16, offset=xfm_off + 4096 * 4)


def emit_ffn_convert_q(ctx, q):
    nc, b, t = ctx.nc, ctx.b, ctx.t
    dr = ctx.dram
    c32, c16 = t["cvt32"], t["cvt16"]
    b.inherit("cvt32", ["fwiF0__0", "fwiF0__1", "fwoF0__0", "fwoF0__1"])
    b.inherit("cvt16", ["fwiF0__0", "fwiF0__1", "fwoF0__0", "fwoF0__1"])
    for dc in range(NCHUNK):
        emit_dyn_dma(ctx, ST_CI + q * 8 + dc, dr["expw_i"],
                     [[4096, 128], [1, 4096]], c32[:, :], False,
                     reads=["expw_i"], writes=["cvt32"], eng="sync")
        b.emit("vector", lambda e: e.tensor_copy(c16[:, :], c32[:, :]),
               reads=["cvt32"], writes=["cvt16"])
        b.emit("gpsimd", lambda e, q=q, dc=dc: e.dma_start(
            out=dr["wscr_i"][q, dc, :, :], in_=c16[:, :]),
            reads=["cvt16"], writes=["wscr_i"], dma=True)
    for oc in range(8):
        emit_dyn_dma(ctx, ST_CO + q * 8 + oc, dr["expw_o"],
                     [[4096, 128], [1, 4096]], c32[:, :], False,
                     reads=["expw_o"], writes=["cvt32"], eng="scalar")
        b.emit("scalar", lambda e: e.activation(out=c16[:, :], in_=c32[:, :], func=AF.Copy),
               reads=["cvt32"], writes=["cvt16"])
        b.emit("gpsimd", lambda e, q=q, oc=oc: e.dma_start(
            out=dr["wscr_o"][q, oc, :, :], in_=c16[:, :]),
            reads=["cvt16"], writes=["wscr_o"], dma=True)


def emit_ffn(ctx, call, ncores, cap, out_dram=None):
    nc, b, t = ctx.nc, ctx.b, ctx.t
    import bass_rust
    S = f"F{call}_"
    resid = t["resid"]
    dr = ctx.dram
    mslab, yslab = dr["mslab"], dr["yslab"]
    mkey, ykey = "mslab" + S, "yslab" + S
    b.inherit(mkey, ["mslab", "mslabF0_", "yslabF0_"])
    b.inherit(ykey, ["yslab", "mslabF0_", "yslabF0_"])
    xlt = t["xlt"]
    flgT, ftp, fz0 = t["flgT"], t["ftp"], t["fz0"]
    fps1, fps2, fps3, fps4 = t["fps1"], t["fps2"], t["fps3"], t["fps4"]
    b.emit("vector", lambda e: e.memset(flgT[:, :], 0.0), writes=["flgT"])
    b.emit("vector", lambda e: e.memset(ftp[:, :], 0.0), writes=["ftp"])
    b.emit("vector", lambda e: e.memset(t["fonesf"][:, :], 1.0), writes=["fonesf"])
    b.emit("vector", lambda e: e.memset(t["frun"][:, :], 0.0), writes=["frun"])
    b.emit("vector", lambda e: e.memset(t["fones8"][:, :], 1.0), writes=["fones8"])
    b.emit("vector", lambda e: e.memset(t["fzero1"][:, :], 0.0), writes=["fzero1"])
    if call == 0:
        b.emit("vector", lambda e: e.memset(fz0[:, :], 0.0), writes=["fz0"])
        for ee in range(NEXP):
            emit_dyn_dma(ctx, ST_ZF + ee, mslab,
                         [[1024, 128], [131072, 6], [1, 1024]],
                         fz0[:, None, :].broadcast_to((128, 6, D)), True,
                         reads=["fz0"], writes=[mkey])
            emit_dyn_dma(ctx, ST_ZF + ee, yslab,
                         [[1024, 128], [131072, 10], [1, 1024]],
                         fz0[:, None, :].broadcast_to((128, 10, D)), True,
                         reads=["fz0"], writes=[ykey])
        emit_dyn_dma(ctx, ST_YTRASH, yslab, [[1024, 1], [1, 1024]],
                     fz0[0:1, :], True, reads=["fz0"], writes=[ykey])
    for s in range(2):
        b.emit("vector", lambda e, h=t[f"ftpb{s}"]: e.memset(h[:, :], 0.0), writes=[f"ftpb{s}"])
        b.emit("vector", lambda e, h=t[f"flgs{s}"]: e.memset(h[:, :], 0.0), writes=[f"flgs{s}"])
    b.emit("vector", lambda e, h=t["jrfm"]: e.memset(h[:, :], 0.0), writes=["jrfm"])
    indfm, jrfm = t["indfm"], t["jrfm"]
    for ti in range(NT):
        s = ti % 2
        def sx(name, s=s):
            return t[f"{name}{s}"], f"{name}{s}"
        xls, xlk = sx("fxlt")
        fxs, fxk = sx("fxfs")
        lgs, lgk = sx("flgs")
        tpa, tpak = sx("ftpa")
        tpb, tpbk = sx("ftpb")
        p8, p8k = sx("fp8s")
        m1, m1k = sx("fm1s")
        m2s, m2k = sx("fm2s")
        ohs, ohk = sx("fohs")
        fin, fink = sx("finds")
        finb, finbk = sx("findbs")
        scr, scrk = sx("fscrs")
        sel, selk = sx("fsels")
        psl, pslk = (fps1, "fps1") if s == 0 else (fps2, "fps2")
        emit_ln_tile(ctx, resid[:, ti, :], "resid", xls[:, :], xlk,
                     t["lmean"][:, ti:ti + 1], t["lrstd"][:, ti:ti + 1],
                     mk=f"lmean{s}", rk=f"lrstd{s}",
                     bneg_ap=t["lbb2"][:, ti:ti + 1], bneg_key=f"lbb2{s}",
                     sfx=f"_{s}", sq_ap=xls[:, :], sq_key=xlk)
        emit_tm2fm(ctx, xls, xlk, lambda c, fxs=fxs: fxs[:, c, :], fxk, eng="sync")
        for cc in range(NCHUNK):
            b.emit("tensor", lambda e, cc=cc, psl=psl, fxs=fxs: e.matmul(
                psl[0:8, 0:128], t["frwT"][:, cc, :], fxs[:, cc, :],
                start=(cc == 0), stop=(cc == NCHUNK - 1)),
                reads=["frwT", fxk], writes=[pslk], group=S + f"lg{ti}",
                group_end=(cc == NCHUNK - 1))
        b.emit("vector", lambda e, lgs=lgs, psl=psl: e.tensor_copy(lgs[0:8, :], psl[0:8, 0:128]),
               reads=[pslk], writes=[lgk])
        emit_tr_32x128(ctx, tpa, tpak, lgs, lgk)
        b.emit("vector", lambda e, p8=p8, tpa=tpa: e.tensor_add(p8[:, :], tpa[:, 0:8], t["frb"][:, :]),
               reads=[tpak, "frb"], writes=[p8k])
        b.emit("vector", lambda e, p8=p8, m1=m1: e.reduce_max(m1[:, :], p8[:, :], axis=AX.X, negate=True),
               reads=[p8k], writes=[m1k])
        b.emit("scalar", lambda e, p8=p8, m1=m1: e.activation(out=p8[:, :], in_=p8[:, :], func=AF.Exp, bias=m1[:, :]),
               reads=[p8k, m1k], writes=[p8k])
        b.emit("vector", lambda e, p8=p8, m1=m1: e.reduce_sum(m1[:, :], p8[:, :], axis=AX.X),
               reads=[p8k], writes=[m1k])
        b.emit("vector", lambda e, m1=m1: e.reciprocal(m1[:, :], m1[:, :]),
               reads=[m1k], writes=[m1k])
        b.emit("vector", lambda e, p8=p8, m1=m1: e.tensor_scalar(p8[:, :], p8[:, :], m1[:, :], None, ALU.mult),
               reads=[p8k, m1k], writes=[p8k])
        b.emit("vector", lambda e, p8=p8, m1=m1: e.reduce_max(m1[:, :], p8[:, :], axis=AX.X),
               reads=[p8k], writes=[m1k])
        b.emit("vector", lambda e, p8=p8, m1=m1, ohs=ohs: e.tensor_scalar(ohs[:, :], p8[:, :], m1[:, :], None, ALU.is_ge),
               reads=[p8k, m1k], writes=[ohk])
        b.emit("vector", lambda e, p8=p8, ohs=ohs, scr=scr: e.scalar_tensor_tensor(scr[:, :], ohs[:, :], -2.0, p8[:, :], ALU.mult, ALU.add),
               reads=[ohk, p8k], writes=[scrk])
        b.emit("vector", lambda e, scr=scr, m2s=m2s: e.reduce_max(m2s[:, :], scr[:, :], axis=AX.X),
               reads=[scrk], writes=[m2k])
        b.emit("vector", lambda e, p8=p8, m2s=m2s, fin=fin: e.tensor_scalar(fin[:, :], p8[:, :], m2s[:, :], None, ALU.is_ge),
               reads=[p8k, m2k], writes=[fink])
        b.emit("scalar", lambda e, fin=fin, finb=finb: e.activation(out=finb[:, :], in_=fin[:, :], func=AF.Copy),
               reads=[fink], writes=[finbk])
        b.emit("vector", lambda e, p8=p8, fin=fin, scr=scr: e.tensor_mul(scr[:, :], p8[:, :], fin[:, :]),
               reads=[p8k, fink], writes=[scrk])
        b.emit("vector", lambda e, ti=ti, scr=scr: e.tensor_copy(t["fwn"][:, ti:ti + 1], scr[:, 7:8]),
               reads=[scrk], writes=["fwn"])
        emit_select(ctx, scr[:, 0:7], scrk, fin[:, 0:7], [fink],
                    t["fiota"][:, 0:7], ["fiota"], t["f99"][:, 0:7], ["f99"], n=7, scr=sel, scrk=selk)
        b.emit("vector", lambda e, ti=ti, scr=scr: e.tensor_reduce(t["fe1"][:, ti:ti + 1], scr[:, 0:7], AX.X, ALU.min),
               reads=[scrk], writes=["fe1"])
        b.emit("vector", lambda e, ti=ti, ohs=ohs: e.tensor_scalar(ohs[:, 0:7], t["fiota"][:, 0:7], t["fe1"][:, ti:ti + 1], None, ALU.is_equal),
               reads=["fiota", "fe1"], writes=[ohk])
        b.emit("vector", lambda e, scr=scr, sel=sel: e.tensor_scalar(sel[:, 0:7], scr[:, 0:7], -99.0, None, ALU.add),
               reads=[scrk], writes=[selk])
        b.emit("vector", lambda e, sel=sel, ohs=ohs: e.tensor_mul(sel[:, 0:7], sel[:, 0:7], ohs[:, 0:7]),
               reads=[selk, ohk], writes=[selk])
        b.emit("vector", lambda e, scr=scr, sel=sel: e.tensor_sub(scr[:, 0:7], scr[:, 0:7], sel[:, 0:7]),
               reads=[scrk, selk], writes=[scrk])
        b.emit("vector", lambda e, ti=ti, scr=scr: e.tensor_reduce(t["fe2"][:, ti:ti + 1], scr[:, 0:7], AX.X, ALU.min),
               reads=[scrk], writes=["fe2"])
        b.emit("vector", lambda e, ohs=ohs, p8=p8, scr=scr: e.tensor_mul(scr[:, 0:7], ohs[:, 0:7], p8[:, 0:7]),
               reads=[ohk, p8k], writes=[scrk])
        b.emit("vector", lambda e, ti=ti, scr=scr: e.tensor_reduce(t["fw1"][:, ti:ti + 1], scr[:, 0:7], AX.X, ALU.add),
               reads=[scrk], writes=["fw1"])
        b.emit("vector", lambda e, ti=ti, scr=scr: e.tensor_scalar(scr[:, 0:7], t["fiota"][:, 0:7], t["fe2"][:, ti:ti + 1], None, ALU.is_equal),
               reads=["fiota", "fe2"], writes=[scrk])
        b.emit("vector", lambda e, scr=scr, p8=p8: e.tensor_mul(scr[:, 0:7], scr[:, 0:7], p8[:, 0:7]),
               reads=[scrk, p8k], writes=[scrk])
        b.emit("vector", lambda e, ti=ti, scr=scr: e.tensor_reduce(t["fw2"][:, ti:ti + 1], scr[:, 0:7], AX.X, ALU.add),
               reads=[scrk], writes=["fw2"])
        b.emit("vector", lambda e, tpb=tpb, finb=finb: e.tensor_copy(tpb[:, 0:8], finb[:, :]),
               reads=[finbk], writes=[tpbk])
        emit_tr_128x32(ctx, indfm[:, ti * 128:(ti + 1) * 128], "indfm", tpb, tpbk)
    b.emit("vector", lambda e: e.memset(t["frun"][:, :], 0.0), writes=["frun"])
    b.emit("vector", lambda e: e.tensor_tensor_scan(
        jrfm[0:8, :], t["fones8"][:, 0:1].broadcast_to((8, 4096)), indfm[0:8, :],
        t["frun"][:, 0:1], ALU.mult, ALU.add),
        reads=["fones8", "indfm", "frun"], writes=["jrfm"])
    b.emit("vector", lambda e: e.tensor_copy(t["fcnt"][:, 0:1], jrfm[0:8, 2047:2048]),
           reads=["jrfm"], writes=["fcnt"])
    b.emit("vector", lambda e: e.tensor_copy(t["fcnt"][:, 1:2], jrfm[0:8, 4095:4096]),
           reads=["jrfm"], writes=["fcnt"])
    b.emit("vector", lambda e: e.tensor_sub(t["fcnt"][:, 1:2], t["fcnt"][:, 1:2], t["fcnt"][:, 0:1]),
           reads=["fcnt"], writes=["fcnt"])
    b.emit("vector", lambda e: e.tensor_sub(jrfm[0:8, :], jrfm[0:8, :], indfm[0:8, :]),
           reads=["jrfm", "indfm"], writes=["jrfm"])
    for ti in range(NT):
        s = ti % 2
        def sx(name, s=s):
            return t[f"{name}{s}"], f"{name}{s}"
        tpc, tpck = sx("ftpc")
        ohs, ohk = sx("fohs")
        scr, scrk = sx("fscrs")
        sc1, sc1k = sx("fsc1c")
        b.emit("scalar", lambda e, ti=ti, s=s: e.activation(
            out=xlt[:, :], in_=resid[:, ti, :], func=AF.Identity,
            scale=t["lrstd"][:, ti:ti + 1], bias=t["lbb2"][:, ti:ti + 1]),
            reads=["resid", f"lrstd{s}", f"lbb2{s}"], writes=["xlt"])
        emit_tr_32x128(ctx, tpc, tpck, jrfm[:, ti * 128:(ti + 1) * 128], "jrfm")
        for (esel, jrn, gxn, gidxn) in (("fe1", "fjr1", "fgx1", "fgidx1"),
                                        ("fe2", "fjr2", "fgx2", "fgidx2")):
            gix, gixk = sx(gidxn)
            b.emit("vector", lambda e, ti=ti, esel=esel, ohs=ohs: e.tensor_scalar(
                ohs[:, :], t["fiota"][:, :], t[esel][:, ti:ti + 1], None, ALU.is_equal),
                reads=["fiota", esel], writes=[ohk])
            b.emit("vector", lambda e, ohs=ohs, scr=scr, tpc=tpc: e.tensor_mul(scr[:, :], ohs[:, :], tpc[:, 0:8]),
                   reads=[ohk, tpck], writes=[scrk])
            b.emit("vector", lambda e, ti=ti, jrn=jrn, scr=scr: e.tensor_reduce(
                t[jrn][:, ti:ti + 1], scr[:, :], AX.X, ALU.add),
                reads=[scrk], writes=[jrn])
            b.emit("vector", lambda e, ti=ti, esel=esel, jrn=jrn, sc1=sc1: e.scalar_tensor_tensor(
                sc1[:, :], t[esel][:, ti:ti + 1], 2.0 * HALF_PAD, t[jrn][:, ti:ti + 1],
                ALU.mult, ALU.add), reads=[esel, jrn], writes=[sc1k])
            b.emit("vector", lambda e, sc1=sc1: e.tensor_add(sc1[:, :], sc1[:, :], t["fhoff"][:, :]),
                   reads=[sc1k, "fhoff"], writes=[sc1k])
            b.emit("vector", lambda e, sc1=sc1: e.tensor_tensor(sc1[:, :], sc1[:, :], t["ftrash"][:, :], ALU.min),
                   reads=[sc1k, "ftrash"], writes=[sc1k])
            b.emit("vector", lambda e, ti=ti, gxn=gxn, sc1=sc1: e.tensor_copy(t[gxn][:, ti:ti + 1], sc1[:, :]),
                   reads=[sc1k], writes=[gxn])
            b.emit("vector", lambda e, gix=gix, sc1=sc1: e.tensor_copy(gix[:, :], sc1[:, :]),
                   reads=[sc1k], writes=[gixk])
            b.emit("gpsimd", lambda e, gix=gix: e.indirect_dma_start(
                mslab[:, :], IndirectOffsetOnAxis(gix[:, 0:1], 0), xlt[:, :], None),
                reads=[gixk, "xlt"], writes=[mkey], dma=True)
    for qq in range(4):
        b.emit("gpsimd", lambda e, qq=qq: e.dma_start(
            out=dr["respill"][qq * 8:(qq + 1) * 8].rearrange("T p f -> p T f"),
            in_=resid[:, qq * 8:(qq + 1) * 8, :]),
            reads=["resid"], writes=["respill"], dma=True)
    b.emit("gpsimd", lambda e: e.dma_start(out=dr["agin"][:, :], in_=t["fcnt"][:, :]),
           reads=["fcnt"], writes=["agin"], dma=True)
    b.emit("gpsimd", lambda e, ncores=ncores: e.collective_compute(
        "AllGather", ALU.bypass, replica_groups=[list(range(ncores))],
        ins=[dr["agin"][:, :]], outs=[dr["agout"][:, :]]),
        reads=["agin", mkey], writes=["agout"], tag="AG")
    b.emit("gpsimd", lambda e: e.dma_start(
        out=t["fagt"][:, :].rearrange("e (c r) -> e c r", r=2),
        in_=dr["agout"].rearrange("(c e) r -> e c r", e=8)),
        reads=["agout"], writes=["fagt"], dma=True)
    for row in range(2):
        b.emit("vector", lambda e, row=row, ncores=ncores: e.tensor_mul(
            t["fscr8w"][:, 0:2 * ncores], t["fagt"][:, :],
            t["fmsk"][:, row * 2 * ncores:(row + 1) * 2 * ncores]),
            reads=["fagt", "fmsk"], writes=["fscr8w"])
        b.emit("vector", lambda e, row=row, ncores=ncores: e.tensor_reduce(
            t["foff"][:, row:row + 1], t["fscr8w"][:, 0:2 * ncores], AX.X, ALU.add),
            reads=["fscr8w"], writes=["foff"])
    for row in range(2):
        b.emit("vector", lambda e: e.memset(flgT[0:32, 0:32], 0.0), writes=["flgT"])
        b.emit("vector", lambda e, row=row: e.tensor_copy(flgT[0:8, 0:1], t["foff"][:, row:row + 1]),
               reads=["foff"], writes=["flgT"])
        b.emit("vector", lambda e: e.transpose(ftp[0:32, 0:32], flgT[0:32, 0:32]),
               reads=["flgT"], writes=["ftp"])
        b.emit("tensor", lambda e: e.matmul(
            fps1[0:128, 0:8], t["fonesf"][0:1, :], ftp[0:1, 0:8],
            start=True, stop=True),
            reads=["fonesf", "ftp"], writes=["fps1"], group=S + f"offb{row}", group_end=True)
        b.emit("vector", lambda e, row=row: e.tensor_copy(t["foffb"][:, row, :], fps1[0:128, 0:8]),
               reads=["fps1"], writes=["foffb"])
    ekeys = EXPERT_KEYS + COMBINE_KEYS
    for k in EXPERT_KEYS:
        base = ["resid", "xfm"] if k in ("fwi", "fwo") else ["resid"]
        b.inherit(k + S, base + ([k + "F0_"] if call == 1 else []))
    for par in range(2):
        b.inherit("fwi" + S + f"_{par}", ["xfm", "cvt32", "cvt16"])
        b.inherit("fwo" + S + f"_{par}", ["xfm", "cvt32", "cvt16"])
    hk, yk, wik, wok, xgt6k, ytm6k, zk, _, _ = [k + S for k in EXPERT_KEYS]
    fhfm, fyfm, fwi, fwo, fxgt6, fytm6, fzero = [t[k] for k in EXPERT_KEYS[:7]]
    b.emit("vector", lambda e: e.memset(fzero[:, :], 0.0), writes=[zk])
    b.emit("vector", lambda e: e.memset(fzero[:, :], 0.0), writes=[zk])
    emit_dyn_dma(ctx, ST_ZROW, yslab, [[1024, 1], [1, 1024]],
                 fzero[0:1, :], True, reads=[zk], writes=[ykey], eng="scalar")
    done_q = ctx.t.setdefault("_doneq", set())

    def ld_wi(jj, m):
        par = m % 2
        wkm = wik + f"_{par}"
        qq = QSEQ[jj]
        b.emit("sync", lambda e, m=m, par=par, qq=qq: e.dma_start(
            out=fwi[:, par, :, :],
            in_=dr["wscr_i"][qq, m, :, :].rearrange("p (dc i) -> p dc i", dc=NCHUNK)),
            reads=["wscr_i"], writes=[wkm], dma=True)

    def gather_xg(jj):
        emit_dyn_dma(ctx, ST_SUB + jj, mslab,
                     [[1024, 128], [131072, 6], [1, 1024]],
                     fxgt6[:, :, :], False, reads=[mkey], writes=[xgt6k])

    def xg_tr(jj):
        fxg_t, xgk_t = (t["fxg0"], "fxg0" + S) if jj % 2 == 0 else (t["fxg1"], "fxg1" + S)
        for kk in range(6):
            for cc in range(NCHUNK):
                b.emit("sync", lambda e, kk=kk, cc=cc, fxg_t=fxg_t: e.dma_start(
                    out=fxg_t[:, cc, kk * 128:(kk + 1) * 128],
                    in_=fxgt6[:, kk, cc * 128:(cc + 1) * 128], transpose=True),
                    reads=[xgt6k], writes=[xgk_t], dma=True)

    for j in range(NEXP):
        q = QSEQ[j]
        if call == 0 and j == 0 and q not in done_q:
            emit_ffn_convert_q(ctx, q)
            done_q.add(q)
        fxg_j, xgk_j = (t["fxg0"], "fxg0" + S) if j % 2 == 0 else (t["fxg1"], "fxg1" + S)
        if j == 0:
            gather_xg(0)
            xg_tr(0)
            ld_wi(0, 0)
            ld_wi(0, 1)
        for m in range(8):
            par = m % 2
            wkm = wik + f"_{par}"
            if m >= 2:
                ld_wi(j, m)
            for icl in range(4):
                ic = m * 4 + icl
                pk_a, pk_b = (("fps1", "fps2") if ic % 2 == 0 else ("fps3", "fps4"))
                for (pk, lo, w) in ((pk_a, 0, 512), (pk_b, 512, 256)):
                    ps = t[pk]
                    for dc in range(NCHUNK):
                        b.emit("tensor", lambda e, ps=ps, par=par, icl=icl, dc=dc, lo=lo, w=w, fxg_j=fxg_j: e.matmul(
                            ps[:, 0:w], fwi[:, par, dc, icl * 128:(icl + 1) * 128],
                            fxg_j[:, dc, lo:lo + w],
                            start=(dc == 0), stop=(dc == NCHUNK - 1)),
                            reads=[wkm, xgk_j], writes=[pk], group=S + f"h{j}_{ic}_{lo}",
                            group_end=(dc == NCHUNK - 1))
                b.emit("scalar", lambda e, ic=ic, pa=t[pk_a]: e.activation(
                    out=fhfm[:, ic, 0:512], in_=pa[:, 0:512], func=AF.Relu),
                    reads=[pk_a], writes=[hk])
                b.emit("scalar", lambda e, ic=ic, pb=t[pk_b]: e.activation(
                    out=fhfm[:, ic, 512:768], in_=pb[:, 0:256], func=AF.Relu),
                    reads=[pk_b], writes=[hk])
        for oc in range(8):
            par = oc % 2
            wkm = wok + f"_{par}"
            b.emit("sync", lambda e, oc=oc, par=par, q=q: e.dma_start(
                out=fwo[:, par, :, :],
                in_=dr["wscr_o"][q, oc, :, :].rearrange("p (ic o) -> p ic o", ic=32)),
                reads=["wscr_o"], writes=[wkm], dma=True)
            pk_a, pk_b = (("fps3", "fps4") if oc % 2 == 0 else ("fps1", "fps2"))
            for (pk, lo, w) in ((pk_a, 0, 512), (pk_b, 512, 256)):
                ps = t[pk]
                for ic in range(32):
                    b.emit("tensor", lambda e, ps=ps, par=par, ic=ic, lo=lo, w=w: e.matmul(
                        ps[:, 0:w], fwo[:, par, ic, :],
                        fhfm[:, ic, lo:lo + w],
                        start=(ic == 0), stop=(ic == 31)),
                        reads=[wkm, hk], writes=[pk], group=S + f"y{j}_{oc}_{lo}",
                        group_end=(ic == 31))
            b.emit("scalar", lambda e, oc=oc, pa=t[pk_a]: e.activation(
                out=fyfm[:, oc, 0:512], in_=pa[:, 0:512], func=AF.Copy),
                reads=[pk_a], writes=[yk])
            b.emit("scalar", lambda e, oc=oc, pb=t[pk_b]: e.activation(
                out=fyfm[:, oc, 512:768], in_=pb[:, 0:256], func=AF.Copy),
                reads=[pk_b], writes=[yk])
        if j + 1 < NEXP:
            qn = QSEQ[j + 1]
            if call == 0 and qn not in done_q:
                emit_ffn_convert_q(ctx, qn)
                done_q.add(qn)
            gather_xg(j + 1)
            xg_tr(j + 1)
            ld_wi(j + 1, 0)
            ld_wi(j + 1, 1)
        for kk in range(6):
            for cc in range(NCHUNK):
                b.emit("sync", lambda e, kk=kk, cc=cc: e.dma_start(
                    out=fytm6[:, kk, cc * 128:(cc + 1) * 128],
                    in_=fyfm[:, cc, kk * 128:(kk + 1) * 128], transpose=True),
                    reads=[yk], writes=[ytm6k], dma=True)
        emit_dyn_dma(ctx, ST_SUB + j, yslab,
                     [[1024, 128], [131072, 6], [1, 1024]],
                     fytm6[:, :, :], True, reads=[ytm6k], writes=[ykey])
    groups = [[2 * p, 2 * p + 1] for p in range(ncores // 2)]
    b.emit("gpsimd", lambda e, groups=groups: e.collective_compute(
        "AllReduce", ALU.add, replica_groups=groups,
        ins=[dr["barin"][:, :]], outs=[dr["barout"][:, :]]),
        reads=[ykey, "barin"], writes=["ybar" + S], tag="BAR")
    b.inherit("resid", [mkey] + [k + S for k in ekeys])
    for sname in ("frt0", "frt1", "fg1b0", "fg1b1", "fg2b0", "fg2b1"):
        b.inherit(sname + S, [wik + "_0", wik + "_1", "cvt32", "cvt16"])
    for sname in ("fdelta0", "fdelta1", "fxltd0", "fxltd1"):
        b.inherit(sname + S, [wok + "_0", wok + "_1", "cvt32", "cvt16"])
    for ti in range(NT):
        row = ti // 16
        s = ti % 2
        def sx(name, s=s):
            return t[f"{name}{s}"], f"{name}{s}"
        frt_s, rtk_s = t[f"frt{s}"], f"frt{s}" + S
        xld, xldk = t[f"fxltd{s}"], f"fxltd{s}" + S
        fdl, dk_s = t[f"fdelta{s}"], f"fdelta{s}" + S
        ohs, ohk = sx("fohs")
        scr, scrk = sx("fscrs")
        sc1, sc1k = sx("fsc1c")
        kpt, kptk = sx("fkeptc")
        sc2, sc2k = sx("fscr2c")
        sel, selk = sx("fsels")
        b.emit("sync", lambda e, ti=ti, frt_s=frt_s: e.dma_start(
            out=frt_s[:, :], in_=dr["respill"][ti]),
            reads=["respill"], writes=[rtk_s], dma=True)
        b.emit("scalar", lambda e, ti=ti, s=s, frt_s=frt_s, xld=xld: e.activation(
            out=xld[:, :], in_=frt_s[:, :], func=AF.Identity,
            scale=t["lrstd"][:, ti:ti + 1], bias=t["lbb2"][:, ti:ti + 1]),
            reads=[rtk_s, f"lrstd{s}", f"lbb2{s}"], writes=[xldk])
        for (esel, jrn, gxn, gi) in (("fe1", "fjr1", "fgx1", 1),
                                     ("fe2", "fjr2", "fgx2", 2)):
            gix, gixk = sx(f"fgidx{gi}")
            gbuf, gbk = t[f"fg{gi}b{s}"], f"fg{gi}b{s}" + S
            b.emit("vector", lambda e, ti=ti, esel=esel, ohs=ohs: e.tensor_scalar(
                ohs[:, :], t["fiota"][:, :], t[esel][:, ti:ti + 1], None, ALU.is_equal),
                reads=["fiota", esel], writes=[ohk])
            b.emit("vector", lambda e, row=row, ohs=ohs, scr=scr: e.tensor_mul(
                scr[:, :], ohs[:, :], t["foffb"][:, row, :]),
                reads=[ohk, "foffb"], writes=[scrk])
            b.emit("vector", lambda e, scr=scr, sc1=sc1: e.tensor_reduce(sc1[:, :], scr[:, :], AX.X, ALU.add),
                   reads=[scrk], writes=[sc1k])
            b.emit("vector", lambda e, ti=ti, jrn=jrn, sc1=sc1: e.tensor_add(sc1[:, :], sc1[:, :], t[jrn][:, ti:ti + 1]),
                   reads=[sc1k, jrn], writes=[sc1k])
            b.emit("vector", lambda e, cap=cap, sc1=sc1, kpt=kpt: e.tensor_single_scalar(
                kpt[:, :], sc1[:, :], float(cap), ALU.is_lt),
                reads=[sc1k], writes=[kptk])
            b.emit("vector", lambda e, ti=ti, esel=esel, sc2=sc2: e.tensor_single_scalar(
                sc2[:, :], t[esel][:, ti:ti + 1], 50.0, ALU.is_lt),
                reads=[esel], writes=[sc2k])
            b.emit("vector", lambda e, kpt=kpt, sc2=sc2: e.tensor_mul(kpt[:, :], kpt[:, :], sc2[:, :]),
                   reads=[kptk, sc2k], writes=[kptk])
            emit_select(ctx, sc1[:, :], sc1k, kpt[:, :], [kptk],
                        t[gxn][:, ti:ti + 1], [gxn], t["fzrowc"][:, :], ["fzrowc"], n=1,
                        scr=sel, scrk=selk)
            b.emit("vector", lambda e, gix=gix, sc1=sc1: e.tensor_copy(gix[:, :], sc1[:, :]),
                   reads=[sc1k], writes=[gixk])
            b.emit("gpsimd", lambda e, gix=gix, gbuf=gbuf: e.indirect_dma_start(
                gbuf[:, :], None, yslab[:, :], IndirectOffsetOnAxis(gix[:, 0:1], 0)),
                reads=[gixk, ykey, "ybar" + S], writes=[gbk], dma=True)
        g1b, g2b = t[f"fg1b{s}"], t[f"fg2b{s}"]
        g1bk, g2bk = f"fg1b{s}" + S, f"fg2b{s}" + S
        b.emit("scalar", lambda e, ti=ti, fdl=fdl, g1b=g1b: e.activation(
            out=fdl[:, :], in_=g1b[:, :], func=AF.Identity,
            scale=t["fw1"][:, ti:ti + 1], bias=t["fzero1"][:, 0:1]),
            reads=[g1bk, "fw1", "fzero1"], writes=[dk_s])
        b.emit("vector", lambda e, ti=ti, fdl=fdl, g2b=g2b: e.scalar_tensor_tensor(
            fdl[:, :], g2b[:, :], t["fw2"][:, ti:ti + 1], fdl[:, :], ALU.mult, ALU.add),
            reads=[g2bk, "fw2", dk_s], writes=[dk_s])
        b.emit("vector", lambda e, ti=ti, fdl=fdl, xld=xld: e.scalar_tensor_tensor(
            fdl[:, :], xld[:, :], t["fwn"][:, ti:ti + 1], fdl[:, :], ALU.mult, ALU.add),
            reads=[xldk, "fwn", dk_s], writes=[dk_s])
        b.emit("vector", lambda e, ti=ti, fdl=fdl, frt_s=frt_s: e.tensor_add(
            resid[:, ti, :], frt_s[:, :], fdl[:, :]),
            reads=[rtk_s, dk_s], writes=["resid"])
        if out_dram is not None:
            b.emit("scalar", lambda e, ti=ti: e.dma_start(
                out=out_dram[ti], in_=resid[:, ti, :]),
                reads=["resid"], writes=["outd"], dma=True)


def emit_ffn_consts(ctx):
    b, t, dr = ctx.b, ctx.t, ctx.dram
    b.emit("gpsimd", lambda e: e.dma_start(
        out=t["frwT"][:, :, :], in_=dr["rwT_d"].rearrange("(c p) e -> p c e", p=128)),
        reads=["rwT_d"], writes=["frwT"], dma=True)
    for name, dst in (("rb_bc", "frb"), ("iota_d", "fiota"), ("c99_d", "f99"),
                      ("hoff_d", "fhoff"), ("trash_d", "ftrash"), ("zrowc_d", "fzrowc"),
                      ("fmsk_d", "fmsk")):
        b.emit("gpsimd", lambda e, name=name, dst=dst: e.dma_start(
            out=t[dst][:], in_=dr[name][:]), reads=[name], writes=[dst], dma=True)
    b.emit("gpsimd", lambda e: e.dma_start(out=t["stab"][:, :], in_=dr["stab_d"][:, :]),
           reads=["stab_d"], writes=["stab"], dma=True)
    b.emit("gpsimd", lambda e: e.dma_start(out=dr["barin"][0:1, 0:8], in_=t["frb"][0:1, 0:8]),
           reads=["frb"], writes=["barin"], dma=True)


TRASH0 = NEXP * 2 * HALF_PAD
"""Host-side input prep for the FFN (and full kernel): per-core tables/consts.
Layout-only numpy (transpose/reshape/broadcast/constants) -- no arithmetic on
input VALUES."""

TRASH0 = NEXP * 2 * HALF_PAD


def core_rows(c, ncores):
    if ncores == 8:
        P, h = divmod(c, 2)
        return [P + 4 * h, P + 4 * h + 8]
    elif ncores == 2:
        return [c, c + 2]
    raise ValueError(ncores)


def ffn_core_inputs(core, ncores, router_w, router_b):
    half = core % 2
    nb = 2 * ncores
    rows = core_rows(core, ncores)
    d = {}
    d["rwT_d"] = np.ascontiguousarray(router_w.T)
    d["rb_bc"] = np.ascontiguousarray(np.broadcast_to(router_b[None, :], (128, 8)))
    d["iota_d"] = np.ascontiguousarray(np.broadcast_to(np.arange(8, dtype=np.float32)[None, :], (128, 8)))
    d["c99_d"] = np.full((128, 8), 99.0, np.float32)
    d["hoff_d"] = np.full((128, 1), float(half * HALF_PAD), np.float32)
    d["trash_d"] = np.full((128, 1), float(TRASH0 + half), np.float32)
    d["zrowc_d"] = np.full((128, 1), float(TRASH0 + 2 + half), np.float32)
    msk = np.zeros((2, ncores, 2), np.float32)
    for row in range(2):
        myb = rows[row]
        for c2 in range(ncores):
            r2rows = core_rows(c2, ncores)
            for r2 in range(2):
                if c2 != core and r2rows[r2] < myb:
                    msk[row, c2, r2] = 1.0
    d["fmsk_d"] = np.ascontiguousarray(
        np.broadcast_to(msk.reshape(1, -1), (8, 2 * ncores * 2)))
    stab = np.zeros(256, np.int32)
    conv = [0, 1, 2, 3] if half == 0 else [4, 5, 6, 3]
    for q, e in enumerate(conv):
        for dc in range(8):
            stab[q * 8 + dc] = e * 4194304 + dc * 524288
        for icg in range(8):
            stab[64 + q * 8 + icg] = e * 4194304 + icg * 524288
    SUB_E0 = [0, 1, 2, 0, 1, 2, 3]
    SUB_HP = [0, 0, 0, 1, 1, 1, half]
    for j in range(7):
        e = SUB_E0[j] + (4 * half if j < 6 else 0)
        hp = SUB_HP[j]
        stab[192 + j] = (e * 2 * HALF_PAD + hp * HALF_PAD) * 1024
    stab[199] = (TRASH0 + 2 + half) * 1024
    for ee in range(7):
        stab[200 + ee] = (ee * 2 * HALF_PAD + half * HALF_PAD) * 1024
    stab[207] = (TRASH0 + half) * 1024
    d["stab_d"] = stab.reshape(1, 256)
    return d


def wi_host(wi):
    return np.ascontiguousarray(wi.reshape(7, 8, 128, 8, 512).transpose(0, 3, 2, 1, 4)
                                .reshape(7, 8, 128, 4096))


def wo_host(wo):
    return np.ascontiguousarray(wo.reshape(7, 32, 128, 8, 128).transpose(0, 3, 2, 1, 4)
                                .reshape(7, 8, 128, 4096))




from concourse.bass_utils import run_bass_kernel_spmd

NCORES = 8
CAP = 4096


def _build():
    nc = bass.Bass("TRN2", target_bir_lowering=False)
    b = B(nc)
    ctx = Ctx(nc, b)
    dr = ctx.dram
    x = nc.dram_tensor("x", [2, 2048, 1024], F32, kind="ExternalInput").ap()
    y = nc.dram_tensor("y", [2, 2048, 1024], F32, kind="ExternalOutput").ap()
    dr["wqkvT"] = nc.dram_tensor("wqkvT", [3, 4, 1024, 128], F32, kind="ExternalInput").ap()
    for name, shape in [("woutTd", [512, 1024]), ("boutd", [128, 8]), ("masksd", [128, 4, 128])]:
        dr[name] = nc.dram_tensor(name, shape, F32, kind="ExternalInput").ap()
    for name, shape in [("bq_fm", [128, 8]), ("bk_fm", [128, 8]), ("bnw_fm", [128, 8]),
                        ("bld_d", [16, 1]), ("bpm_d", [16, 8]), ("bhsel_d", [128, 2]),
                        ("bE1_d", [16, 128]), ("bE2_d", [2, 128]),
                        ("bowT_d", [1024, 1024]), ("onesf_d", [1, 128])]:
        dr[name] = nc.dram_tensor(name, shape, F32, kind="ExternalInput").ap()
    dr["ones_d"] = nc.dram_tensor("ones_d", [128, 1], BF16, kind="ExternalInput").ap()
    dr["expw_i"] = nc.dram_tensor("expw_i", [7, 8, 128, 4096], F32, kind="ExternalInput").ap()
    dr["expw_o"] = nc.dram_tensor("expw_o", [7, 8, 128, 4096], F32, kind="ExternalInput").ap()
    dr["rwT_d"] = nc.dram_tensor("rwT_d", [1024, 8], F32, kind="ExternalInput").ap()
    for name, shape, dt in [("rb_bc", [128, 8], F32), ("iota_d", [128, 8], F32),
                            ("c99_d", [128, 8], F32), ("hoff_d", [128, 1], F32),
                            ("trash_d", [128, 1], F32), ("zrowc_d", [128, 1], F32),
                            ("fmsk_d", [8, 2 * NCORES * 2], F32), ("stab_d", [1, 256], I32)]:
        dr[name] = nc.dram_tensor(name, shape, dt, kind="ExternalInput").ap()
    dr["mslab"] = nc.dram_tensor("mslab", [MROWS, D], BF16, addr_space="Shared").ap()
    dr["yslab"] = nc.dram_tensor("yslab", [MROWS, D], BF16, addr_space="Shared").ap()
    dr["wscr_i"] = nc.dram_tensor("wscr_i", [4, 8, 128, 4096], BF16).ap()
    dr["wscr_o"] = nc.dram_tensor("wscr_o", [4, 8, 128, 4096], BF16).ap()
    dr["respill"] = nc.dram_tensor("respill", [NT, 128, D], F32).ap()
    dr["agin"] = nc.dram_tensor("agin", [8, 2], F32).ap()
    dr["agout"] = nc.dram_tensor("agout", [NCORES * 8, 2], F32).ap()
    dr["barin"] = nc.dram_tensor("barin", [1, 8], F32).ap()
    dr["barout"] = nc.dram_tensor("barout", [1, 8], F32).ap()

    alloc_common(ctx)
    alloc_ffn(ctx, NCORES)
    alloc_ffn_expert(ctx)
    alloc_convert(ctx)
    t = ctx.t
    with ctx.stack:
        for r in range(2):
            b.emit("sync", lambda e, r=r: e.dma_start(
                out=t["resid"][:, 16 * r:16 * (r + 1), :],
                in_=x[r].rearrange("(T p) f -> p T f", p=128)),
                reads=["x"], writes=["resid"], dma=True)
        emit_ffn_consts(ctx)
        ctx.stage_begin()
        alloc_attn_a(ctx)
        b.emit("gpsimd", lambda e: e.dma_start(
            out=t["woutT"][:, :, :], in_=dr["woutTd"].rearrange("(h p) o -> p h o", p=128)),
            writes=["woutT"], dma=True)
        b.emit("gpsimd", lambda e: e.dma_start(out=t["bout"][:, :], in_=dr["boutd"][:, :]),
               writes=["bout"], dma=True)
        b.emit("gpsimd", lambda e: e.dma_start(out=t["amasks"][:, :, :], in_=dr["masksd"][:, :, :]),
               writes=["amasks"], dma=True)
        emit_attn_a(ctx)
        ctx.stage_end()
        ctx.stage_begin()
        alloc_ffn_stage(ctx)
        emit_ffn(ctx, 0, NCORES, CAP)
        ctx.stage_end()
        ctx.stage_begin()
        alloc_attn_b(ctx)
        emit_attn_b_tables(ctx)
        emit_attn_b(ctx)
        ctx.stage_end()
        ctx.stage_begin()
        alloc_ffn_stage(ctx)
        ytiles = [y[ti // 16].rearrange("(T p) f -> p T f", p=128)[:, ti % 16, :]
                  for ti in range(NT)]
        emit_ffn(ctx, 1, NCORES, CAP, out_dram=ytiles)
        ctx.stage_end()
        b.emit("gpsimd", None, reads=["outd"])
        b.finalize()
    return nc


def _host_tables():
    pm = np.zeros((16, 8), np.float32)
    for h in range(16):
        pm[h, h // 2] = 1
    hsel = np.zeros((128, 2), np.float32)
    for p in range(128):
        hsel[p, p // 64] = 1
    E1 = np.zeros((16, 128), np.float32)
    for h in range(16):
        for p in range(128):
            if h % 2 == p // 64:
                E1[h, p] = 1
    return pm, hsel, E1


def kernel(**inputs):
    f32 = lambda a: np.ascontiguousarray(np.asarray(a, dtype=np.float32))
    x = f32(inputs["L0_attn_a_x"])
    Wk = f32(inputs["L0_attn_a_Wk"])
    Wq = f32(inputs["L0_attn_a_Wq"])
    Wv = f32(inputs["L0_attn_a_Wv"])
    W_out = f32(inputs["L0_attn_a_W_out"])
    b_out = f32(inputs["L0_attn_a_b_out"])
    bq = f32(inputs["L0_attn_b_q"])
    bk = f32(inputs["L0_attn_b_k"])
    log_decay = f32(inputs["L0_attn_b_log_decay"])
    bnw = f32(inputs["L0_attn_b_norm_weight"])
    bow = f32(inputs["L0_attn_b_o_w"])
    router_w = f32(inputs["L0_ffn_router_w"])
    router_b = f32(inputs["L0_ffn_router_b"])
    wi = f32(inputs["L0_ffn_experts_inter"])
    wo = f32(inputs["L0_ffn_experts_out"])

    wqkvT = np.ascontiguousarray(np.stack([
        Wq.transpose(0, 2, 1), Wk.transpose(0, 2, 1), Wv.transpose(0, 2, 1)]))
    pm, hsel, E1 = _host_tables()
    common = {
        "wqkvT": wqkvT,
        "woutTd": np.ascontiguousarray(W_out.T),
        "boutd": np.ascontiguousarray(b_out.reshape(8, 128).T),
        "masksd": make_attn_a_masks(),
        "bq_fm": np.ascontiguousarray(bq.reshape(8, 128).T),
        "bk_fm": np.ascontiguousarray(bk.reshape(8, 128).T),
        "bnw_fm": np.ascontiguousarray(bnw.reshape(8, 128).T),
        "bld_d": np.ascontiguousarray(log_decay.reshape(16, 1)),
        "bpm_d": pm, "bhsel_d": hsel, "bE1_d": E1,
        "bE2_d": np.ascontiguousarray(hsel.T),
        "bowT_d": np.ascontiguousarray(bow.T),
        "onesf_d": np.ones((1, 128), np.float32),
        "ones_d": np.ones((128, 1), ml_dtypes.bfloat16),
        "expw_i": wi_host(wi), "expw_o": wo_host(wo),
    }
    in_maps = []
    for c in range(NCORES):
        rows = core_rows(c, NCORES)
        d = {"x": np.ascontiguousarray(x[rows])}
        d.update(common)
        d.update(ffn_core_inputs(c, NCORES, router_w, router_b))
        in_maps.append(d)

    nc = _build()
    res = run_bass_kernel_spmd(nc, in_maps, core_ids=list(range(NCORES)))
    out = np.empty((16, 2048, 1024), np.float32)
    for c in range(NCORES):
        out[core_rows(c, NCORES)] = np.asarray(res.results[c]["y"], dtype=np.float32)
    return out
